# version 01
# baseline (speedup 1.0000x reference)
import numpy as np
import ml_dtypes
import concourse.bass as bass
import concourse.mybir as mybir
from concourse.bass_utils import run_bass_kernel_spmd
from contextlib import ExitStack

F32 = mybir.dt.float32
BF16 = mybir.dt.bfloat16
AF = mybir.ActivationFunctionType
ALU = mybir.AluOpType
AX = mybir.AxisListType

D = 1024
T = 2048
CH = 8
NTT = 4
TT = 512
NHP = 8
NW = 8
DEPTH = 6
DKH = 64
FF = 2730
NJ = 22
JLAST = FF - 21 * 128
PM = 128
NCK = 16
LNM = float(0.5 * np.log(2.0 * DKH))

DEBUG_TAPS = []
N_LAYERS = DEPTH
WITH_PERFORMER = True


class Buf:
    __slots__ = ("name", "wr", "rd")

    def __init__(self, name):
        self.name = name
        self.wr = []
        self.rd = []


class Prog:
    ENGS = ("T", "V", "S", "G", "Q")

    def __init__(self):
        self.groups = []
        self.cnt = {e: 0 for e in self.ENGS}
        self.cnt["DG"] = 0
        self.cnt["DQ"] = 0
        self.waited = {e: {} for e in self.ENGS}

    SERIAL = False

    def emit(self, eng, fn, reads=(), writes=(), dma=False):
        if Prog.SERIAL:
            reads = list(reads)
            writes = list(writes)
            toks_all = [(s, v) for s, v in self.cnt.items() if v > 0]
            fake = Buf("serial")
            fake.wr = toks_all
            reads.append(fake)
        toks = []
        for b in reads:
            toks += b.wr
        for b in writes:
            toks += b.wr + b.rd
        need = {}
        for s, v in toks:
            if s == eng:
                continue
            if need.get(s, -1) < v:
                need[s] = v
        waits = []
        wd = self.waited[eng]
        for s, v in sorted(need.items()):
            if wd.get(s, -1) >= v:
                continue
            wd[s] = v
            waits.append((s, v))
        if dma:
            qsem = "DG" if eng == "G" else "DQ"
            self.cnt[qsem] += 16
            tok = (qsem, self.cnt[qsem])
            inc = qsem
        else:
            self.cnt[eng] += 1
            tok = (eng, self.cnt[eng])
            inc = eng
        self.groups.append((eng, waits, fn, inc, dma))
        for b in writes:
            b.wr = [tok]
            b.rd = []
        for b in reads:
            b.rd.append(tok)
        return tok

    def barrier_q(self, extra_waits):
        self.groups.append(("Q", list(extra_waits), lambda e: e.nop(), None, False))

    @staticmethod
    def seed(dst_bufs, src_bufs):
        toks = []
        for b in src_bufs:
            toks += b.wr + b.rd
        for d in dst_bufs:
            d.wr = list(toks)
            d.rd = []

    def finalize(self, blk, sems):
        per = {e: [] for e in self.ENGS}
        for g in self.groups:
            per[g[0]].append(g)

        def runner(eng):
            def _(e):
                for _, waits, fn, inc, dma in per[eng]:
                    for s, v in waits:
                        e.wait_ge(sems[s], v)
                    inst = fn(e)
                    if dma:
                        inst.then_inc(sems[inc], 16)
                    elif inc is not None:
                        if eng == "V":
                            inst = e.drain()
                        inst.then_inc(sems[inc], 1)
            return _

        blk.tensor(runner("T"))
        blk.vector(runner("V"))
        blk.scalar(runner("S"))
        blk.gpsimd(runner("G"))
        blk.sync(runner("Q"))



def host_constants():
    c = {}
    RT = np.zeros((128, 128), np.float32)
    for d in range(128):
        base = (d // 64) * 64
        dd = d % 64
        pair = base + (dd ^ 32)
        sign = -1.0 if dd < 32 else 1.0
        RT[pair, d] = sign
    c["rotT"] = RT
    BIG = -1.0e30
    low = np.zeros((128, 128), np.float32)
    up = np.zeros((128, 128), np.float32)
    for i in range(128):
        low[i, :i] = BIG
        up[i, i + 1:] = BIG
    c["mask_lowup"] = np.ascontiguousarray(np.concatenate([low, up], axis=1))
    e2c = np.zeros((128, 2), np.float32)
    e2c[0:64, 0] = 1.0
    e2c[64:128, 1] = 1.0
    c["e2col"] = e2c
    e2r = np.zeros((2, 128), np.float32)
    e2r[0, 0:64] = 1.0
    e2r[1, 64:128] = 1.0
    c["e2row"] = e2r
    c["ones2row"] = np.ones((2, 128), np.float32)
    c["ones1row"] = np.ones((1, 128), np.float32)
    n = T
    half = np.arange(0, DKH, 2, dtype=np.float64)
    inv_freq = 1.0 / 10000.0 ** (half / DKH)
    tt_ = np.arange(n, dtype=np.float64)
    freqs = tt_[:, None] * inv_freq[None, :]
    freqs = np.concatenate([freqs, freqs], -1)
    base_scale = (half + 0.4 * DKH) / (1.4 * DKH)
    power = (tt_ - n // 2) / (256 // 2)
    scale = base_scale[None, :] ** power[:, None]
    scale = np.concatenate([scale, scale], -1)
    cos = np.cos(freqs)
    sin = np.sin(freqs)

    def fm(tab):
        tr = np.ascontiguousarray(tab.T.astype(np.float32))
        return np.ascontiguousarray(np.concatenate([tr, tr], axis=0))

    QK = 8.0
    c["cs_q"] = fm(QK * cos * scale)
    c["sn_q"] = fm(QK * sin * scale)
    c["cs_k"] = fm(cos / scale)
    c["sn_k"] = fm(sin / scale)
    c["triu"] = np.ascontiguousarray(np.triu(np.ones((128, 128), np.float32)))
    return c


def host_prepare(inp):
    cst = host_constants()
    shared = {k: np.ascontiguousarray(v) for k, v in cst.items()}
    for nm in ("Wq", "Wk", "Wv", "Wo"):
        shared["p_" + nm.lower()] = np.ascontiguousarray(
            np.asarray(inp[f"L0_attn_{nm}"], np.float32).T)
    for nm in ("bq", "bk", "bo"):
        b = np.asarray(inp[f"L0_attn_{nm}"], np.float32)
        shared["p_" + nm] = np.ascontiguousarray(b.reshape(CH, 128).T)
    shared["p_bv_row"] = np.ascontiguousarray(
        np.asarray(inp["L0_attn_bv"], np.float32).reshape(1, D))
    pj = np.asarray(inp["L0_attn_proj"], np.float32)
    shared["p_projT2"] = np.ascontiguousarray(np.concatenate([pj.T, pj.T], axis=0))
    wqkv = np.asarray(inp["L0_ffn_Wqkv"], np.float32)
    wo = np.asarray(inp["L0_ffn_Wo"], np.float32)
    w1 = np.asarray(inp["L0_ffn_W1"], np.float32)
    w2 = np.asarray(inp["L0_ffn_W2"], np.float32)
    qs = np.asarray(inp["L0_ffn_q_scale"], np.float32)
    ks = np.asarray(inp["L0_ffn_k_scale"], np.float32)
    lng = np.asarray(inp["L0_ffn_ln_g"], np.float32)
    lnb = np.asarray(inp["L0_ffn_ln_b"], np.float32)
    b1 = np.asarray(inp["L0_ffn_b1"], np.float32)
    b2 = np.asarray(inp["L0_ffn_b2"], np.float32)
    for i in range(DEPTH):
        shared[f"wq{i}"] = np.ascontiguousarray(wqkv[i][:, 0:D])
        shared[f"wk{i}"] = np.ascontiguousarray(wqkv[i][:, D:2 * D])
        shared[f"wv{i}"] = np.ascontiguousarray(wqkv[i][:, 2 * D:3 * D])
        shared[f"wo{i}"] = np.ascontiguousarray(wo[i])
        shared[f"qs2{i}"] = np.ascontiguousarray(np.tile(qs[i], 2).reshape(128, 1))
        shared[f"ks2{i}"] = np.ascontiguousarray(np.tile(ks[i], 2).reshape(128, 1))
        shared[f"lng{i}"] = np.ascontiguousarray(lng[i].reshape(CH, 128).T)
        shared[f"lnb{i}"] = np.ascontiguousarray(lnb[i].reshape(CH, 128).T)
        shared[f"b2c{i}"] = np.ascontiguousarray(b2[i].reshape(CH, 128).T)
        w1a = np.zeros((D, NJ * 128), np.float32); w1a[:, :FF] = w1[i][:, 0:FF]
        w1g = np.zeros((D, NJ * 128), np.float32); w1g[:, :FF] = w1[i][:, FF:]
        shared[f"w1a{i}"] = np.ascontiguousarray(w1a)
        shared[f"w1g{i}"] = np.ascontiguousarray(w1g)
        b1a = np.zeros(NJ * 128, np.float32); b1a[:FF] = b1[i][:FF]
        b1g = np.zeros(NJ * 128, np.float32); b1g[:FF] = b1[i][FF:]
        shared[f"b1a{i}"] = np.ascontiguousarray(b1a.reshape(NJ, 128).T)
        shared[f"b1g{i}"] = np.ascontiguousarray(b1g.reshape(NJ, 128).T)
        w2p = np.zeros((NJ * 128, D), np.float32); w2p[:FF] = w2[i]
        shared[f"w2{i}"] = np.ascontiguousarray(w2p)
    x = np.asarray(inp["L0_attn_x"], np.float32)
    maps = []
    for cidx in range(8):
        m = dict(shared)
        m["x_t"] = np.ascontiguousarray(x[cidx].T)
        maps.append(m)
    return maps



def build_program(n_layers=N_LAYERS, with_performer=WITH_PERFORMER, taps=()):
    nc = bass.Bass("TRN2", target_bir_lowering=False)
    dt = {}

    def din(name, shape):
        dt[name] = nc.dram_tensor(name, shape, F32, kind="ExternalInput").ap()

    din("x_t", [D, T])
    for nm in ("rotT", "triu"):
        din(nm, [128, 128])
    din("mask_lowup", [128, 256])
    din("e2col", [128, 2])
    din("e2row", [2, 128])
    din("ones2row", [2, 128])
    din("ones1row", [1, 128])
    for nm in ("cs_q", "sn_q", "cs_k", "sn_k"):
        din(nm, [128, T])
    for nm in ("p_wq", "p_wk", "p_wv", "p_wo"):
        din(nm, [D, D])
    for nm in ("p_bq", "p_bk", "p_bo"):
        din(nm, [128, CH])
    din("p_bv_row", [1, D])
    din("p_projT2", [128, 128])
    for i in range(DEPTH):
        din(f"wq{i}", [D, D]); din(f"wk{i}", [D, D]); din(f"wv{i}", [D, D])
        din(f"wo{i}", [D, D])
        din(f"qs2{i}", [128, 1]); din(f"ks2{i}", [128, 1])
        din(f"lng{i}", [128, CH]); din(f"lnb{i}", [128, CH]); din(f"b2c{i}", [128, CH])
        din(f"w1a{i}", [D, NJ * 128]); din(f"w1g{i}", [D, NJ * 128])
        din(f"b1a{i}", [128, NJ]); din(f"b1g{i}", [128, NJ])
        din(f"w2{i}", [NJ * 128, D])
    y_t = nc.dram_tensor("y_t", [D, T], F32, kind="ExternalOutput").ap()
    tap_outs = {}
    for tp in taps:
        tap_outs[tp] = nc.dram_tensor(f"dbg_{tp}", [128, CH * T], F32,
                                      kind="ExternalOutput").ap()

    with ExitStack() as ctx:
        def sbuf(name, shape, dtype):
            return ctx.enter_context(nc.sbuf_tensor(name, shape, dtype))

        stream = sbuf("stream", [128, CH * T], F32)
        xs = sbuf("xs", [128, CH * T], BF16)
        big = sbuf("big", [128, CH * T], BF16)
        scr2 = sbuf("scr2", [128, CH * T], BF16)
        wtab = sbuf("wtab", [128, 4 * T], BF16)
        wq_w = sbuf("wq_w", [128, 2 * D], BF16)
        wk_w = sbuf("wk_w", [128, 2 * D], BF16)
        wv_w = sbuf("wv_w", [128, CH * 256], BF16)
        wo_w = sbuf("wo_w", [128, D], BF16)
        rotT_b = sbuf("rotT_b", [128, 128], BF16)
        triu_b = sbuf("triu_b", [128, 128], BF16)
        masklu = sbuf("masklu", [128, 256], BF16)
        e2col_b = sbuf("e2col_b", [128, 2], BF16)
        e2row_b = sbuf("e2row_b", [2, 128], BF16)
        ones2_b = sbuf("ones2_b", [2, 128], BF16)
        ones1_b = ones2_b[0:1, :]
        onescol_b = sbuf("onescol_b", [128, 1], BF16)
        projT2_b = sbuf("projT2_b", [128, 128], BF16)
        sqq = sbuf("scrA", [128, 512], BF16)
        qBt = sbuf("scrB", [128, 512], BF16)
        m1q = sqq
        sqk = qBt
        m1k, kBt, kfin = m1q, qBt, sqk
        rq_cols = sbuf("rq_cols", [128, 64], F32)
        cbias = sbuf("cbias", [128, 2], F32)
        cols24 = sbuf("cols24", [128, 24], F32)
        mneg = cols24[:, 0:8]
        ebias = cols24[:, 8:16]
        srow = cols24[:, 16:24]
        rowsA = sbuf("rowsA", [1, 512], F32)
        krows = sbuf("krows", [2, 512], F32)
        rowsB = krows[0:1, :]
        rows_hl = sbuf("rows_hl", [1, 4 * 512], BF16)
        rk_rows_b = sbuf("rk_rows_b", [2, 512], BF16)
        gb_col = sbuf("gb_col", [128, 2 * CH], F32)
        b1col = sbuf("b1col", [128, 2 * NJ], F32)
        b2col = sbuf("b2col", [128, CH], F32)
        qs2c = sbuf("qs2c", [128, 2], F32)
        pbias = sbuf("pbias", [128, 3 * CH], F32)
        state_S = sbuf("state_S", [128, DKH], F32)
        state_Sb = sbuf("state_Sb", [128, DKH], BF16)
        state_z = sbuf("state_z", [128, 1], F32)
        state_zb = sbuf("state_zb", [128, 1], BF16)
        denrow = krows[0:1, 0:256]
        denrow_b = rows_hl[0:1, 0:256]
        phi_f = sbuf("phi_f", [128, 512], F32)

        banks = [ctx.enter_context(nc.psum_tensor(f"bank{i}", [128, 512], F32))
                 for i in range(8)]

        sems = {}
        for s in ("T", "V", "S", "G", "Q", "DG", "DQ"):
            sems[s] = ctx.enter_context(nc.semaphore("sem" + s))
        blk = ctx.enter_context(nc.Block())

        p = Prog()

        b_stream = [Buf(f"st{c}") for c in range(CH)]
        b_xs = [Buf(f"xs{c}") for c in range(CH)]
        b_big = Buf("big")
        b_scr2 = Buf("scr2")
        b_wtab = Buf("wtab")
        b_wq = [Buf("wq0"), Buf("wq1")]
        b_wk = [Buf("wk0"), Buf("wk1")]
        b_wv = Buf("wv")
        b_wo = [Buf("wo0"), Buf("wo1")]
        b_const = Buf("const")
        b_bank = [Buf(f"bk{i}") for i in range(8)]
        b_sqq = Buf("scrA"); b_qBt = Buf("scrB")
        b_m1q = b_sqq; b_m1k = b_sqq
        b_sqk = b_qBt; b_kBt = b_qBt; b_kfin = b_qBt
        b_rq = Buf("rq"); b_mneg = Buf("mneg"); b_ebias = Buf("ebias")
        b_srow = Buf("srow")
        b_rowsA = Buf("rowsA"); b_rowsB = Buf("rowsB"); b_rhl = Buf("rhl")
        b_krows = Buf("krows")
        b_rkb = Buf("rkb")
        b_gb = Buf("gb"); b_b1c = Buf("b1c"); b_b2c = Buf("b2c"); b_qs2 = Buf("qs2")
        b_pbias = Buf("pbias"); b_pbvr = Buf("pbvr")
        b_state = Buf("state")
        b_denrow = Buf("denrow"); b_denrow_b = Buf("denrowb")
        b_phif = Buf("phif")
        b_y = Buf("y")

        def dma(eng, out_ap, in_ap, reads=(), writes=()):
            return p.emit(eng, lambda e, o=out_ap, i=in_ap: e.dma_start(o, i),
                          reads=list(reads), writes=list(writes), dma=True)

        def dmaT(out_ap, in_ap, reads=(), writes=()):
            return p.emit("Q", lambda e, o=out_ap, i=in_ap: e.dma_start(o, i, transpose=True),
                          reads=list(reads), writes=list(writes), dma=True)

        dma("G", rotT_b[:, :], dt["rotT"], writes=[b_const])
        dma("G", triu_b[:, :], dt["triu"], writes=[b_const])
        dma("G", masklu[:, :], dt["mask_lowup"], writes=[b_const])
        dma("G", e2col_b[:, :], dt["e2col"], writes=[b_const])
        dma("G", e2row_b[:, :], dt["e2row"], writes=[b_const])
        dma("G", ones2_b[:, :], dt["ones2row"], writes=[b_const])
        dma("G", projT2_b[:, :], dt["p_projT2"], writes=[b_const])
        p.emit("V", lambda e: e.memset(onescol_b[:, :], 1.0), writes=[b_const])
        p.emit("V", lambda e: e.memset(cbias[:, 0:1], 1.0e-5), writes=[b_const])
        p.emit("V", lambda e: e.memset(cbias[:, 1:2], 1.0e-30), writes=[b_const])

        xr = dt["x_t"].rearrange("(c p) t -> p c t", p=128)
        st3 = stream.ap().rearrange("p (c t) -> p c t", c=CH)
        xs3 = xs.ap().rearrange("p (c t) -> p c t", c=CH)
        dma("Q", st3, xr, writes=b_stream)
        dma("G", xs3, xr, writes=b_xs)

        def ln_stats_norm(src_sb, src_bufs, dst_f32, dst_bufs, dst_bf, dst_bf_buf,
                          g_col=None, b_colv=None):
            for tt in range(NTT):
                for c in range(CH):
                    src = src_sb[:, c * T + tt * TT: c * T + (tt + 1) * TT]
                    p.emit("S", lambda e, s=src: e.activation(sqq[:, :], s, AF.Square),
                           reads=[src_bufs[c]], writes=[b_sqq])

                    def mm(e, c=c, s=src):
                        e.matmul(banks[6][0:1, 0:TT], onescol_b[:, :], s,
                                 start=(c == 0), stop=(c == CH - 1),
                                 tile_position=(0, 0))
                        return e.matmul(banks[7][0:1, 0:TT], onescol_b[:, :],
                                        sqq[:, :], start=(c == 0), stop=(c == CH - 1),
                                        tile_position=(0, 0))
                    p.emit("T", mm, reads=[src_bufs[c], b_sqq, b_const],
                           writes=[b_bank[6], b_bank[7]])
                p.emit("V", lambda e: e.tensor_scalar(rowsA[0:1, 0:TT],
                                                      banks[6][0:1, 0:TT],
                                                      1.0 / D, None, ALU.mult),
                       reads=[b_bank[6]], writes=[b_rowsA])
                p.emit("V", lambda e: e.tensor_mul(rowsB, rowsA[0:1, 0:TT],
                                                   rowsA[0:1, 0:TT]),
                       reads=[b_rowsA], writes=[b_rowsB])
                p.emit("V", lambda e: e.scalar_tensor_tensor(
                    rowsB, banks[7][0:1, 0:TT], 1.0 / D, rowsB,
                    ALU.mult, ALU.subtract),
                    reads=[b_bank[7], b_rowsB], writes=[b_rowsB])
                p.emit("S", lambda e: e.activation(rowsB, rowsB, AF.Ln,
                                                   bias=cbias[0:1, 0:1]),
                       reads=[b_rowsB], writes=[b_rowsB])
                p.emit("S", lambda e: e.activation(rowsB, rowsB, AF.Exp, scale=-0.5),
                       reads=[b_rowsB], writes=[b_rowsB])
                p.emit("V", lambda e: e.tensor_mul(rowsA[0:1, 0:TT],
                                                   rowsA[0:1, 0:TT], rowsB),
                       reads=[b_rowsA, b_rowsB], writes=[b_rowsA])
                p.emit("V", lambda e: e.tensor_copy(rows_hl[0:1, 0:TT], rowsB),
                       reads=[b_rowsB], writes=[b_rhl])
                p.emit("V", lambda e: e.tensor_sub(rows_hl[0:1, TT:2 * TT], rowsB,
                                                   rows_hl[0:1, 0:TT]),
                       reads=[b_rowsB, b_rhl], writes=[b_rhl])
                p.emit("V", lambda e: e.tensor_copy(rows_hl[0:1, 2 * TT:3 * TT],
                                                    rowsA[0:1, 0:TT]),
                       reads=[b_rowsA], writes=[b_rhl])
                p.emit("V", lambda e: e.tensor_sub(rows_hl[0:1, 3 * TT:4 * TT],
                                                   rowsA[0:1, 0:TT],
                                                   rows_hl[0:1, 2 * TT:3 * TT]),
                       reads=[b_rowsA, b_rhl], writes=[b_rhl])

                def bc(e):
                    e.matmul(banks[4][:, 0:TT], ones2_b[0:1, :],
                             rows_hl[0:1, 0:TT], start=True, stop=False,
                             tile_position=(0, 0))
                    e.matmul(banks[4][:, 0:TT], ones2_b[0:1, :],
                             rows_hl[0:1, TT:2 * TT], start=False, stop=True,
                             tile_position=(0, 0))
                    e.matmul(banks[5][:, 0:TT], ones2_b[0:1, :],
                             rows_hl[0:1, 2 * TT:3 * TT], start=True, stop=False,
                             tile_position=(0, 0))
                    return e.matmul(banks[5][:, 0:TT], ones2_b[0:1, :],
                                    rows_hl[0:1, 3 * TT:4 * TT], start=False,
                                    stop=True, tile_position=(0, 0))
                p.emit("T", bc, reads=[b_rhl, b_const], writes=[b_bank[4], b_bank[5]])
                for c in range(CH):
                    if dst_f32 is not None:
                        ssl = dst_f32[:, c * T + tt * TT: c * T + (tt + 1) * TT]
                        p.emit("V", lambda e, s=ssl: e.tensor_mul(s, s, banks[4][:, 0:TT]),
                               reads=[b_bank[4], dst_bufs[c]], writes=[dst_bufs[c]])
                        p.emit("V", lambda e, s=ssl: e.tensor_sub(s, s, banks[5][:, 0:TT]),
                               reads=[b_bank[5], dst_bufs[c]], writes=[dst_bufs[c]])
                    else:
                        src = src_sb[:, c * T + tt * TT: c * T + (tt + 1) * TT]
                        p.emit("V", lambda e, s=src: e.tensor_mul(phi_f[:, :], s,
                                                                  banks[4][:, 0:TT]),
                               reads=[b_bank[4], src_bufs[c]], writes=[b_phif])
                        p.emit("V", lambda e: e.tensor_sub(phi_f[:, :], phi_f[:, :],
                                                           banks[5][:, 0:TT]),
                               reads=[b_bank[5]], writes=[b_phif])
                        dsl = dst_bf[:, c * T + tt * TT: c * T + (tt + 1) * TT]
                        if g_col is not None:
                            gc = g_col[:, c:c + 1]
                            bc2 = b_colv[:, c:c + 1]
                            p.emit("V", lambda e, d=dsl, g=gc, b2=bc2: e.tensor_scalar(
                                d, phi_f[:, :], g, b2, ALU.mult, ALU.add),
                                reads=[b_phif, b_gb], writes=[dst_bf_buf])
                        else:
                            p.emit("V", lambda e, d=dsl: e.tensor_copy(d, phi_f[:, :]),
                                   reads=[b_phif], writes=[dst_bf_buf])

        ln_stats_norm(xs, b_xs, stream, b_stream, None, None)

        if "xn" in tap_outs:
            for c in range(CH):
                dma("Q", tap_outs["xn"][:, c * T:(c + 1) * T],
                    stream[:, c * T:(c + 1) * T], reads=[b_stream[c]])

        VOFF = 0
        QOFF = 8192
        KOFF = 10240
        POFF = 12288
        POFF2 = 13824

        def cast_shift(li):
            for c in range(CH):
                for tt in range(NTT):
                    d0 = c * T + tt * TT
                    if c < 4:
                        src = stream[:, d0:d0 + TT]
                        p.emit("S", lambda e, s=src, d0=d0: e.activation(
                            xs[:, d0:d0 + TT], s, AF.Copy),
                            reads=[b_stream[c]], writes=[b_xs[c]])
                    elif tt > 0:
                        src = stream[:, d0 - 1:d0 + TT - 1]
                        p.emit("S", lambda e, s=src, d0=d0: e.activation(
                            xs[:, d0:d0 + TT], s, AF.Copy),
                            reads=[b_stream[c]], writes=[b_xs[c]])
                    else:
                        src = stream[:, d0:d0 + TT - 1]
                        p.emit("S", lambda e, s=src, d0=d0: e.activation(
                            xs[:, d0 + 1:d0 + TT], s, AF.Copy),
                            reads=[b_stream[c]], writes=[b_xs[c]])
                        p.emit("V", lambda e, d0=d0: e.memset(xs[:, d0:d0 + 1], 0.0),
                               writes=[b_xs[c]])

        def load_wslice(eng, dst, wname, cols, reads=(), writes=()):
            ap = dt[wname][:, cols].rearrange("(c p) n -> p c n", p=128)
            ncols = cols.stop - cols.start
            dst3 = dst.rearrange("p (c n) -> p c n", n=ncols)
            return dma(eng, dst3, ap, reads=reads, writes=writes)

        def attn_layer(li):
            cast_shift(li)
            dma("G", qs2c[:, 0:1], dt[f"qs2{li}"], writes=[b_qs2])
            dma("G", qs2c[:, 1:2], dt[f"ks2{li}"], writes=[b_qs2])
            for idx, nm in enumerate(("cs_q", "sn_q", "cs_k", "sn_k")):
                dma("G", wtab[:, idx * T:(idx + 1) * T], dt[nm], writes=[b_wtab])
            for idx in range(4):
                qcol = qs2c[:, 0:1] if idx < 2 else qs2c[:, 1:2]
                sl = wtab[:, idx * T:(idx + 1) * T]
                p.emit("V", lambda e, s=sl, q=qcol: e.tensor_scalar(
                    s, s, q, None, ALU.mult),
                    reads=[b_wtab, b_qs2], writes=[b_wtab])
            if f"l{li}_wtab" in tap_outs:
                dma("G", tap_outs[f"l{li}_wtab"][:, 0:4 * T], wtab[:, :],
                    reads=[b_wtab])
            load_wslice("G", wq_w[:, 0:D], f"wq{li}", slice(0, 128), writes=[b_wq[0]])
            load_wslice("G", wk_w[:, 0:D], f"wk{li}", slice(0, 128), writes=[b_wk[0]])
            def project_v(group):
                for sub in range(2):
                    load_wslice("G", wv_w[:, :], f"wv{li}",
                                slice(group * 512 + sub * 256,
                                      group * 512 + (sub + 1) * 256),
                                writes=[b_wv])
                    for ttok in range(NCK):
                        def mm(e, ttok=ttok):
                            last = None
                            for kc in range(CH):
                                last = e.matmul(
                                    banks[5][:, 0:256],
                                    xs[:, kc * T + ttok * 128:
                                       kc * T + (ttok + 1) * 128],
                                    wv_w[:, kc * 256:(kc + 1) * 256],
                                    start=(kc == 0), stop=(kc == CH - 1))
                            return last
                        p.emit("T", mm, reads=[b_wv] + b_xs, writes=[b_bank[5]])
                        p.emit("S", lambda e, ttok=ttok, sub=sub: e.activation(
                            scr2[:, VOFF + ttok * TT + sub * 256:
                                 VOFF + ttok * TT + (sub + 1) * 256],
                            banks[5][:, 0:256], AF.Copy),
                            reads=[b_bank[5]], writes=[b_scr2])

            for hp in range(NHP):
                buf = hp % 2
                nbuf = (hp + 1) % 2
                if hp == 0:
                    project_v(0)
                if hp == 4:
                    project_v(1)
                if hp + 1 < NHP:
                    load_wslice("G", wq_w[:, nbuf * D:(nbuf + 1) * D], f"wq{li}",
                                slice((hp + 1) * 128, (hp + 2) * 128), writes=[b_wq[nbuf]])
                    load_wslice("G", wk_w[:, nbuf * D:(nbuf + 1) * D], f"wk{li}",
                                slice((hp + 1) * 128, (hp + 2) * 128), writes=[b_wk[nbuf]])
                for tt in range(NTT):
                    def mmq(e, buf=buf, tt=tt):
                        last = None
                        for kc in range(CH):
                            last = e.matmul(
                                banks[0][:, :],
                                wq_w[:, buf * D + kc * 128: buf * D + (kc + 1) * 128],
                                xs[:, kc * T + tt * TT: kc * T + (tt + 1) * TT],
                                start=(kc == 0), stop=(kc == CH - 1))
                        return last
                    p.emit("T", mmq, reads=[b_wq[buf]] + b_xs, writes=[b_bank[0]])

                    def mmk(e, buf=buf, tt=tt):
                        last = None
                        for kc in range(CH):
                            last = e.matmul(
                                banks[1][:, :],
                                wk_w[:, buf * D + kc * 128: buf * D + (kc + 1) * 128],
                                xs[:, kc * T + tt * TT: kc * T + (tt + 1) * TT],
                                start=(kc == 0), stop=(kc == CH - 1))
                        return last
                    p.emit("T", mmk, reads=[b_wk[buf]] + b_xs, writes=[b_bank[1]])
                    tq = tt * TT
                    for side in range(2):
                        bnk = banks[side]
                        bbnk = b_bank[side]
                        rbank = banks[2 + side]
                        brbank = b_bank[2 + side]
                        rotb = banks[4 + side]
                        brotb = b_bank[4 + side]
                        toff = (0 if side == 0 else 2 * T)
                        doff = QOFF if side == 0 else KOFF
                        p.emit("S", lambda e, bnk=bnk: e.activation(
                            sqq[:, :], bnk[:, :], AF.Square),
                            reads=[bbnk], writes=[b_sqq])
                        p.emit("T", lambda e, rbank=rbank: e.matmul(
                            rbank[0:2, :], e2col_b[:, :], sqq[:, :],
                            start=True, stop=True, tile_position=(0, 0)),
                            reads=[b_sqq, b_const], writes=[brbank])
                        p.emit("S", lambda e, rbank=rbank: e.activation(
                            krows[0:2, :], rbank[0:2, :], AF.Ln,
                            bias=cbias[0:2, 1:2]),
                            reads=[brbank, b_rowsB], writes=[b_krows])
                        p.emit("S", lambda e: e.activation(
                            krows[0:2, :], krows[0:2, :], AF.Exp, scale=-0.5),
                            reads=[b_krows], writes=[b_krows])
                        p.emit("V", lambda e: e.tensor_scalar_min(
                            rk_rows_b[0:2, 0:512], krows[0:2, :], 1.0e12),
                            reads=[b_krows], writes=[b_rkb])
                        p.emit("T", lambda e, rbank=rbank: e.matmul(
                            rbank[:, :], e2row_b[:, :], rk_rows_b[0:2, 0:512],
                            start=True, stop=True, tile_position=(0, 0)),
                            reads=[b_rkb, b_const], writes=[brbank])
                        p.emit("V", lambda e, bnk=bnk, toff=toff, tq=tq: e.tensor_mul(
                            qBt[:, :], bnk[:, :],
                            wtab[:, T + toff + tq:T + toff + tq + TT]),
                            reads=[bbnk, b_wtab], writes=[b_qBt])
                        p.emit("V", lambda e, bnk=bnk, toff=toff, tq=tq: e.tensor_mul(
                            m1q[:, :], bnk[:, :], wtab[:, toff + tq:toff + tq + TT]),
                            reads=[bbnk, b_wtab], writes=[b_m1q])
                        p.emit("T", lambda e, rotb=rotb: e.matmul(
                            rotb[:, :], rotT_b[:, :], qBt[:, :],
                            start=True, stop=True),
                            reads=[b_qBt, b_const], writes=[brotb])
                        p.emit("V", lambda e, rotb=rotb: e.scalar_tensor_tensor(
                            kfin[:, :], rotb[:, :], 1.0, m1q[:, :],
                            ALU.mult, ALU.add),
                            reads=[brotb, b_m1q], writes=[b_kfin])
                        p.emit("V", lambda e, rbank=rbank, doff=doff, tq=tq: e.tensor_mul(
                            scr2[:, doff + tq: doff + tq + TT], kfin[:, :],
                            rbank[:, :]),
                            reads=[b_kfin, brbank], writes=[b_scr2])
                        if hp == 0 and tt == 0 and f"l{li}_misc" in tap_outs:
                            base = side * 1536
                            dma("Q", tap_outs[f"l{li}_misc"][0:2, base:base + 512],
                                krows[0:2, :], reads=[b_krows])
                            dma("G", tap_outs[f"l{li}_misc"][0:2, base + 512:base + 1024],
                                rk_rows_b[0:2, 0:512], reads=[b_rkb])
                            dma("G", tap_outs[f"l{li}_misc"][:, base + 1024:base + 1536],
                                sqq[:, :], reads=[b_sqq])
                if hp == 0 and f"l{li}_scr2b" in tap_outs:
                    dma("G", tap_outs[f"l{li}_scr2b"][:, :], scr2[:, :],
                        reads=[b_scr2])
                import os as _os
                for w in (range(NW) if not _os.environ.get("SKIP_WIN") else []):
                    insts = []
                    for qidx in range(2):
                        qt = 2 * w + qidx
                        if w == 0:
                            W = 128 if qidx == 0 else 256
                            ks0 = 0
                        else:
                            W = 384
                            ks0 = qt * 128 - 256
                        for hh in range(2):
                            insts.append((qidx, hh, qt, W, ks0, 4 + qidx * 2 + hh))
                    for qidx, hh, qt, W, ks0, bk in insts:
                        def mmsim(e, qt=qt, ks0=ks0, W=W, hh=hh, bk=bk):
                            return e.matmul(
                                banks[bk][:, 0:W],
                                scr2[64 * hh:64 * hh + 64,
                                     QOFF + qt * 128: QOFF + (qt + 1) * 128],
                                scr2[64 * hh:64 * hh + 64,
                                     KOFF + ks0: KOFF + ks0 + W],
                                start=True, stop=True,
                                tile_position=(64 * hh, 0))
                        p.emit("T", mmsim, reads=[b_scr2], writes=[b_bank[bk]])
                    for qidx, hh, qt, W, ks0, bk in insts:
                        sb = banks[bk]
                        bsb = b_bank[bk]
                        if w == 0:
                            moff = qidx * 128
                            p.emit("V", lambda e, sb=sb, moff=moff: e.tensor_add(
                                sb[:, moff:moff + 128], sb[:, moff:moff + 128],
                                masklu[:, 128:256]),
                                reads=[b_const, bsb], writes=[bsb])
                        else:
                            p.emit("V", lambda e, sb=sb: e.tensor_add(
                                sb[:, 0:128], sb[:, 0:128], masklu[:, 0:128]),
                                reads=[b_const, bsb], writes=[bsb])
                            p.emit("V", lambda e, sb=sb: e.tensor_add(
                                sb[:, 256:384], sb[:, 256:384], masklu[:, 128:256]),
                                reads=[b_const, bsb], writes=[bsb])
                    for qidx, hh, qt, W, ks0, bk in insts:
                        ccol = qidx * 2 + hh
                        p.emit("V", lambda e, bk=bk, W=W, ccol=ccol: e.reduce_max(
                            mneg[:, ccol:ccol + 1], banks[bk][:, 0:W], axis=AX.X,
                            negate=True),
                            reads=[b_bank[bk]], writes=[b_mneg])
                    for qidx, hh, qt, W, ks0, bk in insts:
                        ccol = qidx * 2 + hh
                        poff = POFF + ccol * 384
                        p.emit("S", lambda e, bk=bk, W=W, ccol=ccol,
                               poff=poff: e.activation(
                            scr2[:, poff:poff + W], banks[bk][:, 0:W], AF.Exp,
                            bias=mneg[:, ccol:ccol + 1]),
                            reads=[b_bank[bk], b_mneg], writes=[b_scr2])
                    for qidx, hh, qt, W, ks0, bk in insts:
                        ccol = qidx * 2 + hh
                        poff = POFF + ccol * 384
                        p.emit("V", lambda e, W=W, ccol=ccol, poff=poff:
                               e.reduce_sum(srow[:, ccol:ccol + 1],
                                            scr2[:, poff:poff + W], axis=AX.X),
                               reads=[b_scr2], writes=[b_srow])
                    for qidx, hh, qt, W, ks0, bk in insts:
                        ccol = qidx * 2 + hh
                        p.emit("V", lambda e, ccol=ccol: e.reciprocal(
                            srow[:, ccol:ccol + 1], srow[:, ccol:ccol + 1]),
                            reads=[b_srow], writes=[b_srow])
                    for qidx, hh, qt, W, ks0, bk in insts:
                        ccol = qidx * 2 + hh
                        poff = POFF + ccol * 384
                        p.emit("V", lambda e, W=W, ccol=ccol, poff=poff:
                               e.tensor_scalar(scr2[:, poff:poff + W],
                                               scr2[:, poff:poff + W],
                                               srow[:, ccol:ccol + 1], None,
                                               ALU.mult),
                               reads=[b_srow, b_scr2], writes=[b_scr2])
                    for qidx, hh, qt, W, ks0, bk in insts:
                        ccol = qidx * 2 + hh
                        poff = POFF + ccol * 384
                        for cpc in range(W // 128):
                            dmaT(scr2[:, POFF2 + ccol * 384 + cpc * 128:
                                      POFF2 + ccol * 384 + (cpc + 1) * 128],
                                 scr2[:, poff + cpc * 128: poff + (cpc + 1) * 128],
                                 reads=[b_scr2], writes=[b_scr2])
                    for qidx, hh, qt, W, ks0, bk in insts:
                        ccol = qidx * 2 + hh

                        def mmo(e, hh=hh, ccol=ccol, qidx=qidx, ks0=ks0, W=W, hp=hp):
                            last = None
                            npc = W // 128
                            for cpc in range(npc):
                                jt = ks0 // 128 + cpc
                                voff = VOFF + jt * TT + (hp % 4) * 128 + 64 * hh
                                last = e.matmul(
                                    banks[0][64 * hh:64 * hh + 64,
                                             qidx * 128:(qidx + 1) * 128],
                                    scr2[:, voff: voff + 64],
                                    scr2[:, POFF2 + ccol * 384 + cpc * 128:
                                         POFF2 + ccol * 384 + (cpc + 1) * 128],
                                    start=(cpc == 0), stop=(cpc == npc - 1),
                                    tile_position=(0, 64 * hh))
                            return last
                        p.emit("T", mmo, reads=[b_scr2], writes=[b_bank[0]])
                    p.emit("S", lambda e, hp=hp, w=w: e.activation(
                        big[:, hp * T + w * 256: hp * T + (w + 1) * 256],
                        banks[0][:, 0:256], AF.Copy),
                        reads=[b_bank[0]], writes=[b_big])
                if hp == 0 and f"l{li}_scr2" in tap_outs:
                    dma("G", tap_outs[f"l{li}_scr2"][:, :], scr2[:, :],
                        reads=[b_scr2])

            if f"l{li}_ao" in tap_outs:
                dma("G", tap_outs[f"l{li}_ao"][:, :], big[:, :], reads=[b_big])
            for f2 in range(CH):
                load_wslice("G", wo_w[:, 0:D], f"wo{li}",
                            slice(f2 * 128, (f2 + 1) * 128), writes=[b_wo[0]])
                for tt in range(NTT):
                    bk = 2 + (tt % 2)

                    def mmwo(e, tt=tt, bk=bk):
                        last = None
                        for kc in range(CH):
                            last = e.matmul(
                                banks[bk][:, :],
                                wo_w[:, kc * 128: (kc + 1) * 128],
                                big[:, kc * T + tt * TT: kc * T + (tt + 1) * TT],
                                start=(kc == 0), stop=(kc == CH - 1))
                        return last
                    p.emit("T", mmwo, reads=[b_wo[0], b_big], writes=[b_bank[bk]])
                    p.emit("V", lambda e, f2=f2, tt=tt, bk=bk: e.tensor_add(
                        stream[:, f2 * T + tt * TT: f2 * T + (tt + 1) * TT],
                        stream[:, f2 * T + tt * TT: f2 * T + (tt + 1) * TT],
                        banks[bk][:, :]),
                        reads=[b_bank[bk], b_stream[f2]], writes=[b_stream[f2]])

        def ffn_layer(li):
            cast_shift(li)
            dma("G", gb_col[:, 0:CH], dt[f"lng{li}"], writes=[b_gb])
            dma("G", gb_col[:, CH:2 * CH], dt[f"lnb{li}"], writes=[b_gb])
            dma("G", b1col[:, 0:NJ], dt[f"b1a{li}"], writes=[b_b1c])
            dma("G", b1col[:, NJ:2 * NJ], dt[f"b1g{li}"], writes=[b_b1c])
            dma("G", b2col[:, :], dt[f"b2c{li}"], writes=[b_b2c])
            ln_stats_norm(xs, b_xs, None, None, scr2, b_scr2,
                          gb_col[:, 0:CH], gb_col[:, CH:2 * CH])
            jblocks = [list(range(0, 8)), list(range(8, 16)), list(range(16, NJ))]
            w1bufs = [wq_w, wk_w]
            b_w1 = [Buf("w10"), Buf("w11")]
            Prog.seed(b_w1, [b_wq[0], b_wq[1], b_wk[0], b_wk[1]])
            load_wslice("G", w1bufs[0][:, 0:D], f"w1a{li}", slice(0, 128),
                        writes=[b_w1[0]])
            load_wslice("G", w1bufs[0][:, D:2 * D], f"w1g{li}", slice(0, 128),
                        writes=[b_w1[0]])
            b_w2s = [Buf(f"w2s{j}") for j in range(8)]
            b_hh = [Buf(f"hh{j}") for j in range(8)]
            Prog.seed(b_hh, [b_big])
            jcount = 0
            for bi, jb in enumerate(jblocks):
                for idx, j in enumerate(jb):
                    buf = jcount % 2
                    jcount += 1
                    nj_global = j + 1
                    if nj_global < NJ:
                        nbuf = jcount % 2
                        ncols = slice(nj_global * 128, (nj_global + 1) * 128)
                        load_wslice("G", w1bufs[nbuf][:, 0:D],
                                    f"w1a{li}", ncols, writes=[b_w1[nbuf]])
                        load_wslice("G", w1bufs[nbuf][:, D:2 * D],
                                    f"w1g{li}", ncols, writes=[b_w1[nbuf]])
                    for tt in range(NTT):
                        def mma(e, buf=buf, tt=tt):
                            last = None
                            for kc in range(CH):
                                last = e.matmul(
                                    banks[0][:, :],
                                    w1bufs[buf][:, kc * 128:(kc + 1) * 128],
                                    scr2[:, kc * T + tt * TT: kc * T + (tt + 1) * TT],
                                    start=(kc == 0), stop=(kc == CH - 1))
                            return last
                        p.emit("T", mma, reads=[b_w1[buf], b_scr2], writes=[b_bank[0]])

                        def mmg(e, buf=buf, tt=tt):
                            last = None
                            for kc in range(CH):
                                last = e.matmul(
                                    banks[1][:, :],
                                    w1bufs[buf][:, D + kc * 128: D + (kc + 1) * 128],
                                    scr2[:, kc * T + tt * TT: kc * T + (tt + 1) * TT],
                                    start=(kc == 0), stop=(kc == CH - 1))
                            return last
                        p.emit("T", mmg, reads=[b_w1[buf], b_scr2], writes=[b_bank[1]])
                        p.emit("S", lambda e, j=j: e.activation(
                            sqk[:, :], banks[1][:, :], AF.Gelu,
                            bias=b1col[:, NJ + j:NJ + j + 1]),
                            reads=[b_bank[1], b_b1c], writes=[b_sqk])
                        p.emit("V", lambda e, j=j, idx=idx, tt=tt:
                               e.scalar_tensor_tensor(
                            big[:, idx * T + tt * TT: idx * T + (tt + 1) * TT],
                            banks[0][:, :], b1col[:, j:j + 1], sqk[:, :],
                            ALU.add, ALU.mult),
                            reads=[b_bank[0], b_b1c, b_sqk], writes=[b_hh[idx]])
                    dma("G", wtab[:, idx * D:(idx + 1) * D],
                        dt[f"w2{li}"][j * 128:(j + 1) * 128, :],
                        writes=[b_wtab, b_w2s[idx]])
                for tt in range(NTT):
                    for f2 in range(CH):
                        bk = 2 + (f2 % 4)

                        def mmw2(e, jb=jb, tt=tt, f2=f2, bk=bk):
                            last = None
                            for idx in range(len(jb)):
                                last = e.matmul(
                                    banks[bk][:, :],
                                    wtab[:, idx * D + f2 * 128:
                                         idx * D + (f2 + 1) * 128],
                                    big[:, idx * T + tt * TT:
                                        idx * T + (tt + 1) * TT],
                                    start=(idx == 0), stop=(idx == len(jb) - 1))
                            return last
                        p.emit("T", mmw2,
                               reads=[b_wtab] + [b_w2s[i2] for i2 in range(len(jb))] +
                                     [b_hh[i2] for i2 in range(len(jb))],
                               writes=[b_bank[bk]])
                        if bi == 0:
                            p.emit("V", lambda e, f2=f2, tt=tt, bk=bk:
                                   e.scalar_tensor_tensor(
                                stream[:, f2 * T + tt * TT: f2 * T + (tt + 1) * TT],
                                banks[bk][:, :], b2col[:, f2:f2 + 1],
                                stream[:, f2 * T + tt * TT: f2 * T + (tt + 1) * TT],
                                ALU.add, ALU.add),
                                reads=[b_bank[bk], b_b2c, b_stream[f2]],
                                writes=[b_stream[f2]])
                        else:
                            p.emit("V", lambda e, f2=f2, tt=tt, bk=bk: e.tensor_add(
                                stream[:, f2 * T + tt * TT: f2 * T + (tt + 1) * TT],
                                stream[:, f2 * T + tt * TT: f2 * T + (tt + 1) * TT],
                                banks[bk][:, :]),
                                reads=[b_bank[bk], b_stream[f2]],
                                writes=[b_stream[f2]])
            Prog.seed([b_big], b_hh + b_w2s)
            Prog.seed([b_wq[0], b_wq[1]], [b_w1[0]])
            Prog.seed([b_wk[0], b_wk[1]], [b_w1[1]])

        for li in range(n_layers):
            attn_layer(li)
            if f"l{li}_res1" in tap_outs:
                for c in range(CH):
                    dma("Q", tap_outs[f"l{li}_res1"][:, c * T:(c + 1) * T],
                        stream[:, c * T:(c + 1) * T], reads=[b_stream[c]])
            ffn_layer(li)
            if f"l{li}_res2" in tap_outs:
                for c in range(CH):
                    dma("Q", tap_outs[f"l{li}_res2"][:, c * T:(c + 1) * T],
                        stream[:, c * T:(c + 1) * T], reads=[b_stream[c]])

        if with_performer:
            for c in range(CH):
                for tt in range(NTT):
                    sl = stream[:, c * T + tt * TT: c * T + (tt + 1) * TT]
                    p.emit("S", lambda e, s=sl: e.activation(s, s, AF.Silu),
                           reads=[b_stream[c]], writes=[b_stream[c]])
            if "perf_g" in tap_outs:
                for c in range(CH):
                    dma("Q", tap_outs["perf_g"][:, c * T:(c + 1) * T],
                        stream[:, c * T:(c + 1) * T], reads=[b_stream[c]])
            dma("G", xs3, xr, writes=b_xs)
            ln_stats_norm(xs, b_xs, None, None, scr2, b_scr2, None, None)
            dma("G", pbias[:, 0:CH], dt["p_bq"], writes=[b_pbias])
            dma("G", pbias[:, CH:2 * CH], dt["p_bk"], writes=[b_pbias])
            dma("G", pbias[:, 2 * CH:3 * CH], dt["p_bo"], writes=[b_pbias])
            Prog.seed([b_pbvr], [b_wtab])
            dma("G", wtab[0:1, 6 * T // 2: 6 * T // 2 + D], dt["p_bv_row"],
                writes=[b_pbvr])
            ic = float(1.0 / (DKH ** 0.25))
            p.emit("V", lambda e: e.tensor_scalar(pbias[:, 0:2 * CH],
                                                  pbias[:, 0:2 * CH], ic, None,
                                                  ALU.mult),
                   reads=[b_pbias], writes=[b_pbias])

            PQ = 0
            PK = 2048
            PV = 4096
            b_qp = Buf("qp"); b_kp = Buf("kp"); b_pv = Buf("pv")
            Prog.seed([b_qp, b_kp, b_pv], b_xs)

            load_wslice("G", wq_w[:, 0:D], "p_wq", slice(0, 128),
                        reads=[b_wq[0]], writes=[b_wq[0]])
            load_wslice("G", wk_w[:, 0:D], "p_wk", slice(0, 128),
                        reads=[b_wk[0]], writes=[b_wk[0]])

            import os as _os2
            for h in range(int(_os2.environ.get('PERF_HEADS', '16'))):
                hp = h // 2
                half = h % 2
                if half == 0:
                    if hp + 1 < 8:
                        pass
                    for tt in range(NTT):
                        def mmq(e, tt=tt):
                            last = None
                            for kc in range(CH):
                                last = e.matmul(
                                    banks[0][:, :],
                                    wq_w[:, kc * 128:(kc + 1) * 128],
                                    scr2[:, kc * T + tt * TT: kc * T + (tt + 1) * TT],
                                    start=(kc == 0), stop=(kc == CH - 1))
                            return last
                        p.emit("T", mmq, reads=[b_wq[0], b_scr2], writes=[b_bank[0]])
                        p.emit("S", lambda e, tt=tt, hp=hp: e.activation(
                            xs[:, PQ + tt * TT: PQ + (tt + 1) * TT], banks[0][:, :],
                            AF.Identity, bias=pbias[:, hp:hp + 1], scale=ic),
                            reads=[b_bank[0], b_pbias], writes=[b_qp])

                        def mmk(e, tt=tt):
                            last = None
                            for kc in range(CH):
                                last = e.matmul(
                                    banks[1][:, :],
                                    wk_w[:, kc * 128:(kc + 1) * 128],
                                    scr2[:, kc * T + tt * TT: kc * T + (tt + 1) * TT],
                                    start=(kc == 0), stop=(kc == CH - 1))
                            return last
                        p.emit("T", mmk, reads=[b_wk[0], b_scr2], writes=[b_bank[1]])
                        p.emit("S", lambda e, tt=tt, hp=hp: e.activation(
                            xs[:, PK + tt * TT: PK + (tt + 1) * TT], banks[1][:, :],
                            AF.Identity, bias=pbias[:, CH + hp:CH + hp + 1], scale=ic),
                            reads=[b_bank[1], b_pbias], writes=[b_kp])
                    if hp + 1 < 8:
                        load_wslice("G", wq_w[:, 0:D], "p_wq",
                                    slice((hp + 1) * 128, (hp + 2) * 128),
                                    reads=[b_qp], writes=[b_wq[0]])
                        load_wslice("G", wk_w[:, 0:D], "p_wk",
                                    slice((hp + 1) * 128, (hp + 2) * 128),
                                    reads=[b_kp], writes=[b_wk[0]])
                if h == 0 or h == 8:
                    grp = h // 8
                    for sub in range(2):
                        load_wslice("G", wv_w[:, :], "p_wv",
                                    slice(grp * 512 + sub * 256,
                                          grp * 512 + (sub + 1) * 256),
                                    writes=[b_wv])
                        for ttok in range(NCK):
                            def mmv(e, ttok=ttok, grp=grp, sub=sub):
                                for kc in range(CH):
                                    e.matmul(
                                        banks[5][:, 0:256],
                                        scr2[:, kc * T + ttok * 128:
                                             kc * T + (ttok + 1) * 128],
                                        wv_w[:, kc * 256:(kc + 1) * 256],
                                        start=(kc == 0), stop=False)
                                boff = 6 * T // 2 + grp * 512 + sub * 256
                                return e.matmul(
                                    banks[5][:, 0:256], ones2_b[0:1, :],
                                    wtab[0:1, boff:boff + 256],
                                    start=False, stop=True, tile_position=(0, 0))
                            p.emit("T", mmv, reads=[b_scr2, b_wv, b_pbvr, b_const],
                                   writes=[b_bank[5]])
                            p.emit("S", lambda e, ttok=ttok, sub=sub: e.activation(
                                xs[:, PV + ttok * TT + sub * 256:
                                   PV + ttok * TT + (sub + 1) * 256],
                                banks[5][:, 0:256], AF.Copy),
                                reads=[b_bank[5]], writes=[b_pv])

                for tt in range(NTT):
                    p.emit("S", lambda e, tt=tt: e.activation(
                        sqq[:, :], xs[:, PQ + tt * TT: PQ + (tt + 1) * TT], AF.Square),
                        reads=[b_qp], writes=[b_sqq])

                    def mmssq(e, tt=tt, half=half):
                        last = None
                        for g in range(4):
                            col = tt * 4 + g
                            last = e.matmul(
                                banks[7][:, col:col + 1],
                                sqq[64 * half:64 * half + 64,
                                    g * 128:(g + 1) * 128],
                                onescol_b[64 * half:64 * half + 64, :],
                                start=True, stop=True,
                                tile_position=(64 * half, 0))
                        return last
                    p.emit("T", mmssq, reads=[b_sqq, b_const], writes=[b_bank[7]])
                    p.emit("S", lambda e, tt=tt: e.activation(
                        sqk[:, :], xs[:, PK + tt * TT: PK + (tt + 1) * TT], AF.Square),
                        reads=[b_kp], writes=[b_sqk])

                    def mmssk(e, tt=tt, half=half):
                        last = None
                        for g in range(4):
                            col = 16 + tt * 4 + g
                            last = e.matmul(
                                banks[7][:, col:col + 1],
                                sqk[64 * half:64 * half + 64,
                                    g * 128:(g + 1) * 128],
                                onescol_b[64 * half:64 * half + 64, :],
                                start=True, stop=True,
                                tile_position=(64 * half, 0))
                        return last
                    p.emit("T", mmssk, reads=[b_sqk, b_const], writes=[b_bank[7]])
                p.emit("V", lambda e: e.tensor_scalar(
                    rq_cols[:, 0:32], banks[7][:, 0:32], -0.5, -LNM,
                    ALU.mult, ALU.add),
                    reads=[b_bank[7]], writes=[b_rq])
                for ttok in range(NCK):
                    tcol = ttok // 4 * 4 + ttok % 4
                    def mmuq(e, ttok=ttok, half=half):
                        return e.matmul(
                            banks[2][:, 0:128],
                            xs[64 * half:64 * half + 64,
                               PQ + ttok * 128: PQ + (ttok + 1) * 128],
                            projT2_b[64 * half:64 * half + 64, :],
                            start=True, stop=True, tile_position=(64 * half, 0))
                    p.emit("T", mmuq, reads=[b_qp, b_const], writes=[b_bank[2]])
                    p.emit("S", lambda e, ttok=ttok: e.activation(
                        phi_f[:, 0:128], banks[2][:, 0:128], AF.Exp,
                        bias=rq_cols[:, ttok:ttok + 1]),
                        reads=[b_bank[2], b_rq], writes=[b_phif])
                    p.emit("V", lambda e: e.tensor_scalar(
                        sqq[:, 0:128], phi_f[:, 0:128], 1.0e-6, None, ALU.add),
                        reads=[b_phif], writes=[b_sqq])
                    dmaT(wtab[:, 2 * T + ttok * 128: 2 * T + (ttok + 1) * 128],
                         sqq[:, 0:128], reads=[b_sqq], writes=[b_wtab])

                    def mmuk(e, ttok=ttok, half=half):
                        return e.matmul(
                            banks[3][:, 0:128],
                            xs[64 * half:64 * half + 64,
                               PK + ttok * 128: PK + (ttok + 1) * 128],
                            projT2_b[64 * half:64 * half + 64, :],
                            start=True, stop=True, tile_position=(64 * half, 0))
                    p.emit("T", mmuk, reads=[b_kp, b_const], writes=[b_bank[3]])
                    p.emit("S", lambda e, ttok=ttok: e.activation(
                        phi_f[:, 128:256], banks[3][:, 0:128], AF.Exp,
                        bias=rq_cols[:, 16 + ttok:16 + ttok + 1]),
                        reads=[b_bank[3], b_rq], writes=[b_phif])
                    p.emit("V", lambda e, ttok=ttok: e.tensor_scalar(
                        wtab[:, ttok * 128:(ttok + 1) * 128], phi_f[:, 128:256],
                        1.0e-6, None, ALU.add),
                        reads=[b_phif], writes=[b_wtab])
                    dmaT(wtab[:, T + ttok * 128: T + (ttok + 1) * 128],
                         wtab[:, ttok * 128:(ttok + 1) * 128],
                         reads=[b_wtab], writes=[b_wtab])
                p.emit("V", lambda e: e.memset(state_S[:, :], 0.0), writes=[b_state])
                p.emit("V", lambda e: e.memset(state_Sb[:, :], 0.0), writes=[b_state])
                p.emit("V", lambda e: e.memset(state_z[:, :], 0.0), writes=[b_state])
                p.emit("V", lambda e: e.memset(state_zb[:, :], 0.0), writes=[b_state])
                for n in range(NCK):
                    qsl = wtab[:, 2 * T + n * 128: 2 * T + (n + 1) * 128]
                    p.emit("T", lambda e, qsl=qsl, half=half: e.matmul(
                        banks[4][64 * half:64 * half + 64, 0:128],
                        state_Sb[:, :], qsl, start=True, stop=False,
                        tile_position=(0, 64 * half)),
                        reads=[b_state, b_wtab], writes=[b_bank[4]])
                    p.emit("T", lambda e, n=n, qsl=qsl: e.matmul(
                        banks[6][:, 0:128], wtab[:, T + n * 128: T + (n + 1) * 128],
                        qsl, start=True, stop=True),
                        reads=[b_wtab], writes=[b_bank[6]])
                    p.emit("V", lambda e: e.tensor_mul(m1q[:, 0:128],
                                                       banks[6][:, 0:128],
                                                       triu_b[:, :]),
                           reads=[b_bank[6], b_const], writes=[b_m1q])
                    def mmden(e, qsl=qsl):
                        e.matmul(banks[6][0:1, 128:256], state_zb[:, :], qsl,
                                 start=True, stop=False, tile_position=(0, 0))
                        return e.matmul(banks[6][0:1, 128:256], onescol_b[:, :],
                                        m1q[:, 0:128], start=False, stop=True,
                                        tile_position=(0, 0))
                    p.emit("T", mmden, reads=[b_state, b_wtab, b_m1q, b_const],
                           writes=[b_bank[6]])
                    p.emit("T", lambda e, n=n, half=half, hp=hp: e.matmul(
                        banks[4][64 * half:64 * half + 64, 0:128],
                        xs[:, PV + n * TT + (hp % 4) * 128 + 64 * half:
                           PV + n * TT + (hp % 4) * 128 + 64 * half + 64],
                        m1q[:, 0:128], start=False, stop=True,
                        tile_position=(0, 64 * half)),
                        reads=[b_pv, b_m1q], writes=[b_bank[4]])
                    p.emit("V", lambda e: e.tensor_scalar(
                        denrow[0:1, 0:128], banks[6][0:1, 128:256], 1.0e-6, None,
                        ALU.add),
                        reads=[b_bank[6], b_rowsB], writes=[b_denrow])
                    p.emit("V", lambda e: e.reciprocal(denrow[0:1, 0:128],
                                                       denrow[0:1, 0:128]),
                           reads=[b_denrow], writes=[b_denrow])
                    p.emit("V", lambda e: e.tensor_copy(denrow_b[0:1, 0:128],
                                                        denrow[0:1, 0:128]),
                           reads=[b_denrow, b_rhl], writes=[b_denrow_b])
                    p.emit("V", lambda e: e.tensor_sub(denrow_b[0:1, 128:256],
                                                       denrow[0:1, 0:128],
                                                       denrow_b[0:1, 0:128]),
                           reads=[b_denrow, b_denrow_b], writes=[b_denrow_b])

                    def mmrden(e, half=half):
                        e.matmul(banks[5][64 * half:64 * half + 64, 0:128],
                                 ones2_b[0:1, 0:64], denrow_b[0:1, 0:128],
                                 start=True, stop=False, tile_position=(0, 64 * half))
                        return e.matmul(banks[5][64 * half:64 * half + 64, 0:128],
                                        ones2_b[0:1, 0:64], denrow_b[0:1, 128:256],
                                        start=False, stop=True,
                                        tile_position=(0, 64 * half))
                    p.emit("T", mmrden, reads=[b_denrow_b, b_const],
                           writes=[b_bank[5]])
                    p.emit("S", lambda e, half=half: e.activation(
                        kBt[64 * half:64 * half + 64, 0:128],
                        banks[5][64 * half:64 * half + 64, 0:128], AF.Copy),
                        reads=[b_bank[5]], writes=[b_kBt])
                    p.emit("V", lambda e, half=half, hp=hp, n=n: e.tensor_mul(
                        big[64 * half:64 * half + 64,
                            hp * T + n * 128: hp * T + (n + 1) * 128],
                        banks[4][64 * half:64 * half + 64, 0:128],
                        kBt[64 * half:64 * half + 64, 0:128]),
                        reads=[b_bank[4], b_kBt], writes=[b_big])
                    def mmkv(e, n=n, half=half, hp=hp):
                        e.matmul(banks[2][:, 0:DKH],
                                 wtab[:, n * 128:(n + 1) * 128],
                                 xs[:, PV + n * TT + (hp % 4) * 128 + 64 * half:
                                    PV + n * TT + (hp % 4) * 128 + 64 * half + 64],
                                 start=True, stop=True)
                        return e.matmul(banks[2][:, 64:65],
                                        wtab[:, n * 128:(n + 1) * 128],
                                        onescol_b[:, :], start=True, stop=True)
                    p.emit("T", mmkv, reads=[b_wtab, b_pv, b_const],
                           writes=[b_bank[2]])
                    p.emit("V", lambda e: e.tensor_add(state_S[:, :], state_S[:, :],
                                                       banks[2][:, 0:DKH]),
                           reads=[b_bank[2], b_state], writes=[b_state])
                    p.emit("V", lambda e: e.tensor_copy(state_Sb[:, :],
                                                        state_S[:, :]),
                           reads=[b_state], writes=[b_state])
                    p.emit("V", lambda e: e.tensor_add(state_z[:, :], state_z[:, :],
                                                       banks[2][:, 64:65]),
                           reads=[b_bank[2], b_state], writes=[b_state])
                    p.emit("V", lambda e: e.tensor_copy(state_zb[:, :],
                                                        state_z[:, :]),
                           reads=[b_state], writes=[b_state])

            if "perf_ao" in tap_outs:
                dma("G", tap_outs["perf_ao"][:, :], big[:, :], reads=[b_big])
            if "perf_xn" in tap_outs:
                dma("G", tap_outs["perf_xn"][:, :], scr2[:, :], reads=[b_scr2])
            if "perf_qf" in tap_outs:
                dma("G", tap_outs["perf_qf"][:, 0:4 * T], wtab[:, :], reads=[b_wtab])
            for f2 in range(CH):
                load_wslice("G", wo_w[:, 0:D], "p_wo",
                            slice(f2 * 128, (f2 + 1) * 128), writes=[b_wo[0]])
                for tt in range(NTT):
                    bk = 2 + (tt % 2)

                    def mmwo(e, tt=tt, bk=bk):
                        last = None
                        for kc in range(CH):
                            last = e.matmul(
                                banks[bk][:, :],
                                wo_w[:, kc * 128: (kc + 1) * 128],
                                big[:, kc * T + tt * TT: kc * T + (tt + 1) * TT],
                                start=(kc == 0), stop=(kc == CH - 1))
                        return last
                    p.emit("T", mmwo, reads=[b_wo[0], b_big], writes=[b_bank[bk]])
                    p.emit("V", lambda e, f2=f2, tt=tt, bk=bk:
                           e.scalar_tensor_tensor(
                        phi_f[:, :], banks[bk][:, :], pbias[:, 2 * CH + f2:2 * CH + f2 + 1],
                        stream[:, f2 * T + tt * TT: f2 * T + (tt + 1) * TT],
                        ALU.add, ALU.mult),
                        reads=[b_bank[bk], b_pbias, b_stream[f2]], writes=[b_phif])
                    dma("Q", stream[:, f2 * T + tt * TT: f2 * T + (tt + 1) * TT],
                        dt["x_t"][f2 * 128:(f2 + 1) * 128, tt * TT:(tt + 1) * TT],
                        writes=[b_stream[f2]])
                    p.emit("V", lambda e, f2=f2, tt=tt: e.tensor_add(
                        phi_f[:, :], phi_f[:, :],
                        stream[:, f2 * T + tt * TT: f2 * T + (tt + 1) * TT]),
                        reads=[b_stream[f2]], writes=[b_phif])
                    dma("Q", y_t[f2 * 128:(f2 + 1) * 128, tt * TT:(tt + 1) * TT],
                        phi_f[:, :], reads=[b_phif], writes=[b_y])
        else:
            for c in range(CH):
                dma("Q", y_t[c * 128:(c + 1) * 128, :],
                    stream[:, c * T:(c + 1) * T], reads=[b_stream[c]], writes=[b_y])

        p.barrier_q([("DQ", p.cnt["DQ"]), ("DG", p.cnt["DG"])])
        p.finalize(blk, sems)
    return nc


_BUILD_CACHE = {}


def kernel(**inputs):
    maps = host_prepare(inputs)
    key = (N_LAYERS, WITH_PERFORMER, tuple(DEBUG_TAPS))
    if key not in _BUILD_CACHE:
        _BUILD_CACHE[key] = build_program(N_LAYERS, WITH_PERFORMER, DEBUG_TAPS)
    nc = _BUILD_CACHE[key]
    res = run_bass_kernel_spmd(nc, maps, core_ids=list(range(8)))
    kernel.last_results = res
    outs = [np.ascontiguousarray(r["y_t"].T) for r in res.results]
    return np.stack(outs, axis=0)


def last_res_tap(name, core=0):
    return kernel.last_results.results[core][f"dbg_{name}"]


# version 04
# speedup vs baseline: 1.2784x; 1.2784x over previous
import numpy as np
import ml_dtypes
import concourse.bass as bass
import concourse.mybir as mybir
from concourse.bass_utils import run_bass_kernel_spmd
from contextlib import ExitStack

F32 = mybir.dt.float32
BF16 = mybir.dt.bfloat16
AF = mybir.ActivationFunctionType
ALU = mybir.AluOpType
AX = mybir.AxisListType

D = 1024
T = 2048
CH = 8
NTT = 4
TT = 512
NHP = 8
NW = 8
DEPTH = 6
DKH = 64
FF = 2730
NJ = 22
JLAST = FF - 21 * 128
PM = 128
NCK = 16
LNM = float(0.5 * np.log(2.0 * DKH))

DEBUG_TAPS = []
N_LAYERS = DEPTH
WITH_PERFORMER = True


class Buf:
    __slots__ = ("name", "wr", "rd")

    def __init__(self, name):
        self.name = name
        self.wr = []
        self.rd = []


class Prog:
    ENGS = ("T", "V", "S", "G", "Q")

    def __init__(self):
        self.groups = []
        self.cnt = {e: 0 for e in self.ENGS}
        self.cnt["DG"] = 0
        self.cnt["DQ"] = 0
        self.waited = {e: {} for e in self.ENGS}

    SERIAL = False

    FENCE_ALL = False

    def emit(self, eng, fn, reads=(), writes=(), dma=False, fence=False):
        fence = fence or Prog.FENCE_ALL
        if Prog.SERIAL:
            reads = list(reads)
            writes = list(writes)
            toks_all = [(s, v) for s, v in self.cnt.items() if v > 0]
            fake = Buf("serial")
            fake.wr = toks_all
            reads.append(fake)
        toks = []
        for b in reads:
            toks += b.wr
        for b in writes:
            toks += b.wr + b.rd
        need = {}
        for s, v in toks:
            if s == eng:
                continue
            if need.get(s, -1) < v:
                need[s] = v
        waits = []
        wd = self.waited[eng]
        for s, v in sorted(need.items()):
            if wd.get(s, -1) >= v:
                continue
            wd[s] = v
            waits.append((s, v))
        if dma:
            qsem = "DG" if eng == "G" else "DQ"
            self.cnt[qsem] += 16
            tok = (qsem, self.cnt[qsem])
            inc = qsem
        else:
            self.cnt[eng] += 1
            tok = (eng, self.cnt[eng])
            inc = eng
        self.groups.append((eng, waits, fn, inc, dma, fence))
        for b in writes:
            b.wr = [tok]
            b.rd = []
        for b in reads:
            b.rd.append(tok)
        return tok

    def barrier_q(self, extra_waits):
        self.groups.append(("Q", list(extra_waits), lambda e: e.nop(), None, False, False))

    @staticmethod
    def seed(dst_bufs, src_bufs):
        toks = []
        for b in src_bufs:
            toks += b.wr + b.rd
        for d in dst_bufs:
            d.wr = list(toks)
            d.rd = []

    def finalize(self, blk, sems):
        per = {e: [] for e in self.ENGS}
        for g in self.groups:
            per[g[0]].append(g)

        def runner(eng):
            def _(e):
                for _, waits, fn, inc, dma, fence in per[eng]:
                    for s, v in waits:
                        e.wait_ge(sems[s], v)
                    inst = fn(e)
                    if dma:
                        inst.then_inc(sems[inc], 16)
                    elif inc is not None:
                        if fence and eng == "V":
                            inst = e.drain()
                        inst.then_inc(sems[inc], 1)
            return _

        blk.tensor(runner("T"))
        blk.vector(runner("V"))
        blk.scalar(runner("S"))
        blk.gpsimd(runner("G"))
        blk.sync(runner("Q"))



def host_constants():
    c = {}
    c["ident"] = np.eye(128, dtype=np.float32)
    RT = np.zeros((128, 128), np.float32)
    for d in range(128):
        base = (d // 64) * 64
        dd = d % 64
        pair = base + (dd ^ 32)
        sign = -1.0 if dd < 32 else 1.0
        RT[pair, d] = sign
    c["rotT"] = RT
    BIG = -1.0e30
    low = np.zeros((128, 128), np.float32)
    up = np.zeros((128, 128), np.float32)
    for i in range(128):
        low[i, :i] = BIG
        up[i, i + 1:] = BIG
    c["mask_lowup"] = np.ascontiguousarray(np.concatenate([low, up], axis=1))
    e2c = np.zeros((128, 2), np.float32)
    e2c[0:64, 0] = 1.0
    e2c[64:128, 1] = 1.0
    c["e2col"] = e2c
    e2r = np.zeros((2, 128), np.float32)
    e2r[0, 0:64] = 1.0
    e2r[1, 64:128] = 1.0
    c["e2row"] = e2r
    c["ones2row"] = np.ones((2, 128), np.float32)
    c["ones1row"] = np.ones((1, 128), np.float32)
    n = T
    half = np.arange(0, DKH, 2, dtype=np.float64)
    inv_freq = 1.0 / 10000.0 ** (half / DKH)
    tt_ = np.arange(n, dtype=np.float64)
    freqs = tt_[:, None] * inv_freq[None, :]
    freqs = np.concatenate([freqs, freqs], -1)
    base_scale = (half + 0.4 * DKH) / (1.4 * DKH)
    power = (tt_ - n // 2) / (256 // 2)
    scale = base_scale[None, :] ** power[:, None]
    scale = np.concatenate([scale, scale], -1)
    cos = np.cos(freqs)
    sin = np.sin(freqs)

    def fm(tab):
        tr = np.ascontiguousarray(tab.T.astype(np.float32))
        return np.ascontiguousarray(np.concatenate([tr, tr], axis=0))

    QK = 8.0
    c["cs_q"] = fm(QK * cos * scale)
    c["sn_q"] = fm(QK * sin * scale)
    c["cs_k"] = fm(cos / scale)
    c["sn_k"] = fm(sin / scale)
    c["triu"] = np.ascontiguousarray(np.triu(np.ones((128, 128), np.float32)))
    return c


def host_prepare(inp):
    cst = host_constants()
    shared = {k: np.ascontiguousarray(v) for k, v in cst.items()}
    for nm in ("Wq", "Wk", "Wv", "Wo"):
        shared["p_" + nm.lower()] = np.ascontiguousarray(
            np.asarray(inp[f"L0_attn_{nm}"], np.float32).T)
    for nm in ("bq", "bk", "bo"):
        b = np.asarray(inp[f"L0_attn_{nm}"], np.float32)
        shared["p_" + nm] = np.ascontiguousarray(b.reshape(CH, 128).T)
    shared["p_bv_row"] = np.ascontiguousarray(
        np.asarray(inp["L0_attn_bv"], np.float32).reshape(1, D))
    pj = np.asarray(inp["L0_attn_proj"], np.float32)
    shared["p_projT2"] = np.ascontiguousarray(np.concatenate([pj.T, pj.T], axis=0))
    wqkv = np.asarray(inp["L0_ffn_Wqkv"], np.float32)
    wo = np.asarray(inp["L0_ffn_Wo"], np.float32)
    w1 = np.asarray(inp["L0_ffn_W1"], np.float32)
    w2 = np.asarray(inp["L0_ffn_W2"], np.float32)
    qs = np.asarray(inp["L0_ffn_q_scale"], np.float32)
    ks = np.asarray(inp["L0_ffn_k_scale"], np.float32)
    lng = np.asarray(inp["L0_ffn_ln_g"], np.float32)
    lnb = np.asarray(inp["L0_ffn_ln_b"], np.float32)
    b1 = np.asarray(inp["L0_ffn_b1"], np.float32)
    b2 = np.asarray(inp["L0_ffn_b2"], np.float32)
    for i in range(DEPTH):
        shared[f"wq{i}"] = np.ascontiguousarray(wqkv[i][:, 0:D])
        shared[f"wk{i}"] = np.ascontiguousarray(wqkv[i][:, D:2 * D])
        shared[f"wv{i}"] = np.ascontiguousarray(wqkv[i][:, 2 * D:3 * D])
        shared[f"wo{i}"] = np.ascontiguousarray(wo[i])
        shared[f"qs2{i}"] = np.ascontiguousarray(np.tile(qs[i], 2).reshape(128, 1))
        shared[f"ks2{i}"] = np.ascontiguousarray(np.tile(ks[i], 2).reshape(128, 1))
        shared[f"lng{i}"] = np.ascontiguousarray(lng[i].reshape(CH, 128).T)
        shared[f"lnb{i}"] = np.ascontiguousarray(lnb[i].reshape(CH, 128).T)
        shared[f"b2c{i}"] = np.ascontiguousarray(b2[i].reshape(CH, 128).T)
        w1a = np.zeros((D, NJ * 128), np.float32); w1a[:, :FF] = w1[i][:, 0:FF]
        w1g = np.zeros((D, NJ * 128), np.float32); w1g[:, :FF] = w1[i][:, FF:]
        shared[f"w1a{i}"] = np.ascontiguousarray(w1a)
        shared[f"w1g{i}"] = np.ascontiguousarray(w1g)
        b1a = np.zeros(NJ * 128, np.float32); b1a[:FF] = b1[i][:FF]
        b1g = np.zeros(NJ * 128, np.float32); b1g[:FF] = b1[i][FF:]
        shared[f"b1a{i}"] = np.ascontiguousarray(b1a.reshape(NJ, 128).T)
        shared[f"b1g{i}"] = np.ascontiguousarray(b1g.reshape(NJ, 128).T)
        w2p = np.zeros((NJ * 128, D), np.float32); w2p[:FF] = w2[i]
        shared[f"w2{i}"] = np.ascontiguousarray(w2p)
    x = np.asarray(inp["L0_attn_x"], np.float32)
    maps = []
    for cidx in range(8):
        m = dict(shared)
        m["x_t"] = np.ascontiguousarray(x[cidx].T)
        maps.append(m)
    return maps



def build_program(n_layers=N_LAYERS, with_performer=WITH_PERFORMER, taps=()):
    nc = bass.Bass("TRN2", target_bir_lowering=False)
    dt = {}

    def din(name, shape):
        dt[name] = nc.dram_tensor(name, shape, F32, kind="ExternalInput").ap()

    din("x_t", [D, T])
    for nm in ("rotT", "triu", "ident"):
        din(nm, [128, 128])
    din("mask_lowup", [128, 256])
    din("e2col", [128, 2])
    din("e2row", [2, 128])
    din("ones2row", [2, 128])
    din("ones1row", [1, 128])
    for nm in ("cs_q", "sn_q", "cs_k", "sn_k"):
        din(nm, [128, T])
    for nm in ("p_wq", "p_wk", "p_wv", "p_wo"):
        din(nm, [D, D])
    for nm in ("p_bq", "p_bk", "p_bo"):
        din(nm, [128, CH])
    din("p_bv_row", [1, D])
    din("p_projT2", [128, 128])
    for i in range(DEPTH):
        din(f"wq{i}", [D, D]); din(f"wk{i}", [D, D]); din(f"wv{i}", [D, D])
        din(f"wo{i}", [D, D])
        din(f"qs2{i}", [128, 1]); din(f"ks2{i}", [128, 1])
        din(f"lng{i}", [128, CH]); din(f"lnb{i}", [128, CH]); din(f"b2c{i}", [128, CH])
        din(f"w1a{i}", [D, NJ * 128]); din(f"w1g{i}", [D, NJ * 128])
        din(f"b1a{i}", [128, NJ]); din(f"b1g{i}", [128, NJ])
        din(f"w2{i}", [NJ * 128, D])
    y_t = nc.dram_tensor("y_t", [D, T], F32, kind="ExternalOutput").ap()
    tap_outs = {}
    for tp in taps:
        tap_outs[tp] = nc.dram_tensor(f"dbg_{tp}", [128, CH * T], F32,
                                      kind="ExternalOutput").ap()

    with ExitStack() as ctx:
        def sbuf(name, shape, dtype):
            return ctx.enter_context(nc.sbuf_tensor(name, shape, dtype))

        stream = sbuf("stream", [128, CH * T], F32)
        xs = sbuf("xs", [128, CH * T], BF16)
        big = sbuf("big", [128, CH * T], BF16)
        scr2 = sbuf("scr2", [128, CH * T], BF16)
        wtab = sbuf("wtab", [128, 4 * T], BF16)
        wq_w = sbuf("wq_w", [128, 2 * D], BF16)
        wk_w = sbuf("wk_w", [128, 2 * D], BF16)
        wv_w = sbuf("wv_w", [128, CH * 256], BF16)
        wo_w = sbuf("wo_w", [128, D], BF16)
        rotT_b = sbuf("rotT_b", [128, 128], BF16)
        ident_b = sbuf("ident_b", [128, 128], BF16)
        triu_b = sbuf("triu_b", [128, 128], BF16)
        masklu = sbuf("masklu", [128, 256], BF16)
        e2col_b = sbuf("e2col_b", [128, 2], BF16)
        e2row_b = sbuf("e2row_b", [2, 128], BF16)
        ones2_b = sbuf("ones2_b", [2, 128], BF16)
        ones1_b = ones2_b[0:1, :]
        onescol_b = sbuf("onescol_b", [128, 1], BF16)
        projT2_b = sbuf("projT2_b", [128, 128], BF16)
        sqq = sbuf("scrA", [128, 512], BF16)
        qBt = sbuf("scrB", [128, 512], BF16)
        m1q = sqq
        sqk = qBt
        m1k, kBt, kfin = m1q, qBt, sqk
        rq_cols = sbuf("rq_cols", [128, 64], F32)
        cbias = sbuf("cbias", [128, 2], F32)
        cols24 = sbuf("cols24", [128, 24], F32)
        mneg = cols24[:, 0:8]
        ebias = cols24[:, 8:16]
        srow = cols24[:, 16:24]
        rowsA = sbuf("rowsA", [1, 512], F32)
        krows = sbuf("krows", [2, 512], F32)
        rowsB = krows[0:1, :]
        rows_hl = sbuf("rows_hl", [1, 4 * 512], BF16)
        rk_rows_b = sbuf("rk_rows_b", [2, 512], BF16)
        gb_col = sbuf("gb_col", [128, 2 * CH], F32)
        b1col = sbuf("b1col", [128, 2 * NJ], F32)
        b2col = sbuf("b2col", [128, CH], F32)
        qs2c = sbuf("qs2c", [128, 2], F32)
        pbias = sbuf("pbias", [128, 3 * CH], F32)
        state_S = sbuf("state_S", [128, DKH], F32)
        state_Sb = sbuf("state_Sb", [128, DKH], BF16)
        state_z = sbuf("state_z", [128, 1], F32)
        state_zb = sbuf("state_zb", [128, 1], BF16)
        denrow = krows[0:1, 0:256]
        denrow_b = rows_hl[0:1, 0:256]
        phi_f = sbuf("phi_f", [128, 512], F32)

        banks = [ctx.enter_context(nc.psum_tensor(f"bank{i}", [128, 512], F32))
                 for i in range(8)]

        sems = {}
        for s in ("T", "V", "S", "G", "Q", "DG", "DQ"):
            sems[s] = ctx.enter_context(nc.semaphore("sem" + s))
        blk = ctx.enter_context(nc.Block())

        p = Prog()

        b_stream = [Buf(f"st{c}") for c in range(CH)]
        b_xs = [Buf(f"xs{c}") for c in range(CH)]
        b_big = Buf("big")
        b_bigc = [Buf(f"bigc{c}") for c in range(CH)]
        b_scr2 = Buf("scr2")
        b_vh = Buf("vh"); b_qh = Buf("qh"); b_kh = Buf("kh")
        b_pt = [Buf(f"pt{c}") for c in range(4)]
        b_pTt = [Buf(f"pTt{c}") for c in range(4)]
        b_mnegc = [Buf(f"mneg{c}") for c in range(8)]
        b_srowc = [Buf(f"srow{c}") for c in range(8)]
        b_wvA = Buf("wvA"); b_wvB = Buf("wvB")
        b_wvC = Buf("wvC"); b_wvD = Buf("wvD")
        b_wtab = Buf("wtab")
        b_wq = [Buf("wq0"), Buf("wq1")]
        b_wk = [Buf("wk0"), Buf("wk1")]
        b_wv = Buf("wv")
        b_wo = [Buf("wo0"), Buf("wo1")]
        b_const = Buf("const")
        b_bank = [Buf(f"bk{i}") for i in range(8)]
        b_sqq = Buf("scrA"); b_qBt = Buf("scrB")
        b_m1q = b_sqq; b_m1k = b_sqq
        b_sqk = b_qBt; b_kBt = b_qBt; b_kfin = b_qBt
        b_rq = Buf("rq"); b_mneg = Buf("mneg"); b_ebias = Buf("ebias")
        b_srow = Buf("srow")
        b_rowsA = Buf("rowsA"); b_rowsB = Buf("rowsB"); b_rhl = Buf("rhl")
        b_krows = Buf("krows")
        b_rkb = Buf("rkb")
        b_gb = Buf("gb"); b_b1c = Buf("b1c"); b_b2c = Buf("b2c"); b_qs2 = Buf("qs2")
        b_pbias = Buf("pbias"); b_pbvr = Buf("pbvr")
        b_state = Buf("state")
        b_denrow = Buf("denrow"); b_denrow_b = Buf("denrowb")
        b_phif = Buf("phif")
        b_y = Buf("y")

        def dma(eng, out_ap, in_ap, reads=(), writes=()):
            return p.emit(eng, lambda e, o=out_ap, i=in_ap: e.dma_start(o, i),
                          reads=list(reads), writes=list(writes), dma=True)

        def dmaT(out_ap, in_ap, reads=(), writes=()):
            return p.emit("Q", lambda e, o=out_ap, i=in_ap: e.dma_start(o, i, transpose=True),
                          reads=list(reads), writes=list(writes), dma=True)

        dma("G", rotT_b[:, :], dt["rotT"], writes=[b_const])
        dma("G", ident_b[:, :], dt["ident"], writes=[b_const])
        dma("G", triu_b[:, :], dt["triu"], writes=[b_const])
        dma("G", masklu[:, :], dt["mask_lowup"], writes=[b_const])
        dma("G", e2col_b[:, :], dt["e2col"], writes=[b_const])
        dma("G", e2row_b[:, :], dt["e2row"], writes=[b_const])
        dma("G", ones2_b[:, :], dt["ones2row"], writes=[b_const])
        dma("G", projT2_b[:, :], dt["p_projT2"], writes=[b_const])
        p.emit("V", lambda e: e.memset(onescol_b[:, :], 1.0), writes=[b_const])
        p.emit("V", lambda e: e.memset(cbias[:, 0:1], 1.0e-5), writes=[b_const])
        p.emit("V", lambda e: e.memset(cbias[:, 1:2], 1.0e-30), writes=[b_const])

        xr = dt["x_t"].rearrange("(c p) t -> p c t", p=128)
        st3 = stream.ap().rearrange("p (c t) -> p c t", c=CH)
        xs3 = xs.ap().rearrange("p (c t) -> p c t", c=CH)
        dma("Q", st3, xr, writes=b_stream)
        dma("G", xs3, xr, writes=b_xs)

        def ln_stats_norm(src_sb, src_bufs, dst_f32, dst_bufs, dst_bf, dst_bf_buf,
                          g_col=None, b_colv=None):
            for tt in range(NTT):
                for c in range(CH):
                    src = src_sb[:, c * T + tt * TT: c * T + (tt + 1) * TT]
                    p.emit("S", lambda e, s=src: e.activation(sqq[:, :], s, AF.Square),
                           reads=[src_bufs[c]], writes=[b_sqq])

                    def mm(e, c=c, s=src):
                        e.matmul(banks[6][0:1, 0:TT], onescol_b[:, :], s,
                                 start=(c == 0), stop=(c == CH - 1),
                                 tile_position=(0, 0))
                        return e.matmul(banks[7][0:1, 0:TT], onescol_b[:, :],
                                        sqq[:, :], start=(c == 0), stop=(c == CH - 1),
                                        tile_position=(0, 0))
                    p.emit("T", mm, reads=[src_bufs[c], b_sqq, b_const],
                           writes=[b_bank[6], b_bank[7]])
                p.emit("V", lambda e: e.tensor_scalar(rowsA[0:1, 0:TT],
                                                      banks[6][0:1, 0:TT],
                                                      1.0 / D, None, ALU.mult),
                       reads=[b_bank[6]], writes=[b_rowsA])
                p.emit("V", lambda e: e.tensor_mul(rowsB, rowsA[0:1, 0:TT],
                                                   rowsA[0:1, 0:TT]),
                       reads=[b_rowsA], writes=[b_rowsB])
                p.emit("V", lambda e: e.scalar_tensor_tensor(
                    rowsB, banks[7][0:1, 0:TT], 1.0 / D, rowsB,
                    ALU.mult, ALU.subtract),
                    reads=[b_bank[7], b_rowsB], writes=[b_rowsB])
                p.emit("S", lambda e: e.activation(rowsB, rowsB, AF.Ln,
                                                   bias=cbias[0:1, 0:1]),
                       reads=[b_rowsB], writes=[b_rowsB])
                p.emit("S", lambda e: e.activation(rowsB, rowsB, AF.Exp, scale=-0.5),
                       reads=[b_rowsB], writes=[b_rowsB])
                p.emit("V", lambda e: e.tensor_mul(rowsA[0:1, 0:TT],
                                                   rowsA[0:1, 0:TT], rowsB),
                       reads=[b_rowsA, b_rowsB], writes=[b_rowsA])
                p.emit("V", lambda e: e.tensor_copy(rows_hl[0:1, 0:TT], rowsB),
                       reads=[b_rowsB], writes=[b_rhl])
                p.emit("V", lambda e: e.tensor_sub(rows_hl[0:1, TT:2 * TT], rowsB,
                                                   rows_hl[0:1, 0:TT]),
                       reads=[b_rowsB, b_rhl], writes=[b_rhl])
                p.emit("V", lambda e: e.tensor_copy(rows_hl[0:1, 2 * TT:3 * TT],
                                                    rowsA[0:1, 0:TT]),
                       reads=[b_rowsA], writes=[b_rhl])
                p.emit("V", lambda e: e.tensor_sub(rows_hl[0:1, 3 * TT:4 * TT],
                                                   rowsA[0:1, 0:TT],
                                                   rows_hl[0:1, 2 * TT:3 * TT]),
                       reads=[b_rowsA, b_rhl], writes=[b_rhl])

                def bc(e):
                    e.matmul(banks[4][:, 0:TT], ones2_b[0:1, :],
                             rows_hl[0:1, 0:TT], start=True, stop=False,
                             tile_position=(0, 0))
                    e.matmul(banks[4][:, 0:TT], ones2_b[0:1, :],
                             rows_hl[0:1, TT:2 * TT], start=False, stop=True,
                             tile_position=(0, 0))
                    e.matmul(banks[5][:, 0:TT], ones2_b[0:1, :],
                             rows_hl[0:1, 2 * TT:3 * TT], start=True, stop=False,
                             tile_position=(0, 0))
                    return e.matmul(banks[5][:, 0:TT], ones2_b[0:1, :],
                                    rows_hl[0:1, 3 * TT:4 * TT], start=False,
                                    stop=True, tile_position=(0, 0))
                p.emit("T", bc, reads=[b_rhl, b_const], writes=[b_bank[4], b_bank[5]])
                for c in range(CH):
                    if dst_f32 is not None:
                        ssl = dst_f32[:, c * T + tt * TT: c * T + (tt + 1) * TT]
                        p.emit("V", lambda e, s=ssl: e.tensor_mul(s, s, banks[4][:, 0:TT]),
                               reads=[b_bank[4], dst_bufs[c]], writes=[dst_bufs[c]])
                        p.emit("V", lambda e, s=ssl: e.tensor_sub(s, s, banks[5][:, 0:TT]),
                               reads=[b_bank[5], dst_bufs[c]], writes=[dst_bufs[c]])
                    else:
                        src = src_sb[:, c * T + tt * TT: c * T + (tt + 1) * TT]
                        p.emit("V", lambda e, s=src: e.tensor_mul(phi_f[:, :], s,
                                                                  banks[4][:, 0:TT]),
                               reads=[b_bank[4], src_bufs[c]], writes=[b_phif])
                        p.emit("V", lambda e: e.tensor_sub(phi_f[:, :], phi_f[:, :],
                                                           banks[5][:, 0:TT]),
                               reads=[b_bank[5]], writes=[b_phif])
                        dsl = dst_bf[:, c * T + tt * TT: c * T + (tt + 1) * TT]
                        if g_col is not None:
                            gc = g_col[:, c:c + 1]
                            bc2 = b_colv[:, c:c + 1]
                            p.emit("V", lambda e, d=dsl, g=gc, b2=bc2: e.tensor_scalar(
                                d, phi_f[:, :], g, b2, ALU.mult, ALU.add),
                                reads=[b_phif, b_gb], writes=[dst_bf_buf])
                        else:
                            p.emit("V", lambda e, d=dsl: e.tensor_copy(d, phi_f[:, :]),
                                   reads=[b_phif], writes=[dst_bf_buf])

        ln_stats_norm(xs, b_xs, stream, b_stream, None, None)

        if "xn" in tap_outs:
            for c in range(CH):
                dma("Q", tap_outs["xn"][:, c * T:(c + 1) * T],
                    stream[:, c * T:(c + 1) * T], reads=[b_stream[c]])

        VOFF = 0
        QOFF = 8192
        KOFF = 10240
        POFF = 12288
        POFF2 = 13824

        def cast_shift(li):
            for c in range(CH):
                for tt in range(NTT):
                    d0 = c * T + tt * TT
                    if c < 4:
                        src = stream[:, d0:d0 + TT]
                        p.emit("S", lambda e, s=src, d0=d0: e.activation(
                            xs[:, d0:d0 + TT], s, AF.Copy),
                            reads=[b_stream[c]], writes=[b_xs[c]])
                    elif tt > 0:
                        src = stream[:, d0 - 1:d0 + TT - 1]
                        p.emit("S", lambda e, s=src, d0=d0: e.activation(
                            xs[:, d0:d0 + TT], s, AF.Copy),
                            reads=[b_stream[c]], writes=[b_xs[c]])
                    else:
                        src = stream[:, d0:d0 + TT - 1]
                        p.emit("S", lambda e, s=src, d0=d0: e.activation(
                            xs[:, d0 + 1:d0 + TT], s, AF.Copy),
                            reads=[b_stream[c]], writes=[b_xs[c]])
                        p.emit("V", lambda e, d0=d0: e.memset(xs[:, d0:d0 + 1], 0.0),
                               writes=[b_xs[c]])

        def load_wslice(eng, dst, wname, cols, reads=(), writes=()):
            ap = dt[wname][:, cols].rearrange("(c p) n -> p c n", p=128)
            ncols = cols.stop - cols.start
            dst3 = dst.rearrange("p (c n) -> p c n", n=ncols)
            return dma(eng, dst3, ap, reads=reads, writes=writes)

        def attn_layer(li):
            Prog.seed([b_vh, b_qh, b_kh] + b_pt + b_pTt, [b_scr2])
            Prog.seed(b_bigc, [b_big])
            cast_shift(li)
            dma("G", qs2c[:, 0:1], dt[f"qs2{li}"], writes=[b_qs2])
            dma("G", qs2c[:, 1:2], dt[f"ks2{li}"], writes=[b_qs2])
            for idx, nm in enumerate(("cs_q", "sn_q", "cs_k", "sn_k")):
                dma("G", wtab[:, idx * T:(idx + 1) * T], dt[nm], writes=[b_wtab])
            for idx in range(4):
                qcol = qs2c[:, 0:1] if idx < 2 else qs2c[:, 1:2]
                sl = wtab[:, idx * T:(idx + 1) * T]
                p.emit("V", lambda e, s=sl, q=qcol: e.tensor_scalar(
                    s, s, q, None, ALU.mult),
                    reads=[b_wtab, b_qs2], writes=[b_wtab])
            if f"l{li}_wtab" in tap_outs:
                dma("G", tap_outs[f"l{li}_wtab"][:, 0:4 * T], wtab[:, :],
                    reads=[b_wtab])
            load_wslice("G", wq_w[:, 0:D], f"wq{li}", slice(0, 128), writes=[b_wq[0]])
            load_wslice("G", wk_w[:, 0:D], f"wk{li}", slice(0, 128), writes=[b_wk[0]])
            def project_v(group):
                Prog.seed([b_wv], [b_wvA, b_wvB, b_wvC, b_wvD])
                for sub in range(2):
                    load_wslice("G", wv_w[:, :], f"wv{li}",
                                slice(group * 512 + sub * 256,
                                      group * 512 + (sub + 1) * 256),
                                writes=[b_wv])
                    for ttok in range(NCK):
                        def mm(e, ttok=ttok):
                            last = None
                            for kc in range(CH):
                                last = e.matmul(
                                    banks[5][:, 0:256],
                                    xs[:, kc * T + ttok * 128:
                                       kc * T + (ttok + 1) * 128],
                                    wv_w[:, kc * 256:(kc + 1) * 256],
                                    start=(kc == 0), stop=(kc == CH - 1))
                            return last
                        p.emit("T", mm, reads=[b_wv] + b_xs, writes=[b_bank[5]])
                        p.emit("S", lambda e, ttok=ttok, sub=sub: e.activation(
                            scr2[:, VOFF + ttok * TT + sub * 256:
                                 VOFF + ttok * TT + (sub + 1) * 256],
                            banks[5][:, 0:256], AF.Copy),
                            reads=[b_bank[5]], writes=[b_vh])

            for hp in range(NHP):
                buf = hp % 2
                nbuf = (hp + 1) % 2
                if hp == 0:
                    project_v(0)
                    Prog.seed([b_wvA, b_wvB, b_wvC, b_wvD], [b_wv])
                if hp == 4:
                    project_v(1)
                    Prog.seed([b_wvA, b_wvB, b_wvC, b_wvD], [b_wv])
                if hp + 1 < NHP:
                    load_wslice("G", wq_w[:, nbuf * D:(nbuf + 1) * D], f"wq{li}",
                                slice((hp + 1) * 128, (hp + 2) * 128), writes=[b_wq[nbuf]])
                    load_wslice("G", wk_w[:, nbuf * D:(nbuf + 1) * D], f"wk{li}",
                                slice((hp + 1) * 128, (hp + 2) * 128), writes=[b_wk[nbuf]])
                for tt in range(NTT):
                    def mmq(e, buf=buf, tt=tt):
                        last = None
                        for kc in range(CH):
                            last = e.matmul(
                                banks[0][:, :],
                                wq_w[:, buf * D + kc * 128: buf * D + (kc + 1) * 128],
                                xs[:, kc * T + tt * TT: kc * T + (tt + 1) * TT],
                                start=(kc == 0), stop=(kc == CH - 1))
                        return last
                    p.emit("T", mmq, reads=[b_wq[buf]] + b_xs, writes=[b_bank[0]])

                    def mmk(e, buf=buf, tt=tt):
                        last = None
                        for kc in range(CH):
                            last = e.matmul(
                                banks[1][:, :],
                                wk_w[:, buf * D + kc * 128: buf * D + (kc + 1) * 128],
                                xs[:, kc * T + tt * TT: kc * T + (tt + 1) * TT],
                                start=(kc == 0), stop=(kc == CH - 1))
                        return last
                    p.emit("T", mmk, reads=[b_wk[buf]] + b_xs, writes=[b_bank[1]])
                    tq = tt * TT
                    for side in range(2):
                        bnk = banks[side]
                        bbnk = b_bank[side]
                        rbank = banks[2 + side]
                        brbank = b_bank[2 + side]
                        rotb = banks[4 + side]
                        brotb = b_bank[4 + side]
                        toff = (0 if side == 0 else 2 * T)
                        doff = QOFF if side == 0 else KOFF
                        slot = side + 2 * (tt % 2)
                        if slot in (0, 3):
                            sA, sB = sqq[:, :], qBt[:, :]
                            bA, bB = b_sqq, b_qBt
                        elif slot == 1:
                            sA, sB = wv_w[:, 0:512], wv_w[:, 512:1024]
                            bA, bB = b_wvA, b_wvB
                        else:
                            sA, sB = wv_w[:, 1024:1536], wv_w[:, 1536:2048]
                            bA, bB = b_wvC, b_wvD
                        p.emit("S", lambda e, bnk=bnk, sA=sA: e.activation(
                            sA, bnk[:, :], AF.Square),
                            reads=[bbnk], writes=[bA])
                        p.emit("T", lambda e, rbank=rbank, sA=sA: e.matmul(
                            rbank[0:2, :], e2col_b[:, :], sA,
                            start=True, stop=True, tile_position=(0, 0)),
                            reads=[bA, b_const], writes=[brbank])
                        p.emit("S", lambda e, rbank=rbank: e.activation(
                            krows[0:2, :], rbank[0:2, :], AF.Ln,
                            bias=cbias[0:2, 1:2]),
                            reads=[brbank, b_rowsB], writes=[b_krows])
                        p.emit("S", lambda e: e.activation(
                            krows[0:2, :], krows[0:2, :], AF.Exp, scale=-0.5),
                            reads=[b_krows], writes=[b_krows])
                        p.emit("V", lambda e: e.tensor_scalar_min(
                            rk_rows_b[0:2, 0:512], krows[0:2, :], 1.0e12),
                            reads=[b_krows], writes=[b_rkb])
                        p.emit("T", lambda e, rbank=rbank: e.matmul(
                            rbank[:, :], e2row_b[:, :], rk_rows_b[0:2, 0:512],
                            start=True, stop=True, tile_position=(0, 0)),
                            reads=[b_rkb, b_const], writes=[brbank])
                        bdst = b_qh if side == 0 else b_kh
                        p.emit("S", lambda e, bnk=bnk, sB=sB: e.activation(
                            sB, bnk[:, :], AF.Copy),
                            reads=[bbnk], writes=[bB])
                        p.emit("V", lambda e, toff=toff, tq=tq, sA=sA, sB=sB:
                               e.tensor_mul(
                            sA, sB, wtab[:, toff + tq:toff + tq + TT]),
                            reads=[bB, b_wtab], writes=[bA])
                        p.emit("V", lambda e, toff=toff, tq=tq, sB=sB: e.tensor_mul(
                            sB, sB, wtab[:, T + toff + tq:T + toff + tq + TT]),
                            reads=[bB, b_wtab], writes=[bB])
                        p.emit("T", lambda e, rotb=rotb, sB=sB: e.matmul(
                            rotb[:, :], rotT_b[:, :], sB,
                            start=True, stop=True),
                            reads=[bB, b_const], writes=[brotb])
                        p.emit("V", lambda e, rotb=rotb, sA=sA, sB=sB:
                               e.scalar_tensor_tensor(
                            sB, rotb[:, :], 1.0, sA,
                            ALU.mult, ALU.add),
                            reads=[brotb, bA], writes=[bB])
                        p.emit("V", lambda e, rbank=rbank, doff=doff, tq=tq,
                               bdst=bdst, sB=sB: e.tensor_mul(
                            scr2[:, doff + tq: doff + tq + TT], sB,
                            rbank[:, :]),
                            reads=[bB, brbank], writes=[bdst])
                        if hp == 0 and tt == 0 and f"l{li}_misc" in tap_outs:
                            base = side * 1536
                            dma("Q", tap_outs[f"l{li}_misc"][0:2, base:base + 512],
                                krows[0:2, :], reads=[b_krows])
                            dma("G", tap_outs[f"l{li}_misc"][0:2, base + 512:base + 1024],
                                rk_rows_b[0:2, 0:512], reads=[b_rkb])
                            dma("G", tap_outs[f"l{li}_misc"][:, base + 1024:base + 1536],
                                sqq[:, :], reads=[b_sqq])
                if hp == 0 and f"l{li}_scr2b" in tap_outs:
                    dma("G", tap_outs[f"l{li}_scr2b"][:, :], scr2[:, :],
                        reads=[b_scr2])
                import os as _os
                for w in (range(NW) if not _os.environ.get("SKIP_WIN") else []):
                    insts = []
                    for qidx in range(2):
                        qt = 2 * w + qidx
                        if w == 0:
                            W = 128 if qidx == 0 else 256
                            ks0 = 0
                        else:
                            W = 384
                            ks0 = qt * 128 - 256
                        for hh in range(2):
                            insts.append((qidx, hh, qt, W, ks0, 4 + qidx * 2 + hh))
                    for qidx, hh, qt, W, ks0, bk in insts:
                        def mmsim(e, qt=qt, ks0=ks0, W=W, hh=hh, bk=bk, w=w,
                                  qidx=qidx):
                            e.matmul(
                                banks[bk][:, 0:W],
                                scr2[64 * hh:64 * hh + 64,
                                     QOFF + qt * 128: QOFF + (qt + 1) * 128],
                                scr2[64 * hh:64 * hh + 64,
                                     KOFF + ks0: KOFF + ks0 + W],
                                start=True, stop=False,
                                tile_position=(64 * hh, 0))
                            if w == 0:
                                moff = qidx * 128
                                return e.matmul(
                                    banks[bk][:, moff:moff + 128],
                                    masklu[:, 0:128], ident_b[:, :],
                                    start=False, stop=True)
                            e.matmul(banks[bk][:, 0:128], masklu[:, 128:256],
                                     ident_b[:, :], start=False, stop=False)
                            return e.matmul(banks[bk][:, 256:384],
                                            masklu[:, 0:128], ident_b[:, :],
                                            start=False, stop=True)
                        p.emit("T", mmsim, reads=[b_qh, b_kh, b_const],
                               writes=[b_bank[bk]])
                    for qidx, hh, qt, W, ks0, bk in insts:
                        ccol = qidx * 2 + hh
                        p.emit("V", lambda e, bk=bk, W=W, ccol=ccol: e.reduce_max(
                            mneg[:, ccol:ccol + 1], banks[bk][:, 0:W], axis=AX.X,
                            negate=True),
                            reads=[b_bank[bk]], writes=[b_mnegc[ccol]])
                    for qidx, hh, qt, W, ks0, bk in insts:
                        ccol = qidx * 2 + hh
                        poff = POFF + ccol * 384
                        p.emit("S", lambda e, bk=bk, W=W, ccol=ccol,
                               poff=poff: e.activation(
                            scr2[:, poff:poff + W], banks[bk][:, 0:W], AF.Exp,
                            bias=mneg[:, ccol:ccol + 1]),
                            reads=[b_bank[bk], b_mnegc[ccol]], writes=[b_pt[ccol]])
                    for ii, (qidx, hh, qt, W, ks0, bk) in enumerate(insts):
                        ccol = qidx * 2 + hh
                        poff = POFF + ccol * 384
                        p.emit("V", lambda e, W=W, ccol=ccol, poff=poff:
                               e.reduce_sum(srow[:, ccol:ccol + 1],
                                            scr2[:, poff:poff + W], axis=AX.X),
                               reads=[b_pt[ccol]], writes=[b_srowc[ccol]],
                               fence=(ii == len(insts) - 1))
                    for qidx, hh, qt, W, ks0, bk in insts:
                        ccol = qidx * 2 + hh
                        p.emit("V", lambda e, ccol=ccol: e.reciprocal(
                            srow[:, ccol:ccol + 1], srow[:, ccol:ccol + 1]),
                            reads=[b_srowc[ccol]], writes=[b_srowc[ccol]])
                    for qidx, hh, qt, W, ks0, bk in insts:
                        ccol = qidx * 2 + hh
                        poff = POFF + ccol * 384
                        p.emit("V", lambda e, W=W, ccol=ccol, poff=poff:
                               e.tensor_scalar(scr2[:, poff:poff + W],
                                               scr2[:, poff:poff + W],
                                               srow[:, ccol:ccol + 1], None,
                                               ALU.mult),
                               reads=[b_srowc[ccol]], writes=[b_pt[ccol]])
                    for qidx, hh, qt, W, ks0, bk in insts:
                        ccol = qidx * 2 + hh
                        poff = POFF + ccol * 384
                        for cpc in range(W // 128):
                            dmaT(scr2[:, POFF2 + ccol * 384 + cpc * 128:
                                      POFF2 + ccol * 384 + (cpc + 1) * 128],
                                 scr2[:, poff + cpc * 128: poff + (cpc + 1) * 128],
                                 reads=[b_pt[ccol]], writes=[b_pTt[ccol]])
                    for qidx, hh, qt, W, ks0, bk in insts:
                        ccol = qidx * 2 + hh

                        def mmo(e, hh=hh, ccol=ccol, qidx=qidx, ks0=ks0, W=W, hp=hp):
                            last = None
                            npc = W // 128
                            for cpc in range(npc):
                                jt = ks0 // 128 + cpc
                                voff = VOFF + jt * TT + (hp % 4) * 128 + 64 * hh
                                last = e.matmul(
                                    banks[0][64 * hh:64 * hh + 64,
                                             qidx * 128:(qidx + 1) * 128],
                                    scr2[:, voff: voff + 64],
                                    scr2[:, POFF2 + ccol * 384 + cpc * 128:
                                         POFF2 + ccol * 384 + (cpc + 1) * 128],
                                    start=(cpc == 0), stop=(cpc == npc - 1),
                                    tile_position=(0, 64 * hh))
                            return last
                        p.emit("T", mmo, reads=[b_vh, b_pTt[ccol]],
                               writes=[b_bank[0]])
                    p.emit("S", lambda e, hp=hp, w=w: e.activation(
                        big[:, hp * T + w * 256: hp * T + (w + 1) * 256],
                        banks[0][:, 0:256], AF.Copy),
                        reads=[b_bank[0]], writes=[b_bigc[hp]])
                if hp == 0 and f"l{li}_scr2" in tap_outs:
                    dma("G", tap_outs[f"l{li}_scr2"][:, :], scr2[:, :],
                        reads=[b_scr2])

            if f"l{li}_ao" in tap_outs:
                dma("G", tap_outs[f"l{li}_ao"][:, :], big[:, :], reads=[b_big])
            for f2 in range(CH):
                load_wslice("G", wo_w[:, 0:D], f"wo{li}",
                            slice(f2 * 128, (f2 + 1) * 128), writes=[b_wo[0]])
                for tt in range(NTT):
                    bk = 2 + (tt % 2)

                    def mmwo(e, tt=tt, bk=bk):
                        last = None
                        for kc in range(CH):
                            last = e.matmul(
                                banks[bk][:, :],
                                wo_w[:, kc * 128: (kc + 1) * 128],
                                big[:, kc * T + tt * TT: kc * T + (tt + 1) * TT],
                                start=(kc == 0), stop=(kc == CH - 1))
                        return last
                    p.emit("T", mmwo, reads=[b_wo[0]] + b_bigc, writes=[b_bank[bk]])
                    p.emit("V", lambda e, f2=f2, tt=tt, bk=bk: e.tensor_add(
                        stream[:, f2 * T + tt * TT: f2 * T + (tt + 1) * TT],
                        stream[:, f2 * T + tt * TT: f2 * T + (tt + 1) * TT],
                        banks[bk][:, :]),
                        reads=[b_bank[bk], b_stream[f2]], writes=[b_stream[f2]])

        def ffn_layer(li):
            cast_shift(li)
            dma("G", gb_col[:, 0:CH], dt[f"lng{li}"], writes=[b_gb])
            dma("G", gb_col[:, CH:2 * CH], dt[f"lnb{li}"], writes=[b_gb])
            dma("G", b1col[:, 0:NJ], dt[f"b1a{li}"], writes=[b_b1c])
            dma("G", b1col[:, NJ:2 * NJ], dt[f"b1g{li}"], writes=[b_b1c])
            dma("G", b2col[:, :], dt[f"b2c{li}"], writes=[b_b2c])
            Prog.seed([b_scr2], [b_vh, b_qh, b_kh] + b_pt + b_pTt)
            ln_stats_norm(xs, b_xs, None, None, scr2, b_scr2,
                          gb_col[:, 0:CH], gb_col[:, CH:2 * CH])
            jblocks = [list(range(0, 8)), list(range(8, 16)), list(range(16, NJ))]
            w1bufs = [wq_w, wk_w]
            b_w1 = [Buf("w10"), Buf("w11")]
            Prog.seed(b_w1, [b_wq[0], b_wq[1], b_wk[0], b_wk[1]])
            load_wslice("G", w1bufs[0][:, 0:D], f"w1a{li}", slice(0, 128),
                        writes=[b_w1[0]])
            load_wslice("G", w1bufs[0][:, D:2 * D], f"w1g{li}", slice(0, 128),
                        writes=[b_w1[0]])
            b_w2s = [Buf(f"w2s{j}") for j in range(8)]
            b_hh = [Buf(f"hh{j}") for j in range(8)]
            Prog.seed(b_hh, [b_big] + b_bigc)
            jcount = 0
            for bi, jb in enumerate(jblocks):
                for idx, j in enumerate(jb):
                    buf = jcount % 2
                    jcount += 1
                    nj_global = j + 1
                    if nj_global < NJ:
                        nbuf = jcount % 2
                        ncols = slice(nj_global * 128, (nj_global + 1) * 128)
                        load_wslice("G", w1bufs[nbuf][:, 0:D],
                                    f"w1a{li}", ncols, writes=[b_w1[nbuf]])
                        load_wslice("G", w1bufs[nbuf][:, D:2 * D],
                                    f"w1g{li}", ncols, writes=[b_w1[nbuf]])
                    for tt in range(NTT):
                        ba = (tt % 2) * 2
                        def mma(e, buf=buf, tt=tt, ba=ba):
                            last = None
                            for kc in range(CH):
                                last = e.matmul(
                                    banks[ba][:, :],
                                    w1bufs[buf][:, kc * 128:(kc + 1) * 128],
                                    scr2[:, kc * T + tt * TT: kc * T + (tt + 1) * TT],
                                    start=(kc == 0), stop=(kc == CH - 1))
                            return last
                        p.emit("T", mma, reads=[b_w1[buf], b_scr2], writes=[b_bank[ba]])

                        def mmg(e, buf=buf, tt=tt, ba=ba):
                            last = None
                            for kc in range(CH):
                                last = e.matmul(
                                    banks[ba + 1][:, :],
                                    w1bufs[buf][:, D + kc * 128: D + (kc + 1) * 128],
                                    scr2[:, kc * T + tt * TT: kc * T + (tt + 1) * TT],
                                    start=(kc == 0), stop=(kc == CH - 1))
                            return last
                        p.emit("T", mmg, reads=[b_w1[buf], b_scr2], writes=[b_bank[ba + 1]])
                        gdst = sqk if tt % 2 == 0 else sqq
                        bgdst = b_sqk if tt % 2 == 0 else b_sqq
                        p.emit("S", lambda e, j=j, ba=ba, gdst=gdst: e.activation(
                            gdst[:, :], banks[ba + 1][:, :], AF.Gelu,
                            bias=b1col[:, NJ + j:NJ + j + 1]),
                            reads=[b_bank[ba + 1], b_b1c], writes=[bgdst])
                        p.emit("V", lambda e, j=j, idx=idx, tt=tt, ba=ba, gdst=gdst:
                               e.scalar_tensor_tensor(
                            big[:, idx * T + tt * TT: idx * T + (tt + 1) * TT],
                            banks[ba][:, :], b1col[:, j:j + 1], gdst[:, :],
                            ALU.add, ALU.mult),
                            reads=[b_bank[ba], b_b1c, bgdst], writes=[b_hh[idx]])
                    dma("G", wtab[:, idx * D:(idx + 1) * D],
                        dt[f"w2{li}"][j * 128:(j + 1) * 128, :],
                        writes=[b_wtab, b_w2s[idx]])
                for tt in range(NTT):
                    for f2 in range(CH):
                        bk = 4 + (f2 % 4)

                        def mmw2(e, jb=jb, tt=tt, f2=f2, bk=bk):
                            last = None
                            for idx in range(len(jb)):
                                last = e.matmul(
                                    banks[bk][:, :],
                                    wtab[:, idx * D + f2 * 128:
                                         idx * D + (f2 + 1) * 128],
                                    big[:, idx * T + tt * TT:
                                        idx * T + (tt + 1) * TT],
                                    start=(idx == 0), stop=(idx == len(jb) - 1))
                            return last
                        p.emit("T", mmw2,
                               reads=[b_wtab] + [b_w2s[i2] for i2 in range(len(jb))] +
                                     [b_hh[i2] for i2 in range(len(jb))],
                               writes=[b_bank[bk]])
                        if bi == 0:
                            p.emit("V", lambda e, f2=f2, tt=tt, bk=bk:
                                   e.scalar_tensor_tensor(
                                stream[:, f2 * T + tt * TT: f2 * T + (tt + 1) * TT],
                                banks[bk][:, :], b2col[:, f2:f2 + 1],
                                stream[:, f2 * T + tt * TT: f2 * T + (tt + 1) * TT],
                                ALU.add, ALU.add),
                                reads=[b_bank[bk], b_b2c, b_stream[f2]],
                                writes=[b_stream[f2]])
                        else:
                            p.emit("V", lambda e, f2=f2, tt=tt, bk=bk: e.tensor_add(
                                stream[:, f2 * T + tt * TT: f2 * T + (tt + 1) * TT],
                                stream[:, f2 * T + tt * TT: f2 * T + (tt + 1) * TT],
                                banks[bk][:, :]),
                                reads=[b_bank[bk], b_stream[f2]],
                                writes=[b_stream[f2]])
            Prog.seed([b_big], b_hh + b_w2s)
            Prog.seed([b_wq[0], b_wq[1]], [b_w1[0]])
            Prog.seed([b_wk[0], b_wk[1]], [b_w1[1]])

        for li in range(n_layers):
            attn_layer(li)
            if f"l{li}_res1" in tap_outs:
                for c in range(CH):
                    dma("Q", tap_outs[f"l{li}_res1"][:, c * T:(c + 1) * T],
                        stream[:, c * T:(c + 1) * T], reads=[b_stream[c]])
            ffn_layer(li)
            if f"l{li}_res2" in tap_outs:
                for c in range(CH):
                    dma("Q", tap_outs[f"l{li}_res2"][:, c * T:(c + 1) * T],
                        stream[:, c * T:(c + 1) * T], reads=[b_stream[c]])

        if with_performer:
            Prog.FENCE_ALL = True
            for c in range(CH):
                for tt in range(NTT):
                    sl = stream[:, c * T + tt * TT: c * T + (tt + 1) * TT]
                    p.emit("S", lambda e, s=sl: e.activation(s, s, AF.Silu),
                           reads=[b_stream[c]], writes=[b_stream[c]])
            if "perf_g" in tap_outs:
                for c in range(CH):
                    dma("Q", tap_outs["perf_g"][:, c * T:(c + 1) * T],
                        stream[:, c * T:(c + 1) * T], reads=[b_stream[c]])
            dma("G", xs3, xr, writes=b_xs)
            ln_stats_norm(xs, b_xs, None, None, scr2, b_scr2, None, None)
            dma("G", pbias[:, 0:CH], dt["p_bq"], writes=[b_pbias])
            dma("G", pbias[:, CH:2 * CH], dt["p_bk"], writes=[b_pbias])
            dma("G", pbias[:, 2 * CH:3 * CH], dt["p_bo"], writes=[b_pbias])
            Prog.seed([b_pbvr], [b_wtab])
            dma("G", wtab[0:1, 6 * T // 2: 6 * T // 2 + D], dt["p_bv_row"],
                writes=[b_pbvr])
            ic = float(1.0 / (DKH ** 0.25))
            p.emit("V", lambda e: e.tensor_scalar(pbias[:, 0:2 * CH],
                                                  pbias[:, 0:2 * CH], ic, None,
                                                  ALU.mult),
                   reads=[b_pbias], writes=[b_pbias])

            PQ = 0
            PK = 2048
            PV = 4096
            b_qp = Buf("qp"); b_kp = Buf("kp"); b_pv = Buf("pv")
            Prog.seed([b_qp, b_kp, b_pv], b_xs)

            load_wslice("G", wq_w[:, 0:D], "p_wq", slice(0, 128),
                        reads=[b_wq[0]], writes=[b_wq[0]])
            load_wslice("G", wk_w[:, 0:D], "p_wk", slice(0, 128),
                        reads=[b_wk[0]], writes=[b_wk[0]])

            import os as _os2
            for h in range(int(_os2.environ.get('PERF_HEADS', '16'))):
                hp = h // 2
                half = h % 2
                if half == 0:
                    if hp + 1 < 8:
                        pass
                    for tt in range(NTT):
                        def mmq(e, tt=tt):
                            last = None
                            for kc in range(CH):
                                last = e.matmul(
                                    banks[0][:, :],
                                    wq_w[:, kc * 128:(kc + 1) * 128],
                                    scr2[:, kc * T + tt * TT: kc * T + (tt + 1) * TT],
                                    start=(kc == 0), stop=(kc == CH - 1))
                            return last
                        p.emit("T", mmq, reads=[b_wq[0], b_scr2], writes=[b_bank[0]])
                        p.emit("S", lambda e, tt=tt, hp=hp: e.activation(
                            xs[:, PQ + tt * TT: PQ + (tt + 1) * TT], banks[0][:, :],
                            AF.Identity, bias=pbias[:, hp:hp + 1], scale=ic),
                            reads=[b_bank[0], b_pbias], writes=[b_qp])

                        def mmk(e, tt=tt):
                            last = None
                            for kc in range(CH):
                                last = e.matmul(
                                    banks[1][:, :],
                                    wk_w[:, kc * 128:(kc + 1) * 128],
                                    scr2[:, kc * T + tt * TT: kc * T + (tt + 1) * TT],
                                    start=(kc == 0), stop=(kc == CH - 1))
                            return last
                        p.emit("T", mmk, reads=[b_wk[0], b_scr2], writes=[b_bank[1]])
                        p.emit("S", lambda e, tt=tt, hp=hp: e.activation(
                            xs[:, PK + tt * TT: PK + (tt + 1) * TT], banks[1][:, :],
                            AF.Identity, bias=pbias[:, CH + hp:CH + hp + 1], scale=ic),
                            reads=[b_bank[1], b_pbias], writes=[b_kp])
                    if hp + 1 < 8:
                        load_wslice("G", wq_w[:, 0:D], "p_wq",
                                    slice((hp + 1) * 128, (hp + 2) * 128),
                                    reads=[b_qp], writes=[b_wq[0]])
                        load_wslice("G", wk_w[:, 0:D], "p_wk",
                                    slice((hp + 1) * 128, (hp + 2) * 128),
                                    reads=[b_kp], writes=[b_wk[0]])
                if h == 0 or h == 8:
                    grp = h // 8
                    for sub in range(2):
                        load_wslice("G", wv_w[:, :], "p_wv",
                                    slice(grp * 512 + sub * 256,
                                          grp * 512 + (sub + 1) * 256),
                                    writes=[b_wv])
                        for ttok in range(NCK):
                            def mmv(e, ttok=ttok, grp=grp, sub=sub):
                                for kc in range(CH):
                                    e.matmul(
                                        banks[5][:, 0:256],
                                        scr2[:, kc * T + ttok * 128:
                                             kc * T + (ttok + 1) * 128],
                                        wv_w[:, kc * 256:(kc + 1) * 256],
                                        start=(kc == 0), stop=False)
                                boff = 6 * T // 2 + grp * 512 + sub * 256
                                return e.matmul(
                                    banks[5][:, 0:256], ones2_b[0:1, :],
                                    wtab[0:1, boff:boff + 256],
                                    start=False, stop=True, tile_position=(0, 0))
                            p.emit("T", mmv, reads=[b_scr2, b_wv, b_pbvr, b_const],
                                   writes=[b_bank[5]])
                            p.emit("S", lambda e, ttok=ttok, sub=sub: e.activation(
                                xs[:, PV + ttok * TT + sub * 256:
                                   PV + ttok * TT + (sub + 1) * 256],
                                banks[5][:, 0:256], AF.Copy),
                                reads=[b_bank[5]], writes=[b_pv])

                for tt in range(NTT):
                    p.emit("S", lambda e, tt=tt: e.activation(
                        sqq[:, :], xs[:, PQ + tt * TT: PQ + (tt + 1) * TT], AF.Square),
                        reads=[b_qp], writes=[b_sqq])

                    def mmssq(e, tt=tt, half=half):
                        last = None
                        for g in range(4):
                            col = tt * 4 + g
                            last = e.matmul(
                                banks[7][:, col:col + 1],
                                sqq[64 * half:64 * half + 64,
                                    g * 128:(g + 1) * 128],
                                onescol_b[64 * half:64 * half + 64, :],
                                start=True, stop=True,
                                tile_position=(64 * half, 0))
                        return last
                    p.emit("T", mmssq, reads=[b_sqq, b_const], writes=[b_bank[7]])
                    p.emit("S", lambda e, tt=tt: e.activation(
                        sqk[:, :], xs[:, PK + tt * TT: PK + (tt + 1) * TT], AF.Square),
                        reads=[b_kp], writes=[b_sqk])

                    def mmssk(e, tt=tt, half=half):
                        last = None
                        for g in range(4):
                            col = 16 + tt * 4 + g
                            last = e.matmul(
                                banks[7][:, col:col + 1],
                                sqk[64 * half:64 * half + 64,
                                    g * 128:(g + 1) * 128],
                                onescol_b[64 * half:64 * half + 64, :],
                                start=True, stop=True,
                                tile_position=(64 * half, 0))
                        return last
                    p.emit("T", mmssk, reads=[b_sqk, b_const], writes=[b_bank[7]])
                p.emit("V", lambda e: e.tensor_scalar(
                    rq_cols[:, 0:32], banks[7][:, 0:32], -0.5, -LNM,
                    ALU.mult, ALU.add),
                    reads=[b_bank[7]], writes=[b_rq], fence=True)
                for ttok in range(NCK):
                    tcol = ttok // 4 * 4 + ttok % 4
                    def mmuq(e, ttok=ttok, half=half):
                        return e.matmul(
                            banks[2][:, 0:128],
                            xs[64 * half:64 * half + 64,
                               PQ + ttok * 128: PQ + (ttok + 1) * 128],
                            projT2_b[64 * half:64 * half + 64, :],
                            start=True, stop=True, tile_position=(64 * half, 0))
                    p.emit("T", mmuq, reads=[b_qp, b_const], writes=[b_bank[2]])
                    p.emit("S", lambda e, ttok=ttok: e.activation(
                        phi_f[:, 0:128], banks[2][:, 0:128], AF.Exp,
                        bias=rq_cols[:, ttok:ttok + 1]),
                        reads=[b_bank[2], b_rq], writes=[b_phif])
                    p.emit("V", lambda e: e.tensor_scalar(
                        sqq[:, 0:128], phi_f[:, 0:128], 1.0e-6, None, ALU.add),
                        reads=[b_phif], writes=[b_sqq], fence=True)
                    dmaT(wtab[:, 2 * T + ttok * 128: 2 * T + (ttok + 1) * 128],
                         sqq[:, 0:128], reads=[b_sqq], writes=[b_wtab])

                    def mmuk(e, ttok=ttok, half=half):
                        return e.matmul(
                            banks[3][:, 0:128],
                            xs[64 * half:64 * half + 64,
                               PK + ttok * 128: PK + (ttok + 1) * 128],
                            projT2_b[64 * half:64 * half + 64, :],
                            start=True, stop=True, tile_position=(64 * half, 0))
                    p.emit("T", mmuk, reads=[b_kp, b_const], writes=[b_bank[3]])
                    p.emit("S", lambda e, ttok=ttok: e.activation(
                        phi_f[:, 128:256], banks[3][:, 0:128], AF.Exp,
                        bias=rq_cols[:, 16 + ttok:16 + ttok + 1]),
                        reads=[b_bank[3], b_rq], writes=[b_phif])
                    p.emit("V", lambda e, ttok=ttok: e.tensor_scalar(
                        wtab[:, ttok * 128:(ttok + 1) * 128], phi_f[:, 128:256],
                        1.0e-6, None, ALU.add),
                        reads=[b_phif], writes=[b_wtab], fence=True)
                    dmaT(wtab[:, T + ttok * 128: T + (ttok + 1) * 128],
                         wtab[:, ttok * 128:(ttok + 1) * 128],
                         reads=[b_wtab], writes=[b_wtab])
                p.emit("V", lambda e: e.memset(state_S[:, :], 0.0), writes=[b_state])
                p.emit("V", lambda e: e.memset(state_Sb[:, :], 0.0), writes=[b_state])
                p.emit("V", lambda e: e.memset(state_z[:, :], 0.0), writes=[b_state])
                p.emit("V", lambda e: e.memset(state_zb[:, :], 0.0), writes=[b_state])
                for n in range(NCK):
                    qsl = wtab[:, 2 * T + n * 128: 2 * T + (n + 1) * 128]
                    p.emit("T", lambda e, qsl=qsl, half=half: e.matmul(
                        banks[4][64 * half:64 * half + 64, 0:128],
                        state_Sb[:, :], qsl, start=True, stop=False,
                        tile_position=(0, 64 * half)),
                        reads=[b_state, b_wtab], writes=[b_bank[4]])
                    p.emit("T", lambda e, n=n, qsl=qsl: e.matmul(
                        banks[6][:, 0:128], wtab[:, T + n * 128: T + (n + 1) * 128],
                        qsl, start=True, stop=True),
                        reads=[b_wtab], writes=[b_bank[6]])
                    p.emit("V", lambda e: e.tensor_mul(m1q[:, 0:128],
                                                       banks[6][:, 0:128],
                                                       triu_b[:, :]),
                           reads=[b_bank[6], b_const], writes=[b_m1q], fence=True)
                    def mmden(e, qsl=qsl):
                        e.matmul(banks[6][0:1, 128:256], state_zb[:, :], qsl,
                                 start=True, stop=False, tile_position=(0, 0))
                        return e.matmul(banks[6][0:1, 128:256], onescol_b[:, :],
                                        m1q[:, 0:128], start=False, stop=True,
                                        tile_position=(0, 0))
                    p.emit("T", mmden, reads=[b_state, b_wtab, b_m1q, b_const],
                           writes=[b_bank[6]])
                    p.emit("T", lambda e, n=n, half=half, hp=hp: e.matmul(
                        banks[4][64 * half:64 * half + 64, 0:128],
                        xs[:, PV + n * TT + (hp % 4) * 128 + 64 * half:
                           PV + n * TT + (hp % 4) * 128 + 64 * half + 64],
                        m1q[:, 0:128], start=False, stop=True,
                        tile_position=(0, 64 * half)),
                        reads=[b_pv, b_m1q], writes=[b_bank[4]])
                    p.emit("V", lambda e: e.tensor_scalar(
                        denrow[0:1, 0:128], banks[6][0:1, 128:256], 1.0e-6, None,
                        ALU.add),
                        reads=[b_bank[6], b_rowsB], writes=[b_denrow])
                    p.emit("V", lambda e: e.reciprocal(denrow[0:1, 0:128],
                                                       denrow[0:1, 0:128]),
                           reads=[b_denrow], writes=[b_denrow])
                    p.emit("V", lambda e: e.tensor_copy(denrow_b[0:1, 0:128],
                                                        denrow[0:1, 0:128]),
                           reads=[b_denrow, b_rhl], writes=[b_denrow_b])
                    p.emit("V", lambda e: e.tensor_sub(denrow_b[0:1, 128:256],
                                                       denrow[0:1, 0:128],
                                                       denrow_b[0:1, 0:128]),
                           reads=[b_denrow, b_denrow_b], writes=[b_denrow_b],
                           fence=True)

                    def mmrden(e, half=half):
                        e.matmul(banks[5][64 * half:64 * half + 64, 0:128],
                                 ones2_b[0:1, 0:64], denrow_b[0:1, 0:128],
                                 start=True, stop=False, tile_position=(0, 64 * half))
                        return e.matmul(banks[5][64 * half:64 * half + 64, 0:128],
                                        ones2_b[0:1, 0:64], denrow_b[0:1, 128:256],
                                        start=False, stop=True,
                                        tile_position=(0, 64 * half))
                    p.emit("T", mmrden, reads=[b_denrow_b, b_const],
                           writes=[b_bank[5]])
                    p.emit("S", lambda e, half=half: e.activation(
                        kBt[64 * half:64 * half + 64, 0:128],
                        banks[5][64 * half:64 * half + 64, 0:128], AF.Copy),
                        reads=[b_bank[5]], writes=[b_kBt])
                    p.emit("V", lambda e, half=half, hp=hp, n=n: e.tensor_mul(
                        big[64 * half:64 * half + 64,
                            hp * T + n * 128: hp * T + (n + 1) * 128],
                        banks[4][64 * half:64 * half + 64, 0:128],
                        kBt[64 * half:64 * half + 64, 0:128]),
                        reads=[b_bank[4], b_kBt], writes=[b_big])
                    def mmkv(e, n=n, half=half, hp=hp):
                        e.matmul(banks[2][:, 0:DKH],
                                 wtab[:, n * 128:(n + 1) * 128],
                                 xs[:, PV + n * TT + (hp % 4) * 128 + 64 * half:
                                    PV + n * TT + (hp % 4) * 128 + 64 * half + 64],
                                 start=True, stop=True)
                        return e.matmul(banks[2][:, 64:65],
                                        wtab[:, n * 128:(n + 1) * 128],
                                        onescol_b[:, :], start=True, stop=True)
                    p.emit("T", mmkv, reads=[b_wtab, b_pv, b_const],
                           writes=[b_bank[2]])
                    p.emit("V", lambda e: e.tensor_add(state_S[:, :], state_S[:, :],
                                                       banks[2][:, 0:DKH]),
                           reads=[b_bank[2], b_state], writes=[b_state])
                    p.emit("V", lambda e: e.tensor_copy(state_Sb[:, :],
                                                        state_S[:, :]),
                           reads=[b_state], writes=[b_state], fence=True)
                    p.emit("V", lambda e: e.tensor_add(state_z[:, :], state_z[:, :],
                                                       banks[2][:, 64:65]),
                           reads=[b_bank[2], b_state], writes=[b_state])
                    p.emit("V", lambda e: e.tensor_copy(state_zb[:, :],
                                                        state_z[:, :]),
                           reads=[b_state], writes=[b_state], fence=True)

            if "perf_ao" in tap_outs:
                dma("G", tap_outs["perf_ao"][:, :], big[:, :], reads=[b_big])
            if "perf_xn" in tap_outs:
                dma("G", tap_outs["perf_xn"][:, :], scr2[:, :], reads=[b_scr2])
            if "perf_qf" in tap_outs:
                dma("G", tap_outs["perf_qf"][:, 0:4 * T], wtab[:, :], reads=[b_wtab])
            for f2 in range(CH):
                load_wslice("G", wo_w[:, 0:D], "p_wo",
                            slice(f2 * 128, (f2 + 1) * 128), writes=[b_wo[0]])
                for tt in range(NTT):
                    bk = 2 + (tt % 2)

                    def mmwo(e, tt=tt, bk=bk):
                        last = None
                        for kc in range(CH):
                            last = e.matmul(
                                banks[bk][:, :],
                                wo_w[:, kc * 128: (kc + 1) * 128],
                                big[:, kc * T + tt * TT: kc * T + (tt + 1) * TT],
                                start=(kc == 0), stop=(kc == CH - 1))
                        return last
                    p.emit("T", mmwo, reads=[b_wo[0], b_big], writes=[b_bank[bk]])
                    p.emit("V", lambda e, f2=f2, tt=tt, bk=bk:
                           e.scalar_tensor_tensor(
                        phi_f[:, :], banks[bk][:, :], pbias[:, 2 * CH + f2:2 * CH + f2 + 1],
                        stream[:, f2 * T + tt * TT: f2 * T + (tt + 1) * TT],
                        ALU.add, ALU.mult),
                        reads=[b_bank[bk], b_pbias, b_stream[f2]], writes=[b_phif])
                    dma("Q", stream[:, f2 * T + tt * TT: f2 * T + (tt + 1) * TT],
                        dt["x_t"][f2 * 128:(f2 + 1) * 128, tt * TT:(tt + 1) * TT],
                        writes=[b_stream[f2]])
                    p.emit("V", lambda e, f2=f2, tt=tt: e.tensor_add(
                        phi_f[:, :], phi_f[:, :],
                        stream[:, f2 * T + tt * TT: f2 * T + (tt + 1) * TT]),
                        reads=[b_stream[f2]], writes=[b_phif])
                    dma("Q", y_t[f2 * 128:(f2 + 1) * 128, tt * TT:(tt + 1) * TT],
                        phi_f[:, :], reads=[b_phif], writes=[b_y])
        else:
            for c in range(CH):
                dma("Q", y_t[c * 128:(c + 1) * 128, :],
                    stream[:, c * T:(c + 1) * T], reads=[b_stream[c]], writes=[b_y])

        Prog.FENCE_ALL = False
        p.barrier_q([("DQ", p.cnt["DQ"]), ("DG", p.cnt["DG"])])
        p.finalize(blk, sems)
    return nc


_BUILD_CACHE = {}


def kernel(**inputs):
    maps = host_prepare(inputs)
    key = (N_LAYERS, WITH_PERFORMER, tuple(DEBUG_TAPS))
    if key not in _BUILD_CACHE:
        _BUILD_CACHE[key] = build_program(N_LAYERS, WITH_PERFORMER, DEBUG_TAPS)
    nc = _BUILD_CACHE[key]
    res = run_bass_kernel_spmd(nc, maps, core_ids=list(range(8)))
    kernel.last_results = res
    outs = [np.ascontiguousarray(r["y_t"].T) for r in res.results]
    return np.stack(outs, axis=0)


def last_res_tap(name, core=0):
    return kernel.last_results.results[core][f"dbg_{name}"]


# version 06
# speedup vs baseline: 1.9365x; 1.5147x over previous
import numpy as np
import ml_dtypes
import concourse.bass as bass
import concourse.mybir as mybir
from concourse.bass_utils import run_bass_kernel_spmd
from contextlib import ExitStack

F32 = mybir.dt.float32
BF16 = mybir.dt.bfloat16
AF = mybir.ActivationFunctionType
ALU = mybir.AluOpType
AX = mybir.AxisListType

D = 1024
T = 2048
CH = 8
NTT = 4
TT = 512
NHP = 8
NW = 8
DEPTH = 6
DKH = 64
FF = 2730
NJ = 22
JLAST = FF - 21 * 128
PM = 128
NCK = 16
LNM = float(0.5 * np.log(2.0 * DKH))

DEBUG_TAPS = []
N_LAYERS = DEPTH
WITH_PERFORMER = True


class Buf:
    __slots__ = ("name", "wr", "rd")

    def __init__(self, name):
        self.name = name
        self.wr = []
        self.rd = []


class Prog:
    ENGS = ("T", "V", "S", "G", "Q")

    def __init__(self):
        self.groups = []
        self.cnt = {e: 0 for e in self.ENGS}
        self.cnt["DG"] = 0
        self.cnt["DQ"] = 0
        self.waited = {e: {} for e in self.ENGS}

    SERIAL = False

    FENCE_ALL = False

    def emit(self, eng, fn, reads=(), writes=(), dma=False, fence=False):
        fence = fence or Prog.FENCE_ALL
        if Prog.SERIAL:
            reads = list(reads)
            writes = list(writes)
            toks_all = [(s, v) for s, v in self.cnt.items() if v > 0]
            fake = Buf("serial")
            fake.wr = toks_all
            reads.append(fake)
        toks = []
        for b in reads:
            toks += b.wr
        for b in writes:
            toks += b.wr + b.rd
        need = {}
        for s, v in toks:
            if s == eng:
                continue
            if need.get(s, -1) < v:
                need[s] = v
        waits = []
        wd = self.waited[eng]
        for s, v in sorted(need.items()):
            if wd.get(s, -1) >= v:
                continue
            wd[s] = v
            waits.append((s, v))
        if dma:
            qsem = "DG" if eng == "G" else "DQ"
            self.cnt[qsem] += 16
            tok = (qsem, self.cnt[qsem])
            inc = qsem
        else:
            self.cnt[eng] += 1
            tok = (eng, self.cnt[eng])
            inc = eng
        self.groups.append((eng, waits, fn, inc, dma, fence))
        for b in writes:
            b.wr = [tok]
            b.rd = []
        for b in reads:
            b.rd.append(tok)
        return tok

    def barrier_q(self, extra_waits):
        self.groups.append(("Q", list(extra_waits), lambda e: e.nop(), None, False, False))

    @staticmethod
    def seed(dst_bufs, src_bufs):
        toks = []
        for b in src_bufs:
            toks += b.wr + b.rd
        for d in dst_bufs:
            d.wr = list(toks)
            d.rd = []

    def finalize(self, blk, sems):
        per = {e: [] for e in self.ENGS}
        for g in self.groups:
            per[g[0]].append(g)

        def runner(eng):
            def _(e):
                for _, waits, fn, inc, dma, fence in per[eng]:
                    for s, v in waits:
                        e.wait_ge(sems[s], v)
                    inst = fn(e)
                    if dma:
                        inst.then_inc(sems[inc], 16)
                    elif inc is not None:
                        if fence and eng == "V":
                            inst = e.drain()
                        inst.then_inc(sems[inc], 1)
            return _

        blk.tensor(runner("T"))
        blk.vector(runner("V"))
        blk.scalar(runner("S"))
        blk.gpsimd(runner("G"))
        blk.sync(runner("Q"))



def host_constants():
    c = {}
    c["ident"] = np.eye(128, dtype=np.float32)
    RT = np.zeros((128, 128), np.float32)
    for d in range(128):
        base = (d // 64) * 64
        dd = d % 64
        pair = base + (dd ^ 32)
        sign = -1.0 if dd < 32 else 1.0
        RT[pair, d] = sign
    c["rotT"] = RT
    BIG = -1.0e30
    low = np.zeros((128, 128), np.float32)
    up = np.zeros((128, 128), np.float32)
    for i in range(128):
        low[i, :i] = BIG
        up[i, i + 1:] = BIG
    c["mask_lowup"] = np.ascontiguousarray(np.concatenate([low, up], axis=1))
    e2c = np.zeros((128, 2), np.float32)
    e2c[0:64, 0] = 1.0
    e2c[64:128, 1] = 1.0
    c["e2col"] = e2c
    e2r = np.zeros((2, 128), np.float32)
    e2r[0, 0:64] = 1.0
    e2r[1, 64:128] = 1.0
    c["e2row"] = e2r
    c["ones2row"] = np.ones((2, 128), np.float32)
    c["ones1row"] = np.ones((1, 128), np.float32)
    n = T
    half = np.arange(0, DKH, 2, dtype=np.float64)
    inv_freq = 1.0 / 10000.0 ** (half / DKH)
    tt_ = np.arange(n, dtype=np.float64)
    freqs = tt_[:, None] * inv_freq[None, :]
    freqs = np.concatenate([freqs, freqs], -1)
    base_scale = (half + 0.4 * DKH) / (1.4 * DKH)
    power = (tt_ - n // 2) / (256 // 2)
    scale = base_scale[None, :] ** power[:, None]
    scale = np.concatenate([scale, scale], -1)
    cos = np.cos(freqs)
    sin = np.sin(freqs)

    def fm(tab):
        tr = np.ascontiguousarray(tab.T.astype(np.float32))
        return np.ascontiguousarray(np.concatenate([tr, tr], axis=0))

    QK = 8.0
    c["cs_q"] = fm(QK * cos * scale)
    c["sn_q"] = fm(QK * sin * scale)
    c["cs_k"] = fm(cos / scale)
    c["sn_k"] = fm(sin / scale)
    c["triu"] = np.ascontiguousarray(np.triu(np.ones((128, 128), np.float32)))
    return c


def host_prepare(inp):
    cst = host_constants()
    shared = {k: np.ascontiguousarray(v) for k, v in cst.items()}
    for nm in ("Wq", "Wk", "Wv", "Wo"):
        shared["p_" + nm.lower()] = np.ascontiguousarray(
            np.asarray(inp[f"L0_attn_{nm}"], np.float32).T)
    for nm in ("bq", "bk", "bo"):
        b = np.asarray(inp[f"L0_attn_{nm}"], np.float32)
        shared["p_" + nm] = np.ascontiguousarray(b.reshape(CH, 128).T)
    shared["p_bv_row"] = np.ascontiguousarray(
        np.asarray(inp["L0_attn_bv"], np.float32).reshape(1, D))
    pj = np.asarray(inp["L0_attn_proj"], np.float32)
    shared["p_projT2"] = np.ascontiguousarray(np.concatenate([pj.T, pj.T], axis=0))
    wqkv = np.asarray(inp["L0_ffn_Wqkv"], np.float32)
    wo = np.asarray(inp["L0_ffn_Wo"], np.float32)
    w1 = np.asarray(inp["L0_ffn_W1"], np.float32)
    w2 = np.asarray(inp["L0_ffn_W2"], np.float32)
    qs = np.asarray(inp["L0_ffn_q_scale"], np.float32)
    ks = np.asarray(inp["L0_ffn_k_scale"], np.float32)
    lng = np.asarray(inp["L0_ffn_ln_g"], np.float32)
    lnb = np.asarray(inp["L0_ffn_ln_b"], np.float32)
    b1 = np.asarray(inp["L0_ffn_b1"], np.float32)
    b2 = np.asarray(inp["L0_ffn_b2"], np.float32)
    for i in range(DEPTH):
        shared[f"wq{i}"] = np.ascontiguousarray(wqkv[i][:, 0:D])
        shared[f"wk{i}"] = np.ascontiguousarray(wqkv[i][:, D:2 * D])
        shared[f"wv{i}"] = np.ascontiguousarray(wqkv[i][:, 2 * D:3 * D])
        shared[f"wo{i}"] = np.ascontiguousarray(wo[i])
        shared[f"qs2{i}"] = np.ascontiguousarray(np.tile(qs[i], 2).reshape(128, 1))
        shared[f"ks2{i}"] = np.ascontiguousarray(np.tile(ks[i], 2).reshape(128, 1))
        shared[f"lng{i}"] = np.ascontiguousarray(lng[i].reshape(CH, 128).T)
        shared[f"lnb{i}"] = np.ascontiguousarray(lnb[i].reshape(CH, 128).T)
        shared[f"b2c{i}"] = np.ascontiguousarray(b2[i].reshape(CH, 128).T)
        w1a = np.zeros((D, NJ * 128), np.float32); w1a[:, :FF] = w1[i][:, 0:FF]
        w1g = np.zeros((D, NJ * 128), np.float32); w1g[:, :FF] = w1[i][:, FF:]
        shared[f"w1a{i}"] = np.ascontiguousarray(w1a)
        shared[f"w1g{i}"] = np.ascontiguousarray(w1g)
        b1a = np.zeros(NJ * 128, np.float32); b1a[:FF] = b1[i][:FF]
        b1g = np.zeros(NJ * 128, np.float32); b1g[:FF] = b1[i][FF:]
        shared[f"b1a{i}"] = np.ascontiguousarray(b1a.reshape(NJ, 128).T)
        shared[f"b1g{i}"] = np.ascontiguousarray(b1g.reshape(NJ, 128).T)
        w2p = np.zeros((NJ * 128, D), np.float32); w2p[:FF] = w2[i]
        shared[f"w2{i}"] = np.ascontiguousarray(w2p)
    x = np.asarray(inp["L0_attn_x"], np.float32)
    maps = []
    for cidx in range(8):
        m = dict(shared)
        m["x_t"] = np.ascontiguousarray(x[cidx].T)
        maps.append(m)
    return maps



def build_program(n_layers=N_LAYERS, with_performer=WITH_PERFORMER, taps=()):
    nc = bass.Bass("TRN2", target_bir_lowering=False)
    dt = {}

    def din(name, shape):
        dt[name] = nc.dram_tensor(name, shape, F32, kind="ExternalInput").ap()

    din("x_t", [D, T])
    for nm in ("rotT", "triu", "ident"):
        din(nm, [128, 128])
    din("mask_lowup", [128, 256])
    din("e2col", [128, 2])
    din("e2row", [2, 128])
    din("ones2row", [2, 128])
    din("ones1row", [1, 128])
    for nm in ("cs_q", "sn_q", "cs_k", "sn_k"):
        din(nm, [128, T])
    for nm in ("p_wq", "p_wk", "p_wv", "p_wo"):
        din(nm, [D, D])
    for nm in ("p_bq", "p_bk", "p_bo"):
        din(nm, [128, CH])
    din("p_bv_row", [1, D])
    din("p_projT2", [128, 128])
    for i in range(DEPTH):
        din(f"wq{i}", [D, D]); din(f"wk{i}", [D, D]); din(f"wv{i}", [D, D])
        din(f"wo{i}", [D, D])
        din(f"qs2{i}", [128, 1]); din(f"ks2{i}", [128, 1])
        din(f"lng{i}", [128, CH]); din(f"lnb{i}", [128, CH]); din(f"b2c{i}", [128, CH])
        din(f"w1a{i}", [D, NJ * 128]); din(f"w1g{i}", [D, NJ * 128])
        din(f"b1a{i}", [128, NJ]); din(f"b1g{i}", [128, NJ])
        din(f"w2{i}", [NJ * 128, D])
    y_t = nc.dram_tensor("y_t", [D, T], F32, kind="ExternalOutput").ap()
    tap_outs = {}
    for tp in taps:
        tap_outs[tp] = nc.dram_tensor(f"dbg_{tp}", [128, CH * T], F32,
                                      kind="ExternalOutput").ap()

    with ExitStack() as ctx:
        def sbuf(name, shape, dtype):
            return ctx.enter_context(nc.sbuf_tensor(name, shape, dtype))

        stream = sbuf("stream", [128, CH * T], F32)
        xs = sbuf("xs", [128, CH * T], BF16)
        big = sbuf("big", [128, CH * T], BF16)
        scr2 = sbuf("scr2", [128, CH * T], BF16)
        wtab = sbuf("wtab", [128, 4 * T], BF16)
        wq_w = sbuf("wq_w", [128, 2 * D], BF16)
        wk_w = sbuf("wk_w", [128, 2 * D], BF16)
        wv_w = sbuf("wv_w", [128, CH * 256], BF16)
        wo_w = sbuf("wo_w", [128, D], BF16)
        rotT_b = sbuf("rotT_b", [128, 128], BF16)
        ident_b = sbuf("ident_b", [128, 128], BF16)
        triu_b = sbuf("triu_b", [128, 128], BF16)
        masklu = sbuf("masklu", [128, 256], BF16)
        e2col_b = sbuf("e2col_b", [128, 2], BF16)
        e2row_b = sbuf("e2row_b", [2, 128], BF16)
        ones2_b = sbuf("ones2_b", [2, 128], BF16)
        ones1_b = ones2_b[0:1, :]
        onescol_b = sbuf("onescol_b", [128, 1], BF16)
        projT2_b = sbuf("projT2_b", [128, 128], BF16)
        sqq = sbuf("scrA", [128, 512], BF16)
        qBt = sbuf("scrB", [128, 512], BF16)
        m1q = sqq
        sqk = qBt
        m1k, kBt, kfin = m1q, qBt, sqk
        rq_cols = sbuf("rq_cols", [128, 64], F32)
        cbias = sbuf("cbias", [128, 2], F32)
        cols24 = sbuf("cols24", [128, 24], F32)
        mneg = cols24[:, 0:8]
        ebias = cols24[:, 8:16]
        srow = cols24[:, 16:24]
        rowsA = sbuf("rowsA", [1, 512], F32)
        krows = sbuf("krows", [2, 512], F32)
        rowsB = krows[0:1, :]
        rows_hl = sbuf("rows_hl", [128, 4 * 512], BF16)
        rk_rows_b = sbuf("rk_rows_b", [2, 512], BF16)
        gb_col = sbuf("gb_col", [128, 2 * CH], F32)
        b1col = sbuf("b1col", [128, 2 * NJ], F32)
        b2col = sbuf("b2col", [128, CH], F32)
        qs2c = sbuf("qs2c", [128, 2], F32)
        pbias = sbuf("pbias", [128, 3 * CH], F32)
        state_S = sbuf("state_S", [128, DKH], F32)
        state_Sb = sbuf("state_Sb", [128, DKH], BF16)
        state_z = sbuf("state_z", [128, 1], F32)
        state_zb = sbuf("state_zb", [128, 1], BF16)
        denrow = krows[0:1, 0:256]
        denrow_b = rows_hl[0:1, 0:256]
        phi_f = sbuf("phi_f", [128, 512], F32)

        banks = [ctx.enter_context(nc.psum_tensor(f"bank{i}", [128, 512], F32))
                 for i in range(8)]

        sems = {}
        for s in ("T", "V", "S", "G", "Q", "DG", "DQ"):
            sems[s] = ctx.enter_context(nc.semaphore("sem" + s))
        blk = ctx.enter_context(nc.Block())

        p = Prog()

        b_stream = [Buf(f"st{c}") for c in range(CH)]
        b_xs = [Buf(f"xs{c}") for c in range(CH)]
        b_big = Buf("big")
        b_bigc = [Buf(f"bigc{c}") for c in range(CH)]
        b_scr2 = Buf("scr2")
        b_vh = Buf("vh"); b_qh = Buf("qh"); b_kh = Buf("kh")
        b_pt = [Buf(f"pt{c}") for c in range(4)]
        b_pTt = [Buf(f"pTt{c}") for c in range(4)]
        b_mnegc = [Buf(f"mneg{c}") for c in range(8)]
        b_srowc = [Buf(f"srow{c}") for c in range(8)]
        b_ptB = [Buf(f"ptB{c}") for c in range(4)]
        b_wvA = Buf("wvA"); b_wvB = Buf("wvB")
        b_wvC = Buf("wvC"); b_wvD = Buf("wvD")
        b_wtab = Buf("wtab")
        b_wq = [Buf("wq0"), Buf("wq1")]
        b_wk = [Buf("wk0"), Buf("wk1")]
        b_wv = Buf("wv")
        b_wo = [Buf("wo0"), Buf("wo1")]
        b_const = Buf("const")
        b_bank = [Buf(f"bk{i}") for i in range(8)]
        b_sqq = Buf("scrA"); b_qBt = Buf("scrB")
        b_m1q = b_sqq; b_m1k = b_sqq
        b_sqk = b_qBt; b_kBt = b_qBt; b_kfin = b_qBt
        b_rq = Buf("rq"); b_mneg = Buf("mneg"); b_ebias = Buf("ebias")
        b_srow = Buf("srow")
        b_rowsA = Buf("rowsA"); b_rowsB = Buf("rowsB"); b_rhl = Buf("rhl")
        b_krows = Buf("krows")
        b_rkb = Buf("rkb")
        b_gb = Buf("gb"); b_b1c = Buf("b1c"); b_b2c = Buf("b2c"); b_qs2 = Buf("qs2")
        b_pbias = Buf("pbias"); b_pbvr = Buf("pbvr")
        b_state = Buf("state")
        b_denrow = Buf("denrow"); b_denrow_b = Buf("denrowb")
        b_phif = Buf("phif")
        b_y = Buf("y")

        def dma(eng, out_ap, in_ap, reads=(), writes=()):
            return p.emit(eng, lambda e, o=out_ap, i=in_ap: e.dma_start(o, i),
                          reads=list(reads), writes=list(writes), dma=True)

        def dmaT(out_ap, in_ap, reads=(), writes=()):
            return p.emit("Q", lambda e, o=out_ap, i=in_ap: e.dma_start(o, i, transpose=True),
                          reads=list(reads), writes=list(writes), dma=True)

        dma("G", rotT_b[:, :], dt["rotT"], writes=[b_const])
        dma("G", ident_b[:, :], dt["ident"], writes=[b_const])
        dma("G", triu_b[:, :], dt["triu"], writes=[b_const])
        dma("G", masklu[:, :], dt["mask_lowup"], writes=[b_const])
        dma("G", e2col_b[:, :], dt["e2col"], writes=[b_const])
        dma("G", e2row_b[:, :], dt["e2row"], writes=[b_const])
        dma("G", ones2_b[:, :], dt["ones2row"], writes=[b_const])
        dma("G", projT2_b[:, :], dt["p_projT2"], writes=[b_const])
        p.emit("V", lambda e: e.memset(onescol_b[:, :], 1.0), writes=[b_const])
        p.emit("V", lambda e: e.memset(cbias[:, 0:1], 1.0e-5), writes=[b_const])
        p.emit("V", lambda e: e.memset(cbias[:, 1:2], 1.0e-30), writes=[b_const])

        xr = dt["x_t"].rearrange("(c p) t -> p c t", p=128)
        st3 = stream.ap().rearrange("p (c t) -> p c t", c=CH)
        xs3 = xs.ap().rearrange("p (c t) -> p c t", c=CH)
        dma("Q", st3, xr, writes=b_stream)
        dma("G", xs3, xr, writes=b_xs)

        def ln_stats_norm(src_sb, src_bufs, dst_f32, dst_bufs, dst_bf, dst_bf_buf,
                          g_col=None, b_colv=None):
            for tt in range(NTT):
                for c in range(CH):
                    src = src_sb[:, c * T + tt * TT: c * T + (tt + 1) * TT]
                    p.emit("S", lambda e, s=src: e.activation(sqq[:, :], s, AF.Square),
                           reads=[src_bufs[c]], writes=[b_sqq])

                    def mm(e, c=c, s=src):
                        e.matmul(banks[6][0:1, 0:TT], onescol_b[:, :], s,
                                 start=(c == 0), stop=(c == CH - 1),
                                 tile_position=(0, 0))
                        return e.matmul(banks[7][0:1, 0:TT], onescol_b[:, :],
                                        sqq[:, :], start=(c == 0), stop=(c == CH - 1),
                                        tile_position=(0, 0))
                    p.emit("T", mm, reads=[src_bufs[c], b_sqq, b_const],
                           writes=[b_bank[6], b_bank[7]])
                p.emit("V", lambda e: e.tensor_scalar(rowsA[0:1, 0:TT],
                                                      banks[6][0:1, 0:TT],
                                                      1.0 / D, None, ALU.mult),
                       reads=[b_bank[6]], writes=[b_rowsA])
                p.emit("V", lambda e: e.tensor_mul(rowsB, rowsA[0:1, 0:TT],
                                                   rowsA[0:1, 0:TT]),
                       reads=[b_rowsA], writes=[b_rowsB])
                p.emit("V", lambda e: e.scalar_tensor_tensor(
                    rowsB, banks[7][0:1, 0:TT], 1.0 / D, rowsB,
                    ALU.mult, ALU.subtract),
                    reads=[b_bank[7], b_rowsB], writes=[b_rowsB])
                p.emit("S", lambda e: e.activation(rowsB, rowsB, AF.Ln,
                                                   bias=cbias[0:1, 0:1]),
                       reads=[b_rowsB], writes=[b_rowsB])
                p.emit("S", lambda e: e.activation(rowsB, rowsB, AF.Exp, scale=-0.5),
                       reads=[b_rowsB], writes=[b_rowsB])
                p.emit("V", lambda e: e.tensor_mul(rowsA[0:1, 0:TT],
                                                   rowsA[0:1, 0:TT], rowsB),
                       reads=[b_rowsA, b_rowsB], writes=[b_rowsA])
                p.emit("V", lambda e: e.tensor_copy(rows_hl[0:1, 0:TT], rowsB),
                       reads=[b_rowsB], writes=[b_rhl])
                p.emit("V", lambda e: e.tensor_sub(rows_hl[0:1, TT:2 * TT], rowsB,
                                                   rows_hl[0:1, 0:TT]),
                       reads=[b_rowsB, b_rhl], writes=[b_rhl])
                p.emit("V", lambda e: e.tensor_copy(rows_hl[0:1, 2 * TT:3 * TT],
                                                    rowsA[0:1, 0:TT]),
                       reads=[b_rowsA], writes=[b_rhl])
                p.emit("V", lambda e: e.tensor_sub(rows_hl[0:1, 3 * TT:4 * TT],
                                                   rowsA[0:1, 0:TT],
                                                   rows_hl[0:1, 2 * TT:3 * TT]),
                       reads=[b_rowsA, b_rhl], writes=[b_rhl])

                def bc(e):
                    e.matmul(banks[4][:, 0:TT], ones2_b[0:1, :],
                             rows_hl[0:1, 0:TT], start=True, stop=False,
                             tile_position=(0, 0))
                    e.matmul(banks[4][:, 0:TT], ones2_b[0:1, :],
                             rows_hl[0:1, TT:2 * TT], start=False, stop=True,
                             tile_position=(0, 0))
                    e.matmul(banks[5][:, 0:TT], ones2_b[0:1, :],
                             rows_hl[0:1, 2 * TT:3 * TT], start=True, stop=False,
                             tile_position=(0, 0))
                    return e.matmul(banks[5][:, 0:TT], ones2_b[0:1, :],
                                    rows_hl[0:1, 3 * TT:4 * TT], start=False,
                                    stop=True, tile_position=(0, 0))
                p.emit("T", bc, reads=[b_rhl, b_const], writes=[b_bank[4], b_bank[5]])
                for c in range(CH):
                    if dst_f32 is not None:
                        ssl = dst_f32[:, c * T + tt * TT: c * T + (tt + 1) * TT]
                        p.emit("V", lambda e, s=ssl: e.tensor_mul(s, s, banks[4][:, 0:TT]),
                               reads=[b_bank[4], dst_bufs[c]], writes=[dst_bufs[c]])
                        p.emit("V", lambda e, s=ssl: e.tensor_sub(s, s, banks[5][:, 0:TT]),
                               reads=[b_bank[5], dst_bufs[c]], writes=[dst_bufs[c]])
                    else:
                        src = src_sb[:, c * T + tt * TT: c * T + (tt + 1) * TT]
                        p.emit("V", lambda e, s=src: e.tensor_mul(phi_f[:, :], s,
                                                                  banks[4][:, 0:TT]),
                               reads=[b_bank[4], src_bufs[c]], writes=[b_phif])
                        p.emit("V", lambda e: e.tensor_sub(phi_f[:, :], phi_f[:, :],
                                                           banks[5][:, 0:TT]),
                               reads=[b_bank[5]], writes=[b_phif])
                        dsl = dst_bf[:, c * T + tt * TT: c * T + (tt + 1) * TT]
                        if g_col is not None:
                            gc = g_col[:, c:c + 1]
                            bc2 = b_colv[:, c:c + 1]
                            p.emit("V", lambda e, d=dsl, g=gc, b2=bc2: e.tensor_scalar(
                                d, phi_f[:, :], g, b2, ALU.mult, ALU.add),
                                reads=[b_phif, b_gb], writes=[dst_bf_buf])
                        else:
                            p.emit("V", lambda e, d=dsl: e.tensor_copy(d, phi_f[:, :]),
                                   reads=[b_phif], writes=[dst_bf_buf])

        ln_stats_norm(xs, b_xs, stream, b_stream, None, None)

        if "xn" in tap_outs:
            for c in range(CH):
                dma("Q", tap_outs["xn"][:, c * T:(c + 1) * T],
                    stream[:, c * T:(c + 1) * T], reads=[b_stream[c]])

        VOFF = 0
        QOFF = 8192
        KOFF = 10240
        POFF = 12288
        POFF2 = 13824

        def cast_shift(li):
            for c in range(CH):
                for tt in range(NTT):
                    d0 = c * T + tt * TT
                    if c < 4:
                        src = stream[:, d0:d0 + TT]
                        p.emit("S", lambda e, s=src, d0=d0: e.activation(
                            xs[:, d0:d0 + TT], s, AF.Copy),
                            reads=[b_stream[c]], writes=[b_xs[c]])
                    elif tt > 0:
                        src = stream[:, d0 - 1:d0 + TT - 1]
                        p.emit("S", lambda e, s=src, d0=d0: e.activation(
                            xs[:, d0:d0 + TT], s, AF.Copy),
                            reads=[b_stream[c]], writes=[b_xs[c]])
                    else:
                        src = stream[:, d0:d0 + TT - 1]
                        p.emit("S", lambda e, s=src, d0=d0: e.activation(
                            xs[:, d0 + 1:d0 + TT], s, AF.Copy),
                            reads=[b_stream[c]], writes=[b_xs[c]])
                        p.emit("V", lambda e, d0=d0: e.memset(xs[:, d0:d0 + 1], 0.0),
                               writes=[b_xs[c]])

        def load_wslice(eng, dst, wname, cols, reads=(), writes=()):
            ap = dt[wname][:, cols].rearrange("(c p) n -> p c n", p=128)
            ncols = cols.stop - cols.start
            dst3 = dst.rearrange("p (c n) -> p c n", n=ncols)
            return dma(eng, dst3, ap, reads=reads, writes=writes)

        def attn_layer(li):
            Prog.seed([b_vh, b_qh, b_kh] + b_pt + b_pTt, [b_scr2])
            Prog.seed(b_ptB, [b_rhl])
            Prog.seed(b_bigc, [b_big])
            cast_shift(li)
            dma("G", qs2c[:, 0:1], dt[f"qs2{li}"], writes=[b_qs2])
            dma("G", qs2c[:, 1:2], dt[f"ks2{li}"], writes=[b_qs2])
            for idx, nm in enumerate(("cs_q", "sn_q", "cs_k", "sn_k")):
                dma("G", wtab[:, idx * T:(idx + 1) * T], dt[nm], writes=[b_wtab])
            for idx in range(4):
                qcol = qs2c[:, 0:1] if idx < 2 else qs2c[:, 1:2]
                sl = wtab[:, idx * T:(idx + 1) * T]
                p.emit("V", lambda e, s=sl, q=qcol: e.tensor_scalar(
                    s, s, q, None, ALU.mult),
                    reads=[b_wtab, b_qs2], writes=[b_wtab])
            if f"l{li}_wtab" in tap_outs:
                dma("G", tap_outs[f"l{li}_wtab"][:, 0:4 * T], wtab[:, :],
                    reads=[b_wtab])
            load_wslice("G", wq_w[:, 0:D], f"wq{li}", slice(0, 128), writes=[b_wq[0]])
            load_wslice("G", wk_w[:, 0:D], f"wk{li}", slice(0, 128), writes=[b_wk[0]])
            def project_v(group):
                Prog.seed([b_wv], [b_wvA, b_wvB, b_wvC, b_wvD])
                for sub in range(2):
                    load_wslice("G", wv_w[:, :], f"wv{li}",
                                slice(group * 512 + sub * 256,
                                      group * 512 + (sub + 1) * 256),
                                writes=[b_wv])
                    for ttok in range(NCK):
                        def mm(e, ttok=ttok):
                            last = None
                            for kc in range(CH):
                                last = e.matmul(
                                    banks[5][:, 0:256],
                                    xs[:, kc * T + ttok * 128:
                                       kc * T + (ttok + 1) * 128],
                                    wv_w[:, kc * 256:(kc + 1) * 256],
                                    start=(kc == 0), stop=(kc == CH - 1))
                            return last
                        p.emit("T", mm, reads=[b_wv] + b_xs, writes=[b_bank[5]])
                        p.emit("S", lambda e, ttok=ttok, sub=sub: e.activation(
                            scr2[:, VOFF + ttok * TT + sub * 256:
                                 VOFF + ttok * TT + (sub + 1) * 256],
                            banks[5][:, 0:256], AF.Copy),
                            reads=[b_bank[5]], writes=[b_vh])

            for hp in range(NHP):
                buf = hp % 2
                nbuf = (hp + 1) % 2
                if hp == 0:
                    project_v(0)
                    Prog.seed([b_wvA, b_wvB, b_wvC, b_wvD], [b_wv])
                if hp == 4:
                    project_v(1)
                    Prog.seed([b_wvA, b_wvB, b_wvC, b_wvD], [b_wv])
                if hp + 1 < NHP:
                    load_wslice("G", wq_w[:, nbuf * D:(nbuf + 1) * D], f"wq{li}",
                                slice((hp + 1) * 128, (hp + 2) * 128), writes=[b_wq[nbuf]])
                    load_wslice("G", wk_w[:, nbuf * D:(nbuf + 1) * D], f"wk{li}",
                                slice((hp + 1) * 128, (hp + 2) * 128), writes=[b_wk[nbuf]])
                for tt in range(NTT):
                    def mmq(e, buf=buf, tt=tt):
                        last = None
                        for kc in range(CH):
                            last = e.matmul(
                                banks[0][:, :],
                                wq_w[:, buf * D + kc * 128: buf * D + (kc + 1) * 128],
                                xs[:, kc * T + tt * TT: kc * T + (tt + 1) * TT],
                                start=(kc == 0), stop=(kc == CH - 1))
                        return last
                    p.emit("T", mmq, reads=[b_wq[buf]] + b_xs, writes=[b_bank[0]])

                    def mmk(e, buf=buf, tt=tt):
                        last = None
                        for kc in range(CH):
                            last = e.matmul(
                                banks[1][:, :],
                                wk_w[:, buf * D + kc * 128: buf * D + (kc + 1) * 128],
                                xs[:, kc * T + tt * TT: kc * T + (tt + 1) * TT],
                                start=(kc == 0), stop=(kc == CH - 1))
                        return last
                    p.emit("T", mmk, reads=[b_wk[buf]] + b_xs, writes=[b_bank[1]])
                    tq = tt * TT
                    for side in range(2):
                        bnk = banks[side]
                        bbnk = b_bank[side]
                        rbank = banks[2 + side]
                        brbank = b_bank[2 + side]
                        rotb = banks[4 + side]
                        brotb = b_bank[4 + side]
                        toff = (0 if side == 0 else 2 * T)
                        doff = QOFF if side == 0 else KOFF
                        slot = side + 2 * (tt % 2)
                        if slot in (0, 3):
                            sA, sB = sqq[:, :], qBt[:, :]
                            bA, bB = b_sqq, b_qBt
                        elif slot == 1:
                            sA, sB = wv_w[:, 0:512], wv_w[:, 512:1024]
                            bA, bB = b_wvA, b_wvB
                        else:
                            sA, sB = wv_w[:, 1024:1536], wv_w[:, 1536:2048]
                            bA, bB = b_wvC, b_wvD
                        p.emit("S", lambda e, bnk=bnk, sA=sA: e.activation(
                            sA, bnk[:, :], AF.Square),
                            reads=[bbnk], writes=[bA])
                        p.emit("T", lambda e, rbank=rbank, sA=sA: e.matmul(
                            rbank[0:2, :], e2col_b[:, :], sA,
                            start=True, stop=True, tile_position=(0, 0)),
                            reads=[bA, b_const], writes=[brbank])
                        p.emit("S", lambda e, rbank=rbank: e.activation(
                            krows[0:2, :], rbank[0:2, :], AF.Ln,
                            bias=cbias[0:2, 1:2]),
                            reads=[brbank, b_rowsB], writes=[b_krows])
                        p.emit("S", lambda e: e.activation(
                            krows[0:2, :], krows[0:2, :], AF.Exp, scale=-0.5),
                            reads=[b_krows], writes=[b_krows])
                        p.emit("V", lambda e: e.tensor_scalar_min(
                            rk_rows_b[0:2, 0:512], krows[0:2, :], 1.0e12),
                            reads=[b_krows], writes=[b_rkb])
                        p.emit("T", lambda e, rbank=rbank: e.matmul(
                            rbank[:, :], e2row_b[:, :], rk_rows_b[0:2, 0:512],
                            start=True, stop=True, tile_position=(0, 0)),
                            reads=[b_rkb, b_const], writes=[brbank])
                        bdst = b_qh if side == 0 else b_kh
                        p.emit("S", lambda e, bnk=bnk, sB=sB: e.activation(
                            sB, bnk[:, :], AF.Copy),
                            reads=[bbnk], writes=[bB])
                        p.emit("V", lambda e, toff=toff, tq=tq, sA=sA, sB=sB:
                               e.tensor_mul(
                            sA, sB, wtab[:, toff + tq:toff + tq + TT]),
                            reads=[bB, b_wtab], writes=[bA])
                        p.emit("V", lambda e, toff=toff, tq=tq, sB=sB: e.tensor_mul(
                            sB, sB, wtab[:, T + toff + tq:T + toff + tq + TT]),
                            reads=[bB, b_wtab], writes=[bB])
                        p.emit("T", lambda e, rotb=rotb, sB=sB: e.matmul(
                            rotb[:, :], rotT_b[:, :], sB,
                            start=True, stop=True),
                            reads=[bB, b_const], writes=[brotb])
                        p.emit("V", lambda e, rotb=rotb, sA=sA, sB=sB:
                               e.scalar_tensor_tensor(
                            sB, rotb[:, :], 1.0, sA,
                            ALU.mult, ALU.add),
                            reads=[brotb, bA], writes=[bB])
                        p.emit("V", lambda e, rbank=rbank, doff=doff, tq=tq,
                               bdst=bdst, sB=sB: e.tensor_mul(
                            scr2[:, doff + tq: doff + tq + TT], sB,
                            rbank[:, :]),
                            reads=[bB, brbank], writes=[bdst])
                        if hp == 0 and tt == 0 and f"l{li}_misc" in tap_outs:
                            base = side * 1536
                            dma("Q", tap_outs[f"l{li}_misc"][0:2, base:base + 512],
                                krows[0:2, :], reads=[b_krows])
                            dma("G", tap_outs[f"l{li}_misc"][0:2, base + 512:base + 1024],
                                rk_rows_b[0:2, 0:512], reads=[b_rkb])
                            dma("G", tap_outs[f"l{li}_misc"][:, base + 1024:base + 1536],
                                sqq[:, :], reads=[b_sqq])
                if hp == 0 and f"l{li}_scr2b" in tap_outs:
                    dma("G", tap_outs[f"l{li}_scr2b"][:, :], scr2[:, :],
                        reads=[b_scr2])
                import os as _os
                for w in (range(NW) if not _os.environ.get("SKIP_WIN") else []):
                    par = w % 2
                    ptbuf = (lambda off, ln: scr2[:, POFF + off: POFF + off + ln])                         if par == 0 else                         (lambda off, ln: rows_hl[:, off: off + ln])
                    bpt = b_pt if par == 0 else b_ptB
                    cbase = par * 4
                    insts = []
                    for qidx in range(2):
                        qt = 2 * w + qidx
                        if w == 0:
                            W = 128 if qidx == 0 else 256
                            ks0 = 0
                        else:
                            W = 384
                            ks0 = qt * 128 - 256
                        for hh in range(2):
                            insts.append((qidx, hh, qt, W, ks0, 4 + qidx * 2 + hh))
                    for qidx, hh, qt, W, ks0, bk in insts:
                        def mmsim(e, qt=qt, ks0=ks0, W=W, hh=hh, bk=bk, w=w,
                                  qidx=qidx):
                            e.matmul(
                                banks[bk][:, 0:W],
                                scr2[64 * hh:64 * hh + 64,
                                     QOFF + qt * 128: QOFF + (qt + 1) * 128],
                                scr2[64 * hh:64 * hh + 64,
                                     KOFF + ks0: KOFF + ks0 + W],
                                start=True, stop=False,
                                tile_position=(64 * hh, 0))
                            if w == 0:
                                moff = qidx * 128
                                return e.matmul(
                                    banks[bk][:, moff:moff + 128],
                                    masklu[:, 0:128], ident_b[:, :],
                                    start=False, stop=True)
                            e.matmul(banks[bk][:, 0:128], masklu[:, 128:256],
                                     ident_b[:, :], start=False, stop=False)
                            return e.matmul(banks[bk][:, 256:384],
                                            masklu[:, 0:128], ident_b[:, :],
                                            start=False, stop=True)
                        p.emit("T", mmsim, reads=[b_qh, b_kh, b_const],
                               writes=[b_bank[bk]])
                    for qidx, hh, qt, W, ks0, bk in insts:
                        ccol = qidx * 2 + hh
                        p.emit("V", lambda e, bk=bk, W=W, ccol=ccol, cbase=cbase:
                               e.reduce_max(
                            mneg[:, cbase + ccol:cbase + ccol + 1],
                            banks[bk][:, 0:W], axis=AX.X, negate=True),
                            reads=[b_bank[bk]], writes=[b_mnegc[cbase + ccol]])
                    for qidx, hh, qt, W, ks0, bk in insts:
                        ccol = qidx * 2 + hh
                        p.emit("S", lambda e, bk=bk, W=W, ccol=ccol,
                               ptbuf=ptbuf, cbase=cbase: e.activation(
                            ptbuf(ccol * 384, W), banks[bk][:, 0:W], AF.Exp,
                            bias=mneg[:, cbase + ccol:cbase + ccol + 1]),
                            reads=[b_bank[bk], b_mnegc[cbase + ccol]],
                            writes=[bpt[ccol]])
                    for ii, (qidx, hh, qt, W, ks0, bk) in enumerate(insts):
                        ccol = qidx * 2 + hh
                        p.emit("V", lambda e, W=W, ccol=ccol, ptbuf=ptbuf,
                               cbase=cbase:
                               e.reduce_sum(srow[:, cbase + ccol:cbase + ccol + 1],
                                            ptbuf(ccol * 384, W), axis=AX.X),
                               reads=[bpt[ccol]], writes=[b_srowc[cbase + ccol]],
                               fence=(ii == len(insts) - 1))
                    for qidx, hh, qt, W, ks0, bk in insts:
                        ccol = qidx * 2 + hh
                        p.emit("V", lambda e, ccol=ccol, cbase=cbase: e.reciprocal(
                            srow[:, cbase + ccol:cbase + ccol + 1],
                            srow[:, cbase + ccol:cbase + ccol + 1]),
                            reads=[b_srowc[cbase + ccol]],
                            writes=[b_srowc[cbase + ccol]])
                    for ii, (qidx, hh, qt, W, ks0, bk) in enumerate(insts):
                        ccol = qidx * 2 + hh
                        p.emit("V", lambda e, W=W, ccol=ccol, ptbuf=ptbuf,
                               cbase=cbase:
                               e.tensor_scalar(ptbuf(ccol * 384, W),
                                               ptbuf(ccol * 384, W),
                                               srow[:, cbase + ccol:cbase + ccol + 1],
                                               None, ALU.mult),
                               reads=[b_srowc[cbase + ccol]], writes=[bpt[ccol]],
                               fence=(ii == len(insts) - 1))
                    for qidx, hh, qt, W, ks0, bk in insts:
                        ccol = qidx * 2 + hh

                        def mmtr(e, W=W, ccol=ccol, bk=bk, ptbuf=ptbuf):
                            last = None
                            for cpc in range(W // 128):
                                last = e.matmul(
                                    banks[bk][:, cpc * 128:(cpc + 1) * 128],
                                    ptbuf(ccol * 384 + cpc * 128, 128),
                                    ident_b[:, :], start=True, stop=True)
                            return last
                        p.emit("T", mmtr, reads=[bpt[ccol], b_const],
                               writes=[b_bank[bk]])
                        p.emit("S", lambda e, W=W, ccol=ccol, bk=bk: e.activation(
                            scr2[:, POFF2 + ccol * 384: POFF2 + ccol * 384 + W],
                            banks[bk][:, 0:W], AF.Copy),
                            reads=[b_bank[bk]], writes=[b_pTt[ccol]])
                    for qidx, hh, qt, W, ks0, bk in insts:
                        ccol = qidx * 2 + hh

                        def mmo(e, hh=hh, ccol=ccol, qidx=qidx, ks0=ks0, W=W, hp=hp):
                            last = None
                            npc = W // 128
                            for cpc in range(npc):
                                jt = ks0 // 128 + cpc
                                voff = VOFF + jt * TT + (hp % 4) * 128 + 64 * hh
                                last = e.matmul(
                                    banks[0][64 * hh:64 * hh + 64,
                                             qidx * 128:(qidx + 1) * 128],
                                    scr2[:, voff: voff + 64],
                                    scr2[:, POFF2 + ccol * 384 + cpc * 128:
                                         POFF2 + ccol * 384 + (cpc + 1) * 128],
                                    start=(cpc == 0), stop=(cpc == npc - 1),
                                    tile_position=(0, 64 * hh))
                            return last
                        p.emit("T", mmo, reads=[b_vh, b_pTt[ccol]],
                               writes=[b_bank[0]])
                    p.emit("S", lambda e, hp=hp, w=w: e.activation(
                        big[:, hp * T + w * 256: hp * T + (w + 1) * 256],
                        banks[0][:, 0:256], AF.Copy),
                        reads=[b_bank[0]], writes=[b_bigc[hp]])
                if hp == 0 and f"l{li}_scr2" in tap_outs:
                    dma("G", tap_outs[f"l{li}_scr2"][:, :], scr2[:, :],
                        reads=[b_scr2])

            if f"l{li}_ao" in tap_outs:
                dma("G", tap_outs[f"l{li}_ao"][:, :], big[:, :], reads=[b_big])
            for f2 in range(CH):
                load_wslice("G", wo_w[:, 0:D], f"wo{li}",
                            slice(f2 * 128, (f2 + 1) * 128), writes=[b_wo[0]])
                for tt in range(NTT):
                    bk = 2 + (tt % 2)

                    def mmwo(e, tt=tt, bk=bk):
                        last = None
                        for kc in range(CH):
                            last = e.matmul(
                                banks[bk][:, :],
                                wo_w[:, kc * 128: (kc + 1) * 128],
                                big[:, kc * T + tt * TT: kc * T + (tt + 1) * TT],
                                start=(kc == 0), stop=(kc == CH - 1))
                        return last
                    p.emit("T", mmwo, reads=[b_wo[0]] + b_bigc, writes=[b_bank[bk]])
                    p.emit("V", lambda e, f2=f2, tt=tt, bk=bk: e.tensor_add(
                        stream[:, f2 * T + tt * TT: f2 * T + (tt + 1) * TT],
                        stream[:, f2 * T + tt * TT: f2 * T + (tt + 1) * TT],
                        banks[bk][:, :]),
                        reads=[b_bank[bk], b_stream[f2]], writes=[b_stream[f2]])

        def ffn_layer(li):
            cast_shift(li)
            dma("G", gb_col[:, 0:CH], dt[f"lng{li}"], writes=[b_gb])
            dma("G", gb_col[:, CH:2 * CH], dt[f"lnb{li}"], writes=[b_gb])
            dma("G", b1col[:, 0:NJ], dt[f"b1a{li}"], writes=[b_b1c])
            dma("G", b1col[:, NJ:2 * NJ], dt[f"b1g{li}"], writes=[b_b1c])
            dma("G", b2col[:, :], dt[f"b2c{li}"], writes=[b_b2c])
            Prog.seed([b_scr2], [b_vh, b_qh, b_kh] + b_pt + b_pTt)
            Prog.seed([b_rhl], b_ptB)
            ln_stats_norm(xs, b_xs, None, None, scr2, b_scr2,
                          gb_col[:, 0:CH], gb_col[:, CH:2 * CH])
            jblocks = [list(range(0, 8)), list(range(8, 16)), list(range(16, NJ))]
            w1bufs = [wq_w, wk_w]
            b_w1 = [Buf("w10"), Buf("w11")]
            Prog.seed(b_w1, [b_wq[0], b_wq[1], b_wk[0], b_wk[1]])
            load_wslice("G", w1bufs[0][:, 0:D], f"w1a{li}", slice(0, 128),
                        writes=[b_w1[0]])
            load_wslice("G", w1bufs[0][:, D:2 * D], f"w1g{li}", slice(0, 128),
                        writes=[b_w1[0]])
            b_w2s = [Buf(f"w2s{j}") for j in range(8)]
            b_hh = [Buf(f"hh{j}") for j in range(8)]
            Prog.seed(b_hh, [b_big] + b_bigc)
            jcount = 0
            for bi, jb in enumerate(jblocks):
                for idx, j in enumerate(jb):
                    buf = jcount % 2
                    jcount += 1
                    nj_global = j + 1
                    if nj_global < NJ:
                        nbuf = jcount % 2
                        ncols = slice(nj_global * 128, (nj_global + 1) * 128)
                        load_wslice("G", w1bufs[nbuf][:, 0:D],
                                    f"w1a{li}", ncols, writes=[b_w1[nbuf]])
                        load_wslice("G", w1bufs[nbuf][:, D:2 * D],
                                    f"w1g{li}", ncols, writes=[b_w1[nbuf]])
                    for tt in range(NTT):
                        ba = (tt % 2) * 2
                        def mma(e, buf=buf, tt=tt, ba=ba):
                            last = None
                            for kc in range(CH):
                                last = e.matmul(
                                    banks[ba][:, :],
                                    w1bufs[buf][:, kc * 128:(kc + 1) * 128],
                                    scr2[:, kc * T + tt * TT: kc * T + (tt + 1) * TT],
                                    start=(kc == 0), stop=(kc == CH - 1))
                            return last
                        p.emit("T", mma, reads=[b_w1[buf], b_scr2], writes=[b_bank[ba]])

                        def mmg(e, buf=buf, tt=tt, ba=ba):
                            last = None
                            for kc in range(CH):
                                last = e.matmul(
                                    banks[ba + 1][:, :],
                                    w1bufs[buf][:, D + kc * 128: D + (kc + 1) * 128],
                                    scr2[:, kc * T + tt * TT: kc * T + (tt + 1) * TT],
                                    start=(kc == 0), stop=(kc == CH - 1))
                            return last
                        p.emit("T", mmg, reads=[b_w1[buf], b_scr2], writes=[b_bank[ba + 1]])
                        gdst = sqk if tt % 2 == 0 else sqq
                        bgdst = b_sqk if tt % 2 == 0 else b_sqq
                        p.emit("S", lambda e, j=j, ba=ba, gdst=gdst: e.activation(
                            gdst[:, :], banks[ba + 1][:, :], AF.Gelu,
                            bias=b1col[:, NJ + j:NJ + j + 1]),
                            reads=[b_bank[ba + 1], b_b1c], writes=[bgdst])
                        p.emit("V", lambda e, j=j, idx=idx, tt=tt, ba=ba, gdst=gdst:
                               e.scalar_tensor_tensor(
                            big[:, idx * T + tt * TT: idx * T + (tt + 1) * TT],
                            banks[ba][:, :], b1col[:, j:j + 1], gdst[:, :],
                            ALU.add, ALU.mult),
                            reads=[b_bank[ba], b_b1c, bgdst], writes=[b_hh[idx]])
                    dma("G", wtab[:, idx * D:(idx + 1) * D],
                        dt[f"w2{li}"][j * 128:(j + 1) * 128, :],
                        writes=[b_wtab, b_w2s[idx]])
                for tt in range(NTT):
                    for f2 in range(CH):
                        bk = 4 + (f2 % 4)

                        def mmw2(e, jb=jb, tt=tt, f2=f2, bk=bk):
                            last = None
                            for idx in range(len(jb)):
                                last = e.matmul(
                                    banks[bk][:, :],
                                    wtab[:, idx * D + f2 * 128:
                                         idx * D + (f2 + 1) * 128],
                                    big[:, idx * T + tt * TT:
                                        idx * T + (tt + 1) * TT],
                                    start=(idx == 0), stop=(idx == len(jb) - 1))
                            return last
                        p.emit("T", mmw2,
                               reads=[b_wtab] + [b_w2s[i2] for i2 in range(len(jb))] +
                                     [b_hh[i2] for i2 in range(len(jb))],
                               writes=[b_bank[bk]])
                        if bi == 0:
                            p.emit("V", lambda e, f2=f2, tt=tt, bk=bk:
                                   e.scalar_tensor_tensor(
                                stream[:, f2 * T + tt * TT: f2 * T + (tt + 1) * TT],
                                banks[bk][:, :], b2col[:, f2:f2 + 1],
                                stream[:, f2 * T + tt * TT: f2 * T + (tt + 1) * TT],
                                ALU.add, ALU.add),
                                reads=[b_bank[bk], b_b2c, b_stream[f2]],
                                writes=[b_stream[f2]])
                        else:
                            p.emit("V", lambda e, f2=f2, tt=tt, bk=bk: e.tensor_add(
                                stream[:, f2 * T + tt * TT: f2 * T + (tt + 1) * TT],
                                stream[:, f2 * T + tt * TT: f2 * T + (tt + 1) * TT],
                                banks[bk][:, :]),
                                reads=[b_bank[bk], b_stream[f2]],
                                writes=[b_stream[f2]])
            Prog.seed([b_big], b_hh + b_w2s)
            Prog.seed([b_wq[0], b_wq[1]], [b_w1[0]])
            Prog.seed([b_wk[0], b_wk[1]], [b_w1[1]])

        for li in range(n_layers):
            attn_layer(li)
            if f"l{li}_res1" in tap_outs:
                for c in range(CH):
                    dma("Q", tap_outs[f"l{li}_res1"][:, c * T:(c + 1) * T],
                        stream[:, c * T:(c + 1) * T], reads=[b_stream[c]])
            ffn_layer(li)
            if f"l{li}_res2" in tap_outs:
                for c in range(CH):
                    dma("Q", tap_outs[f"l{li}_res2"][:, c * T:(c + 1) * T],
                        stream[:, c * T:(c + 1) * T], reads=[b_stream[c]])

        if with_performer:
            Prog.FENCE_ALL = True
            for c in range(CH):
                for tt in range(NTT):
                    sl = stream[:, c * T + tt * TT: c * T + (tt + 1) * TT]
                    p.emit("S", lambda e, s=sl: e.activation(s, s, AF.Silu),
                           reads=[b_stream[c]], writes=[b_stream[c]])
            if "perf_g" in tap_outs:
                for c in range(CH):
                    dma("Q", tap_outs["perf_g"][:, c * T:(c + 1) * T],
                        stream[:, c * T:(c + 1) * T], reads=[b_stream[c]])
            dma("G", xs3, xr, writes=b_xs)
            ln_stats_norm(xs, b_xs, None, None, scr2, b_scr2, None, None)
            dma("G", pbias[:, 0:CH], dt["p_bq"], writes=[b_pbias])
            dma("G", pbias[:, CH:2 * CH], dt["p_bk"], writes=[b_pbias])
            dma("G", pbias[:, 2 * CH:3 * CH], dt["p_bo"], writes=[b_pbias])
            Prog.seed([b_pbvr], [b_wtab])
            dma("G", wtab[0:1, 6 * T // 2: 6 * T // 2 + D], dt["p_bv_row"],
                writes=[b_pbvr])
            ic = float(1.0 / (DKH ** 0.25))
            p.emit("V", lambda e: e.tensor_scalar(pbias[:, 0:2 * CH],
                                                  pbias[:, 0:2 * CH], ic, None,
                                                  ALU.mult),
                   reads=[b_pbias], writes=[b_pbias])

            PQ = 0
            PK = 2048
            PV = 4096
            b_qp = Buf("qp"); b_kp = Buf("kp"); b_pv = Buf("pv")
            Prog.seed([b_qp, b_kp, b_pv], b_xs)

            load_wslice("G", wq_w[:, 0:D], "p_wq", slice(0, 128),
                        reads=[b_wq[0]], writes=[b_wq[0]])
            load_wslice("G", wk_w[:, 0:D], "p_wk", slice(0, 128),
                        reads=[b_wk[0]], writes=[b_wk[0]])

            import os as _os2
            for h in range(int(_os2.environ.get('PERF_HEADS', '16'))):
                hp = h // 2
                half = h % 2
                if half == 0:
                    if hp + 1 < 8:
                        pass
                    for tt in range(NTT):
                        def mmq(e, tt=tt):
                            last = None
                            for kc in range(CH):
                                last = e.matmul(
                                    banks[0][:, :],
                                    wq_w[:, kc * 128:(kc + 1) * 128],
                                    scr2[:, kc * T + tt * TT: kc * T + (tt + 1) * TT],
                                    start=(kc == 0), stop=(kc == CH - 1))
                            return last
                        p.emit("T", mmq, reads=[b_wq[0], b_scr2], writes=[b_bank[0]])
                        p.emit("S", lambda e, tt=tt, hp=hp: e.activation(
                            xs[:, PQ + tt * TT: PQ + (tt + 1) * TT], banks[0][:, :],
                            AF.Identity, bias=pbias[:, hp:hp + 1], scale=ic),
                            reads=[b_bank[0], b_pbias], writes=[b_qp])

                        def mmk(e, tt=tt):
                            last = None
                            for kc in range(CH):
                                last = e.matmul(
                                    banks[1][:, :],
                                    wk_w[:, kc * 128:(kc + 1) * 128],
                                    scr2[:, kc * T + tt * TT: kc * T + (tt + 1) * TT],
                                    start=(kc == 0), stop=(kc == CH - 1))
                            return last
                        p.emit("T", mmk, reads=[b_wk[0], b_scr2], writes=[b_bank[1]])
                        p.emit("S", lambda e, tt=tt, hp=hp: e.activation(
                            xs[:, PK + tt * TT: PK + (tt + 1) * TT], banks[1][:, :],
                            AF.Identity, bias=pbias[:, CH + hp:CH + hp + 1], scale=ic),
                            reads=[b_bank[1], b_pbias], writes=[b_kp])
                    if hp + 1 < 8:
                        load_wslice("G", wq_w[:, 0:D], "p_wq",
                                    slice((hp + 1) * 128, (hp + 2) * 128),
                                    reads=[b_qp], writes=[b_wq[0]])
                        load_wslice("G", wk_w[:, 0:D], "p_wk",
                                    slice((hp + 1) * 128, (hp + 2) * 128),
                                    reads=[b_kp], writes=[b_wk[0]])
                if h == 0 or h == 8:
                    grp = h // 8
                    for sub in range(2):
                        load_wslice("G", wv_w[:, :], "p_wv",
                                    slice(grp * 512 + sub * 256,
                                          grp * 512 + (sub + 1) * 256),
                                    writes=[b_wv])
                        for ttok in range(NCK):
                            def mmv(e, ttok=ttok, grp=grp, sub=sub):
                                for kc in range(CH):
                                    e.matmul(
                                        banks[5][:, 0:256],
                                        scr2[:, kc * T + ttok * 128:
                                             kc * T + (ttok + 1) * 128],
                                        wv_w[:, kc * 256:(kc + 1) * 256],
                                        start=(kc == 0), stop=False)
                                boff = 6 * T // 2 + grp * 512 + sub * 256
                                return e.matmul(
                                    banks[5][:, 0:256], ones2_b[0:1, :],
                                    wtab[0:1, boff:boff + 256],
                                    start=False, stop=True, tile_position=(0, 0))
                            p.emit("T", mmv, reads=[b_scr2, b_wv, b_pbvr, b_const],
                                   writes=[b_bank[5]])
                            p.emit("S", lambda e, ttok=ttok, sub=sub: e.activation(
                                xs[:, PV + ttok * TT + sub * 256:
                                   PV + ttok * TT + (sub + 1) * 256],
                                banks[5][:, 0:256], AF.Copy),
                                reads=[b_bank[5]], writes=[b_pv])

                for tt in range(NTT):
                    p.emit("S", lambda e, tt=tt: e.activation(
                        sqq[:, :], xs[:, PQ + tt * TT: PQ + (tt + 1) * TT], AF.Square),
                        reads=[b_qp], writes=[b_sqq])

                    def mmssq(e, tt=tt, half=half):
                        last = None
                        for g in range(4):
                            col = tt * 4 + g
                            last = e.matmul(
                                banks[7][:, col:col + 1],
                                sqq[64 * half:64 * half + 64,
                                    g * 128:(g + 1) * 128],
                                onescol_b[64 * half:64 * half + 64, :],
                                start=True, stop=True,
                                tile_position=(64 * half, 0))
                        return last
                    p.emit("T", mmssq, reads=[b_sqq, b_const], writes=[b_bank[7]])
                    p.emit("S", lambda e, tt=tt: e.activation(
                        sqk[:, :], xs[:, PK + tt * TT: PK + (tt + 1) * TT], AF.Square),
                        reads=[b_kp], writes=[b_sqk])

                    def mmssk(e, tt=tt, half=half):
                        last = None
                        for g in range(4):
                            col = 16 + tt * 4 + g
                            last = e.matmul(
                                banks[7][:, col:col + 1],
                                sqk[64 * half:64 * half + 64,
                                    g * 128:(g + 1) * 128],
                                onescol_b[64 * half:64 * half + 64, :],
                                start=True, stop=True,
                                tile_position=(64 * half, 0))
                        return last
                    p.emit("T", mmssk, reads=[b_sqk, b_const], writes=[b_bank[7]])
                p.emit("V", lambda e: e.tensor_scalar(
                    rq_cols[:, 0:32], banks[7][:, 0:32], -0.5, -LNM,
                    ALU.mult, ALU.add),
                    reads=[b_bank[7]], writes=[b_rq], fence=True)
                for ttok in range(NCK):
                    tcol = ttok // 4 * 4 + ttok % 4
                    def mmuq(e, ttok=ttok, half=half):
                        return e.matmul(
                            banks[2][:, 0:128],
                            xs[64 * half:64 * half + 64,
                               PQ + ttok * 128: PQ + (ttok + 1) * 128],
                            projT2_b[64 * half:64 * half + 64, :],
                            start=True, stop=True, tile_position=(64 * half, 0))
                    p.emit("T", mmuq, reads=[b_qp, b_const], writes=[b_bank[2]])
                    p.emit("S", lambda e, ttok=ttok: e.activation(
                        phi_f[:, 0:128], banks[2][:, 0:128], AF.Exp,
                        bias=rq_cols[:, ttok:ttok + 1]),
                        reads=[b_bank[2], b_rq], writes=[b_phif])
                    p.emit("V", lambda e: e.tensor_scalar(
                        sqq[:, 0:128], phi_f[:, 0:128], 1.0e-6, None, ALU.add),
                        reads=[b_phif], writes=[b_sqq], fence=True)
                    dmaT(wtab[:, 2 * T + ttok * 128: 2 * T + (ttok + 1) * 128],
                         sqq[:, 0:128], reads=[b_sqq], writes=[b_wtab])

                    def mmuk(e, ttok=ttok, half=half):
                        return e.matmul(
                            banks[3][:, 0:128],
                            xs[64 * half:64 * half + 64,
                               PK + ttok * 128: PK + (ttok + 1) * 128],
                            projT2_b[64 * half:64 * half + 64, :],
                            start=True, stop=True, tile_position=(64 * half, 0))
                    p.emit("T", mmuk, reads=[b_kp, b_const], writes=[b_bank[3]])
                    p.emit("S", lambda e, ttok=ttok: e.activation(
                        phi_f[:, 128:256], banks[3][:, 0:128], AF.Exp,
                        bias=rq_cols[:, 16 + ttok:16 + ttok + 1]),
                        reads=[b_bank[3], b_rq], writes=[b_phif])
                    p.emit("V", lambda e, ttok=ttok: e.tensor_scalar(
                        wtab[:, ttok * 128:(ttok + 1) * 128], phi_f[:, 128:256],
                        1.0e-6, None, ALU.add),
                        reads=[b_phif], writes=[b_wtab], fence=True)
                    dmaT(wtab[:, T + ttok * 128: T + (ttok + 1) * 128],
                         wtab[:, ttok * 128:(ttok + 1) * 128],
                         reads=[b_wtab], writes=[b_wtab])
                p.emit("V", lambda e: e.memset(state_S[:, :], 0.0), writes=[b_state])
                p.emit("V", lambda e: e.memset(state_Sb[:, :], 0.0), writes=[b_state])
                p.emit("V", lambda e: e.memset(state_z[:, :], 0.0), writes=[b_state])
                p.emit("V", lambda e: e.memset(state_zb[:, :], 0.0), writes=[b_state])
                for n in range(NCK):
                    qsl = wtab[:, 2 * T + n * 128: 2 * T + (n + 1) * 128]
                    p.emit("T", lambda e, qsl=qsl, half=half: e.matmul(
                        banks[4][64 * half:64 * half + 64, 0:128],
                        state_Sb[:, :], qsl, start=True, stop=False,
                        tile_position=(0, 64 * half)),
                        reads=[b_state, b_wtab], writes=[b_bank[4]])
                    p.emit("T", lambda e, n=n, qsl=qsl: e.matmul(
                        banks[6][:, 0:128], wtab[:, T + n * 128: T + (n + 1) * 128],
                        qsl, start=True, stop=True),
                        reads=[b_wtab], writes=[b_bank[6]])
                    p.emit("V", lambda e: e.tensor_mul(m1q[:, 0:128],
                                                       banks[6][:, 0:128],
                                                       triu_b[:, :]),
                           reads=[b_bank[6], b_const], writes=[b_m1q], fence=True)
                    def mmden(e, qsl=qsl):
                        e.matmul(banks[6][0:1, 128:256], state_zb[:, :], qsl,
                                 start=True, stop=False, tile_position=(0, 0))
                        return e.matmul(banks[6][0:1, 128:256], onescol_b[:, :],
                                        m1q[:, 0:128], start=False, stop=True,
                                        tile_position=(0, 0))
                    p.emit("T", mmden, reads=[b_state, b_wtab, b_m1q, b_const],
                           writes=[b_bank[6]])
                    p.emit("T", lambda e, n=n, half=half, hp=hp: e.matmul(
                        banks[4][64 * half:64 * half + 64, 0:128],
                        xs[:, PV + n * TT + (hp % 4) * 128 + 64 * half:
                           PV + n * TT + (hp % 4) * 128 + 64 * half + 64],
                        m1q[:, 0:128], start=False, stop=True,
                        tile_position=(0, 64 * half)),
                        reads=[b_pv, b_m1q], writes=[b_bank[4]])
                    p.emit("V", lambda e: e.tensor_scalar(
                        denrow[0:1, 0:128], banks[6][0:1, 128:256], 1.0e-6, None,
                        ALU.add),
                        reads=[b_bank[6], b_rowsB], writes=[b_denrow])
                    p.emit("V", lambda e: e.reciprocal(denrow[0:1, 0:128],
                                                       denrow[0:1, 0:128]),
                           reads=[b_denrow], writes=[b_denrow])
                    p.emit("V", lambda e: e.tensor_copy(denrow_b[0:1, 0:128],
                                                        denrow[0:1, 0:128]),
                           reads=[b_denrow, b_rhl], writes=[b_denrow_b])
                    p.emit("V", lambda e: e.tensor_sub(denrow_b[0:1, 128:256],
                                                       denrow[0:1, 0:128],
                                                       denrow_b[0:1, 0:128]),
                           reads=[b_denrow, b_denrow_b], writes=[b_denrow_b],
                           fence=True)

                    def mmrden(e, half=half):
                        e.matmul(banks[5][64 * half:64 * half + 64, 0:128],
                                 ones2_b[0:1, 0:64], denrow_b[0:1, 0:128],
                                 start=True, stop=False, tile_position=(0, 64 * half))
                        return e.matmul(banks[5][64 * half:64 * half + 64, 0:128],
                                        ones2_b[0:1, 0:64], denrow_b[0:1, 128:256],
                                        start=False, stop=True,
                                        tile_position=(0, 64 * half))
                    p.emit("T", mmrden, reads=[b_denrow_b, b_const],
                           writes=[b_bank[5]])
                    p.emit("S", lambda e, half=half: e.activation(
                        kBt[64 * half:64 * half + 64, 0:128],
                        banks[5][64 * half:64 * half + 64, 0:128], AF.Copy),
                        reads=[b_bank[5]], writes=[b_kBt])
                    p.emit("V", lambda e, half=half, hp=hp, n=n: e.tensor_mul(
                        big[64 * half:64 * half + 64,
                            hp * T + n * 128: hp * T + (n + 1) * 128],
                        banks[4][64 * half:64 * half + 64, 0:128],
                        kBt[64 * half:64 * half + 64, 0:128]),
                        reads=[b_bank[4], b_kBt], writes=[b_big])
                    def mmkv(e, n=n, half=half, hp=hp):
                        e.matmul(banks[2][:, 0:DKH],
                                 wtab[:, n * 128:(n + 1) * 128],
                                 xs[:, PV + n * TT + (hp % 4) * 128 + 64 * half:
                                    PV + n * TT + (hp % 4) * 128 + 64 * half + 64],
                                 start=True, stop=True)
                        return e.matmul(banks[2][:, 64:65],
                                        wtab[:, n * 128:(n + 1) * 128],
                                        onescol_b[:, :], start=True, stop=True)
                    p.emit("T", mmkv, reads=[b_wtab, b_pv, b_const],
                           writes=[b_bank[2]])
                    p.emit("V", lambda e: e.tensor_add(state_S[:, :], state_S[:, :],
                                                       banks[2][:, 0:DKH]),
                           reads=[b_bank[2], b_state], writes=[b_state])
                    p.emit("V", lambda e: e.tensor_copy(state_Sb[:, :],
                                                        state_S[:, :]),
                           reads=[b_state], writes=[b_state], fence=True)
                    p.emit("V", lambda e: e.tensor_add(state_z[:, :], state_z[:, :],
                                                       banks[2][:, 64:65]),
                           reads=[b_bank[2], b_state], writes=[b_state])
                    p.emit("V", lambda e: e.tensor_copy(state_zb[:, :],
                                                        state_z[:, :]),
                           reads=[b_state], writes=[b_state], fence=True)

            if "perf_ao" in tap_outs:
                dma("G", tap_outs["perf_ao"][:, :], big[:, :], reads=[b_big])
            if "perf_xn" in tap_outs:
                dma("G", tap_outs["perf_xn"][:, :], scr2[:, :], reads=[b_scr2])
            if "perf_qf" in tap_outs:
                dma("G", tap_outs["perf_qf"][:, 0:4 * T], wtab[:, :], reads=[b_wtab])
            for f2 in range(CH):
                load_wslice("G", wo_w[:, 0:D], "p_wo",
                            slice(f2 * 128, (f2 + 1) * 128), writes=[b_wo[0]])
                for tt in range(NTT):
                    bk = 2 + (tt % 2)

                    def mmwo(e, tt=tt, bk=bk):
                        last = None
                        for kc in range(CH):
                            last = e.matmul(
                                banks[bk][:, :],
                                wo_w[:, kc * 128: (kc + 1) * 128],
                                big[:, kc * T + tt * TT: kc * T + (tt + 1) * TT],
                                start=(kc == 0), stop=(kc == CH - 1))
                        return last
                    p.emit("T", mmwo, reads=[b_wo[0], b_big], writes=[b_bank[bk]])
                    p.emit("V", lambda e, f2=f2, tt=tt, bk=bk:
                           e.scalar_tensor_tensor(
                        phi_f[:, :], banks[bk][:, :], pbias[:, 2 * CH + f2:2 * CH + f2 + 1],
                        stream[:, f2 * T + tt * TT: f2 * T + (tt + 1) * TT],
                        ALU.add, ALU.mult),
                        reads=[b_bank[bk], b_pbias, b_stream[f2]], writes=[b_phif])
                    dma("Q", stream[:, f2 * T + tt * TT: f2 * T + (tt + 1) * TT],
                        dt["x_t"][f2 * 128:(f2 + 1) * 128, tt * TT:(tt + 1) * TT],
                        writes=[b_stream[f2]])
                    p.emit("V", lambda e, f2=f2, tt=tt: e.tensor_add(
                        phi_f[:, :], phi_f[:, :],
                        stream[:, f2 * T + tt * TT: f2 * T + (tt + 1) * TT]),
                        reads=[b_stream[f2]], writes=[b_phif])
                    dma("Q", y_t[f2 * 128:(f2 + 1) * 128, tt * TT:(tt + 1) * TT],
                        phi_f[:, :], reads=[b_phif], writes=[b_y])
        else:
            for c in range(CH):
                dma("Q", y_t[c * 128:(c + 1) * 128, :],
                    stream[:, c * T:(c + 1) * T], reads=[b_stream[c]], writes=[b_y])

        Prog.FENCE_ALL = False
        p.barrier_q([("DQ", p.cnt["DQ"]), ("DG", p.cnt["DG"])])
        p.finalize(blk, sems)
    return nc


_BUILD_CACHE = {}


def kernel(**inputs):
    maps = host_prepare(inputs)
    key = (N_LAYERS, WITH_PERFORMER, tuple(DEBUG_TAPS))
    if key not in _BUILD_CACHE:
        _BUILD_CACHE[key] = build_program(N_LAYERS, WITH_PERFORMER, DEBUG_TAPS)
    nc = _BUILD_CACHE[key]
    res = run_bass_kernel_spmd(nc, maps, core_ids=list(range(8)))
    kernel.last_results = res
    outs = [np.ascontiguousarray(r["y_t"].T) for r in res.results]
    return np.stack(outs, axis=0)


def last_res_tap(name, core=0):
    return kernel.last_results.results[core][f"dbg_{name}"]
